# version 06
# baseline (speedup 1.0000x reference)
import numpy as np
import concourse.bass as bass
import concourse.mybir as mybir
from concourse.alu_op_type import AluOpType
from concourse.bass_utils import run_bass_kernel_spmd
from contextlib import ExitStack

F32 = mybir.dt.float32
BF16 = mybir.dt.bfloat16
AF = mybir.ActivationFunctionType
AX = mybir.AxisListType

B, L, H, DI, N, R = 4, 2048, 2048, 4096, 16, 128
KC = 4
DH = DI // 2
HID = 8192
NC = 8
LH = L // 2
LC = 1024
NT = 16
RINT_C = 12582912.0
LN_EPS = 1e-5
RMS_EPS = 1.1920929e-07
PAIRS = [[0, 1], [2, 3], [4, 5], [6, 7]]
ALLG = [list(range(NC))]
MUL, ADD, SUB, MAXO, MINO, BYP = (AluOpType.mult, AluOpType.add, AluOpType.subtract,
                                  AluOpType.max, AluOpType.min, AluOpType.bypass)


def build_program():
    nc = bass.Bass("TRN2", target_bir_lowering=False, debug=False)

    def din(name, shape, dt=F32):
        return nc.dram_tensor(name, shape, dt, kind="ExternalInput").ap()

    def dtmp(name, shape, dt=F32, shared=False):
        if shared:
            return nc.dram_tensor(name, shape, dt, addr_space="Shared").ap()
        return nc.dram_tensor(name, shape, dt).ap()

    x_b = din("x_b", [L, H])
    x_h = din("x_h", [LH, H])
    wis = din("wis", [DH, H])
    wig = din("wig", [DH, H])
    cw = din("cw", [DH, KC])
    cbia = din("cbia", [DH, 1])
    wdt = din("wdt", [R, DH])
    wbc = din("wbc", [2 * N, DH])
    wdp = din("wdp", [DH, R])
    bdp = din("bdp", [DH, 1])
    alog = din("alog", [DH, N])
    dvec = din("dvec", [DH, 1])
    wout_h = din("wout_h", [H, DH])
    wfc_s = din("wfc_s", [HID // NC, H])
    wpj_s = din("wpj_s", [H, HID // NC])
    wmod_s = din("wmod_s", [H, 6144])
    c0_my = din("c0_my", [1, H])
    eyef = din("eyef", [128, 128])
    eyeb = din("eyeb", [128, 128], BF16)

    out_my = nc.dram_tensor("out_my", [LH, H], F32, kind="ExternalOutput").ap()

    hs_sp = dtmp("hs_sp", [DH, L])
    silu_sp = dtmp("silu_sp", [DH, L], BF16)
    dt_sp = dtmp("dt_sp", [DH, L])
    y3_sp = dtmp("y3_sp", [DH, L], BF16)
    h_sp = dtmp("h_sp", [HID, LH], BF16)
    wisT_sp = dtmp("wisT_sp", [DH, H], BF16)
    wigT_sp = dtmp("wigT_sp", [DH, H], BF16)
    woutT_sp = dtmp("woutT_sp", [H, DH], BF16)
    dtbc_i = dtmp("dtbc_i", [R + 2 * N, L])
    dtbc_o = dtmp("dtbc_o", [R + 2 * N, L])
    op_i = dtmp("op_i", [2 * H, LH])
    op_o = dtmp("op_o", [H, LH])
    mod_i = dtmp("mod_i", [1, 6144])
    mod_o = dtmp("mod_o", [2, 6144])
    wfcT_i = dtmp("wfcT_i", [HID // NC, H], BF16)
    wfcT_o = dtmp("wfcT_o", [HID, H], BF16, shared=True)
    wpjT_i = dtmp("wpjT_i", [H, HID // NC], BF16)
    wpjT_o = dtmp("wpjT_o", [NC * H, HID // NC], BF16, shared=True)

    cnt = {}

    def inc(instr, sem, n=1):
        instr.then_inc(sem, n)
        cnt[sem.num] = cnt.get(sem.num, 0) + n
        return cnt[sem.num]

    def val(sem):
        return cnt.get(sem.num, 0)

    st = ExitStack()
    E = st.enter_context
    with st:
        E(nc.allow_non_contiguous_dma(
            reason="small column-vector loads and interleaved row assembly"))
        sD = E(nc.semaphore("sD"))
        sDw = E(nc.semaphore("sDw"))
        sDx = E(nc.semaphore("sDx"))
        sDo = E(nc.semaphore("sDo"))
        sDs = [E(nc.semaphore("sDs0")), E(nc.semaphore("sDs1"))]
        sDp = [E(nc.semaphore("sDp0")), E(nc.semaphore("sDp1"))]
        sDq = [E(nc.semaphore("sDq0")), E(nc.semaphore("sDq1"))]
        sGP = E(nc.semaphore("sGP"))
        sV = E(nc.semaphore("sV"))
        sS = E(nc.semaphore("sS"))
        sT = E(nc.semaphore("sT"))
        sG = E(nc.semaphore("sG"))

        def sb(name, shape, dt=F32, scope=None):
            return (scope or st).enter_context(nc.sbuf_tensor(name, shape, dt))

        ident_f = sb("ident_f", [128, 128])
        ident_b = sb("ident_b", [128, 128], BF16)
        ones_cb = sb("ones_cb", [128, 1], BF16)
        ones_cf = sb("ones_cf", [128, 1])
        ones_rb = sb("ones_rb", [1, 128], BF16)
        ones_rf = sb("ones_rf", [1, 128])
        acol = sb("acol", [128, NT * N])
        dcol = sb("dcol", [128, NT])
        cbcol = sb("cbcol", [128, NT])
        bdcol = sb("bdcol", [128, NT])
        state = sb("state", [128, NT * N])
        ca1 = sb("ca1", [128, NT])
        sa_c = sb("sa_c", [128, NT])
        ga_c = sb("ga_c", [128, NT])
        cb1 = sb("cb1", [128, NT])
        sb_c = sb("sb_c", [128, NT])
        gb_c = sb("gb_c", [128, NT])
        wdtT = sb("wdtT", [128, NT * 128], BF16)
        wbcT = sb("wbcT", [128, NT * 32], BF16)
        wdpT = sb("wdpT", [128, NT * 128], BF16)
        diagw = sb("diagw", [128, NT * KC * 128], BF16)
        bcb = sb("bcb", [32, L], BF16)
        ts_b = sb("ts_b", [R, L], BF16)
        xs = sb("xs", [128, H])
        tbf2 = sb("tbf2", [128, 2 * H])
        colw = sb("colw", [128, 8])

        pb = [E(nc.psum_tensor(f"pb{i}", [128, 512], F32)) for i in range(8)]
        bank_guard = {i: None for i in range(8)}

        def bank_wait(t, i):
            if bank_guard[i] is not None:
                s, c = bank_guard[i]
                t.wait_ge(s, c)

        blk = E(nc.Block())

        def tp_blocks_bf16(src_sb, wcols, dst_sb, coloff=0, wait_v=None,
                           extra_waits=()):
            nb = wcols // 128

            @blk.tensor
            def _(t):
                if wait_v is not None:
                    t.wait_ge(sV, wait_v)
                for g0 in range(0, nb, 4):
                    gn = min(4, nb - g0)
                    bk = 6 + (g0 // 4) % 2
                    bank_wait(t, bk)
                    for j in range(gn):
                        cb = g0 + j
                        inc(t.transpose(
                            pb[bk][:, j * 64:(j + 1) * 64].bitcast(BF16),
                            in_=src_sb[:, cb * 128:(cb + 1) * 128],
                            identity=ident_b[:]), sT)
                    t_here = val(sT)

                    @blk.vector
                    def _(v, g0=g0, gn=gn, bk=bk, t_here=t_here):
                        v.wait_ge(sT, t_here)
                        for s_, c_ in extra_waits:
                            v.wait_ge(s_, c_)
                        c = inc(v.tensor_copy(
                            out=dst_sb[:, coloff + g0 * 128:
                                       coloff + (g0 + gn) * 128],
                            in_=pb[bk][:, 0:gn * 64].bitcast(BF16)), sV)
                        bank_guard[bk] = (sV, c)
            return val(sV)

        @blk.sync
        def _(sy):
            sy.wait_ge(sDw, val(sDw))
            inc(sy.dma_start(out=ident_f[:], in_=eyef), sDw, 16)
            inc(sy.dma_start(out=ident_b[:], in_=eyeb), sDw, 16)
            inc(sy.dma_start(out=acol.ap().rearrange("p (t n) -> p t n", n=N),
                             in_=alog.rearrange("(t p) n -> p t n", p=128)), sDw, 16)
            inc(sy.dma_start(out=dcol.ap(),
                             in_=dvec.rearrange("(t p) o -> p (t o)", p=128)), sDw, 16)
            inc(sy.dma_start(out=cbcol.ap(),
                             in_=cbia.rearrange("(t p) o -> p (t o)", p=128)), sDw, 16)
            inc(sy.dma_start(out=bdcol.ap(),
                             in_=bdp.rearrange("(t p) o -> p (t o)", p=128)), sDw, 16)
        p0a_d = val(sDw)

        @blk.vector
        def _(v):
            inc(v.memset(ones_cb[:], 1.0), sV)
            inc(v.memset(ones_cf[:], 1.0), sV)
            inc(v.memset(ones_rb[:], 1.0), sV)
            inc(v.memset(ones_rf[:], 1.0), sV)
        p0_ones_v = val(sV)

        @blk.scalar
        def _(sc):
            sc.wait_ge(sDw, p0a_d)
            inc(sc.activation(out=acol[:], in_=acol[:], func=AF.Exp), sS)
        p0_aexp_s = val(sS)

        @blk.vector
        def _(v):
            v.wait_ge(sS, p0_aexp_s)
            inc(v.tensor_scalar(out=acol[:], in0=acol[:], scalar1=-1.0, scalar2=None,
                                op0=MUL, op1=BYP), sV)
        p0_acol_v = val(sV)

        with ExitStack() as sc0:
            c0s = sb("c0s", [1, H], scope=sc0)
            silT = sb("silT", [128, NT], BF16, scope=sc0)
            wmstg = sb("wmstg", [128, 3072], scope=sc0)
            wmb = sb("wmb", [128, 3072], BF16, scope=sc0)
            mvrow = sb("mvrow", [1, 3072], scope=sc0)
            modr = sb("modr", [1, 6 * H], scope=sc0)
            modc = sb("modc", [128, 96], scope=sc0)
            tmpc = sb("tmpc", [128, 2 * NT], scope=sc0)

            @blk.sync
            def _(sy):
                sy.wait_ge(sDx, val(sDx))
                inc(sy.dma_start(out=c0s[:], in_=c0_my), sDx, 16)
            d0 = val(sDx)

            @blk.scalar
            def _(scl):
                scl.wait_ge(sDx, d0)
                inc(scl.activation(out=c0s[:], in_=c0s[:], func=AF.Silu), sS)
            s0 = val(sS)

            @blk.tensor
            def _(t):
                t.wait_ge(sS, s0)
                t.wait_ge(sDw, p0a_d)
                for j in range(NT):
                    bank_wait(t, 6)
                    inc(t.transpose(pb[6][:, j:j + 1],
                                    in_=c0s[0:1, j * 128:(j + 1) * 128],
                                    identity=ident_f[0:1, 0:1]), sT)
            t0 = val(sT)

            @blk.vector
            def _(v):
                v.wait_ge(sT, t0)
                c = inc(v.tensor_copy(out=silT[:], in_=pb[6][:, 0:NT]), sV)
                bank_guard[6] = (sV, c)
            v_silT = val(sV)

            mvrow_dma = 0
            for piece in range(2):
                mm_last = None
                for kt in range(NT):
                    @blk.sync
                    def _(sy, piece=piece, kt=kt):
                        sy.wait_ge(sDw, val(sDw))
                        sy.wait_ge(sV, val(sV))
                        inc(sy.dma_start(
                            out=wmstg[:],
                            in_=wmod_s[kt * 128:(kt + 1) * 128,
                                       piece * 3072:(piece + 1) * 3072]), sDw, 16)
                    d1 = val(sDw)

                    @blk.vector
                    def _(v, d1=d1, mm_prev=mm_last):
                        v.wait_ge(sDw, d1)
                        v.wait_ge(sV, v_silT)
                        if mm_prev is not None:
                            v.wait_ge(sT, mm_prev)
                        inc(v.tensor_copy(out=wmb[:], in_=wmstg[:]), sV)
                    v1 = val(sV)

                    @blk.tensor
                    def _(t, kt=kt, v1=v1):
                        t.wait_ge(sV, v1)
                        for fb in range(6):
                            if kt == 0:
                                bank_wait(t, fb)
                            inc(t.matmul(pb[fb][0:1, 0:512],
                                         lhsT=silT[:, kt:kt + 1],
                                         rhs=wmb[:, fb * 512:(fb + 1) * 512],
                                         start=(kt == 0), stop=(kt == NT - 1)), sT)
                    mm_last = val(sT)

                @blk.vector
                def _(v, mm_last=mm_last, mvrow_dma=mvrow_dma):
                    v.wait_ge(sT, mm_last)
                    v.wait_ge(sDo, mvrow_dma)
                    for fb in range(6):
                        c = inc(v.tensor_copy(out=mvrow[0:1, fb * 512:(fb + 1) * 512],
                                              in_=pb[fb][0:1, 0:512]), sV)
                        bank_guard[fb] = (sV, c)
                v2 = val(sV)

                @blk.sync
                def _(sy, piece=piece, v2=v2):
                    sy.wait_ge(sDo, val(sDo))
                    sy.wait_ge(sV, v2)
                    inc(sy.dma_start(out=mod_i[0:1, piece * 3072:(piece + 1) * 3072],
                                     in_=mvrow[:]), sDo, 16)
                mvrow_dma = val(sDo)
            p0b_d = val(sDo)

            @blk.gpsimd
            def _(g):
                g.wait_ge(sDo, p0b_d)
                inc(g.collective_compute("AllGather", BYP, replica_groups=PAIRS,
                                         ins=[mod_i.opt()], outs=[mod_o.opt()]), sG)
            g0 = val(sG)

            @blk.sync
            def _(sy):
                sy.wait_ge(sDx, val(sDx))
                sy.wait_ge(sG, g0)
                inc(sy.dma_start(out=modr[0:1, 0:6144], in_=mod_o[0:1, :]), sDx, 16)
                inc(sy.dma_start(out=modr[0:1, 6144:12288], in_=mod_o[1:2, :]), sDx, 16)
            d2 = val(sDx)

            @blk.tensor
            def _(t):
                t.wait_ge(sDx, d2)
                for j in range(96):
                    bank_wait(t, 6)
                    inc(t.transpose(pb[6][:, j:j + 1],
                                    in_=modr[0:1, j * 128:(j + 1) * 128],
                                    identity=ident_f[0:1, 0:1]), sT)
            t1 = val(sT)

            @blk.vector
            def _(v):
                v.wait_ge(sT, t1)
                c = inc(v.tensor_copy(out=modc[:], in_=pb[6][:, 0:96]), sV)
                bank_guard[6] = (sV, c)
                v.wait_ge(sV, c)
                inc(v.tensor_copy(out=sa_c[:], in_=modc[:, 0:NT]), sV)
                inc(v.tensor_scalar(out=ca1[:], in0=modc[:, NT:2 * NT], scalar1=1.0,
                                    scalar2=None, op0=ADD, op1=BYP), sV)
                inc(v.tensor_copy(out=sb_c[:], in_=modc[:, 3 * NT:4 * NT]), sV)
                inc(v.tensor_scalar(out=cb1[:], in0=modc[:, 4 * NT:5 * NT], scalar1=1.0,
                                    scalar2=None, op0=ADD, op1=BYP), sV)
                for src_o, dst in ((2 * NT, ga_c), (5 * NT, gb_c)):
                    g0ap = modc[:, src_o:src_o + NT]
                    v.wait_ge(sV, val(sV))
                    inc(v.tensor_scalar(out=tmpc[:, 0:NT], in0=g0ap, scalar1=0.0,
                                        scalar2=2.0, op0=AluOpType.is_ge, op1=MUL), sV)
                    v.wait_ge(sV, val(sV))
                    inc(v.tensor_scalar(out=tmpc[:, 0:NT], in0=tmpc[:, 0:NT],
                                        scalar1=1.0, scalar2=None, op0=SUB, op1=BYP),
                        sV)
                    v.wait_ge(sV, val(sV))
                    inc(v.tensor_tensor(out=tmpc[:, NT:2 * NT], in0=g0ap,
                                        in1=tmpc[:, 0:NT], op=MUL), sV)
                    v.wait_ge(sV, val(sV))
                    inc(v.tensor_scalar(out=tmpc[:, NT:2 * NT], in0=tmpc[:, NT:2 * NT],
                                        scalar1=0.25, scalar2=None, op0=MAXO, op1=BYP),
                        sV)
                    v.wait_ge(sV, val(sV))
                    inc(v.tensor_tensor(out=dst[:], in0=tmpc[:, NT:2 * NT],
                                        in1=tmpc[:, 0:NT], op=MUL), sV)
            p0b_mod_v = val(sV)

        with ExitStack() as sc1:
            qstg = sb("qstg", [128, H], scope=sc1)
            qwb = sb("qwb", [128, H], BF16, scope=sc1)
            qsum = sb("qsum", [128, 32], scope=sc1)
            qrec = sb("qrec", [128, 32], scope=sc1)
            qscb = sb("qscb", [128, 32], BF16, scope=sc1)
            qrt = sb("qrt", [128, H], BF16, scope=sc1)
            qq = sb("qq", [128, H], scope=sc1)
            qqb = sb("qqb", [128, H], BF16, scope=sc1)
            qwide = sb("qwide", [128, H], BF16, scope=sc1)

            def quant_tile(ncols, d_wait):
                ng = ncols // 64

                @blk.vector
                def _(v):
                    v.wait_ge(sDw, d_wait)
                    inc(v.tensor_copy(out=qwb[:, 0:ncols], in_=qstg[:, 0:ncols]), sV)
                    v.wait_ge(sV, val(sV))
                    inc(v.tensor_reduce(out=qsum[:, 0:ng],
                                        in_=qwb.ap()[:, 0:ncols].rearrange(
                                            "p (g k) -> p g k", k=64),
                                        axis=AX.X, op=ADD, apply_absolute_value=True),
                        sV)
                    v.wait_ge(sV, val(sV))
                    inc(v.tensor_scalar(out=qscb[:, 0:ng], in0=qsum[:, 0:ng],
                                        scalar1=1.0 / 64, scalar2=1e-8,
                                        op0=MUL, op1=MAXO), sV)
                    v.wait_ge(sV, val(sV))
                    inc(v.reciprocal(out=qrec[:, 0:ng], in_=qscb[:, 0:ng]), sV)
                    v.wait_ge(sV, val(sV))
                    inc(v.tensor_tensor(
                        out=qrt.ap()[:, 0:ncols].rearrange("p (g k) -> p g k", k=64),
                        in0=qwb.ap()[:, 0:ncols].rearrange("p (g k) -> p g k", k=64),
                        in1=qrec.ap()[:, 0:ng].unsqueeze(2).broadcast_to([128, ng, 64]),
                        op=MUL), sV)
                    v.wait_ge(sV, val(sV))
                    inc(v.tensor_scalar(out=qq[:, 0:ncols], in0=qrt[:, 0:ncols],
                                        scalar1=RINT_C, scalar2=RINT_C,
                                        op0=ADD, op1=SUB), sV)
                    v.wait_ge(sV, val(sV))
                    inc(v.tensor_scalar(out=qq[:, 0:ncols], in0=qq[:, 0:ncols],
                                        scalar1=1.0, scalar2=-1.0,
                                        op0=MINO, op1=MAXO), sV)
                    v.wait_ge(sV, val(sV))
                    inc(v.tensor_tensor(
                        out=qqb.ap()[:, 0:ncols].rearrange("p (g k) -> p g k", k=64),
                        in0=qq.ap()[:, 0:ncols].rearrange("p (g k) -> p g k", k=64),
                        in1=qscb.ap()[:, 0:ng].unsqueeze(2).broadcast_to([128, ng, 64]),
                        op=MUL), sV)
                return val(sV)

            qdma = 0
            for rt in range(8):
                @blk.sync
                def _(sy, rt=rt):
                    sy.wait_ge(sDw, val(sDw))
                    sy.wait_ge(sV, val(sV))
                    if rt == 0:
                        sy.wait_ge(sV, p0b_mod_v)
                    inc(sy.dma_start(out=qstg[:],
                                     in_=wfc_s[rt * 128:(rt + 1) * 128, :]), sDw, 16)
                vq = quant_tile(H, val(sDw))
                v2 = tp_blocks_bf16(qqb, H, qwide, wait_v=vq,
                                    extra_waits=[(sDo, qdma)])

                @blk.sync
                def _(sy, rt=rt, v2=v2):
                    sy.wait_ge(sDo, val(sDo))
                    sy.wait_ge(sV, v2)
                    inc(sy.dma_start(out=wfcT_i[rt * 128:(rt + 1) * 128, :],
                                     in_=qwide[:]), sDo, 16)
                qdma = val(sDo)

            for rt in range(NT):
                @blk.sync
                def _(sy, rt=rt):
                    sy.wait_ge(sDw, val(sDw))
                    sy.wait_ge(sV, val(sV))
                    inc(sy.dma_start(out=qstg[:, 0:1024],
                                     in_=wpj_s[rt * 128:(rt + 1) * 128, :]), sDw, 16)
                vq = quant_tile(1024, val(sDw))
                v2 = tp_blocks_bf16(qqb, 1024, qwide, wait_v=vq,
                                    extra_waits=[(sDo, qdma)])

                @blk.sync
                def _(sy, rt=rt, v2=v2):
                    sy.wait_ge(sDo, val(sDo))
                    sy.wait_ge(sV, v2)
                    inc(sy.dma_start(out=wpjT_i[rt * 128:(rt + 1) * 128, :],
                                     in_=qwide[:, 0:1024]), sDo, 16)
                qdma = val(sDo)
            p0c_d = val(sDo)

            @blk.gpsimd
            def _(g):
                g.wait_ge(sDo, p0c_d)
                inc(g.collective_compute("AllGather", BYP, replica_groups=ALLG,
                                         ins=[wfcT_i.opt()], outs=[wfcT_o.opt()]), sG)
                inc(g.collective_compute("AllGather", BYP, replica_groups=ALLG,
                                         ins=[wpjT_i.opt()], outs=[wpjT_o.opt()]), sG)
            p0c_g = val(sG)

        with ExitStack() as sc2:
            tstg = sb("tstg", [128, H], scope=sc2)
            tstg1 = sb("tstg1", [128, H], scope=sc2)
            tb16 = sb("tb16", [128, H], BF16, scope=sc2)
            tb161 = sb("tb161", [128, H], BF16, scope=sc2)
            twide = sb("twide", [128, H], BF16, scope=sc2)
            twide1 = sb("twide1", [128, H], BF16, scope=sc2)
            tstgs, tb16s, twides = [tstg, tstg1], [tb16, tb161], [twide, twide1]
            cts_i = [0]
            cast_cons = [0, 0]
            cast_done = [0, 0]

            sp_dma = [0, 0]
            p0c_entry = (val(sV), val(sDo))

            def cts_strips(src, nrt, cols, dst_sp):
                for rt in range(nrt):
                    p = cts_i[0] % 2
                    cts_i[0] += 1

                    @blk.sync
                    def _(sy, rt=rt, p=p, cc=cast_cons[p], cd=cast_done[p]):
                        sem = sDs[p]
                        sy.wait_ge(sem, val(sem))
                        sy.wait_ge(sV, cd)
                        sy.wait_ge(sT, cc)
                        sy.wait_ge(sDo, p0c_entry[1])
                        inc(sy.dma_start(out=tstgs[p][:, 0:cols],
                                         in_=src[rt * 128:(rt + 1) * 128, :]),
                            sem, 16)
                    d_here = val(sDs[p])

                    @blk.vector
                    def _(v, d_here=d_here, p=p):
                        v.wait_ge(sDs[p], d_here)
                        inc(v.tensor_copy(out=tb16s[p][:, 0:cols],
                                          in_=tstgs[p][:, 0:cols]), sV)
                    v1 = val(sV)
                    cast_done[p] = v1
                    v2 = tp_blocks_bf16(tb16s[p], cols, twides[p], wait_v=v1,
                                        extra_waits=[(sDo, sp_dma[p])])
                    cast_cons[p] = val(sT)

                    @blk.sync
                    def _(sy, rt=rt, v2=v2, p=p):
                        sy.wait_ge(sDo, val(sDo))
                        sy.wait_ge(sV, v2)
                        inc(sy.dma_start(out=dst_sp[rt * 128:(rt + 1) * 128, 0:cols],
                                         in_=twide1[:, 0:cols] if p else twide[:, 0:cols]),
                            sDo, 16)
                    sp_dma[p] = val(sDo)

            cts_strips(wis, NT, H, wisT_sp)
            cts_strips(wig, NT, H, wigT_sp)
            cts_strips(wout_h, NT, DH, woutT_sp)

            @blk.sync
            def _(sy):
                sy.wait_ge(sDw, val(sDw))
                sy.wait_ge(sV, val(sV))
                inc(sy.dma_start(out=tstg[:, 0:DH], in_=wdt), sDw, 16)
            d1 = val(sDw)

            @blk.vector
            def _(v, d1=d1):
                v.wait_ge(sDw, d1)
                inc(v.tensor_copy(out=tb16[:, 0:DH], in_=tstg[:, 0:DH]), sV)
            v1 = val(sV)
            tp_blocks_bf16(tb16, DH, wdtT, wait_v=v1)

            @blk.sync
            def _(sy):
                sy.wait_ge(sDw, val(sDw))
                sy.wait_ge(sV, val(sV))
                inc(sy.dma_start(out=tstg[0:2 * N, 0:DH], in_=wbc), sDw, 16)
            d2 = val(sDw)

            @blk.vector
            def _(v, d2=d2):
                v.wait_ge(sDw, d2)
                inc(v.tensor_copy(out=tb16[0:2 * N, 0:DH], in_=tstg[0:2 * N, 0:DH]),
                    sV)
            v2 = val(sV)

            @blk.tensor
            def _(t, v2=v2):
                t.wait_ge(sV, v2)
                for cb in range(NT):
                    bank_wait(t, 6 + cb % 2)
                    inc(t.transpose(pb[6 + cb % 2][:, 0:16].bitcast(BF16),
                                    in_=tb16[0:2 * N, cb * 128:(cb + 1) * 128],
                                    identity=ident_b[0:2 * N, 0:2 * N]), sT)
                    t_here = val(sT)

                    @blk.vector
                    def _(v, cb=cb, t_here=t_here):
                        v.wait_ge(sT, t_here)
                        c = inc(v.tensor_copy(
                            out=wbcT[:, cb * 32:(cb + 1) * 32],
                            in_=pb[6 + cb % 2][:, 0:16].bitcast(BF16)), sV)
                        bank_guard[6 + cb % 2] = (sV, c)

            @blk.sync
            def _(sy):
                sy.wait_ge(sDw, val(sDw))
                sy.wait_ge(sV, val(sV))
                inc(sy.dma_start(
                    out=tstg.ap()[:, 0:DH].rearrange("p (t r) -> p t r", r=128),
                    in_=wdp.rearrange("(t p) r -> p t r", p=128)), sDw, 16)
            d3 = val(sDw)

            @blk.vector
            def _(v, d3=d3):
                v.wait_ge(sDw, d3)
                inc(v.tensor_copy(out=tb16[:, 0:DH], in_=tstg[:, 0:DH]), sV)
            v3 = val(sV)
            tp_blocks_bf16(tb16, DH, wdpT, wait_v=v3)

            @blk.sync
            def _(sy):
                sy.wait_ge(sDw, val(sDw))
                sy.wait_ge(sV, val(sV))
                inc(sy.dma_start(
                    out=tstg.ap()[:, 0:NT * KC].rearrange("p (t k) -> p t k", k=KC),
                    in_=cw.rearrange("(t p) k -> p t k", p=128)), sDw, 16)
            d4 = val(sDw)

            @blk.vector
            def _(v, d4=d4):
                v.wait_ge(sDw, d4)
                for tt in range(NT):
                    for j in range(KC):
                        inc(v.tensor_scalar(
                            out=diagw[:, (tt * KC + j) * 128:(tt * KC + j + 1) * 128],
                            in0=ident_b[:],
                            scalar1=tstg[:, tt * KC + j:tt * KC + j + 1],
                            scalar2=None, op0=MUL, op1=BYP), sV)
            p0_diag_v = val(sV)

        with ExitStack() as sc3:
            h1 = sb("h1", [128, NT * L], BF16, scope=sc3)

            p0d_spill = val(sDo)
            p0d_tp = val(sT)
            prev_tp_t = 0
            prev_apply_v = 0
            for lt in range(NT):
                half = lt % 2

                @blk.sync
                def _(sy, lt=lt, prev_apply_v=prev_apply_v):
                    sy.wait_ge(sDx, val(sDx))
                    if lt > 0:
                        sy.wait_ge(sV, prev_apply_v)
                    inc(sy.dma_start(out=xs[:], in_=x_b[lt * 128:(lt + 1) * 128, :]),
                        sDx, 16)
                d_here = val(sDx)

                @blk.scalar
                def _(scl, d_here=d_here, half=half, prev_tp_t=prev_tp_t,
                      prev_apply_v=prev_apply_v):
                    scl.wait_ge(sDx, d_here)
                    scl.wait_ge(sT, prev_tp_t)
                    scl.wait_ge(sV, prev_apply_v)
                    inc(scl.activation(out=tbf2[:, half * H:(half + 1) * H],
                                       in_=xs[:], func=AF.Square,
                                       accum_out=colw[:, 0:1]), sS)
                    scl.wait_ge(sS, val(sS))
                    inc(scl.activation(out=tbf2[:, half * H:(half + 1) * H],
                                       in_=xs[:], func=AF.Copy,
                                       accum_out=colw[:, 1:2]), sS)
                s_here = val(sS)

                @blk.vector
                def _(v, s_here=s_here):
                    v.wait_ge(sS, s_here)
                    inc(v.tensor_scalar(out=colw[:, 2:3], in0=colw[:, 1:2],
                                        scalar1=1.0 / H, scalar2=None, op0=MUL,
                                        op1=BYP), sV)
                    v.wait_ge(sV, val(sV))
                    inc(v.tensor_tensor(out=colw[:, 3:4], in0=colw[:, 2:3],
                                        in1=colw[:, 2:3], op=MUL), sV)
                    v.wait_ge(sV, val(sV))
                    inc(v.scalar_tensor_tensor(out=colw[:, 4:5], in0=colw[:, 0:1],
                                               scalar=1.0 / H, in1=colw[:, 3:4],
                                               op0=MUL, op1=SUB), sV)
                    v.wait_ge(sV, val(sV))
                    inc(v.tensor_scalar(out=colw[:, 4:5], in0=colw[:, 4:5],
                                        scalar1=LN_EPS, scalar2=None,
                                        op0=ADD, op1=BYP), sV)
                v_here = val(sV)

                @blk.scalar
                def _(scl, v_here=v_here):
                    scl.wait_ge(sV, v_here)
                    inc(scl.activation(out=colw[:, 5:6], in_=colw[:, 4:5],
                                       func=AF.Sqrt), sS)
                s2 = val(sS)

                @blk.vector
                def _(v, s2=s2, half=half):
                    v.wait_ge(sS, s2)
                    inc(v.reciprocal(out=colw[:, 6:7], in_=colw[:, 5:6]), sV)
                    v.wait_ge(sV, val(sV))
                    inc(v.tensor_scalar(out=tbf2[:, half * H:(half + 1) * H],
                                        in0=xs[:], scalar1=colw[:, 2:3],
                                        scalar2=colw[:, 6:7], op0=SUB, op1=MUL), sV)
                prev_apply_v = val(sV)

                if half == 1:
                    @blk.tensor
                    def _(t, lt=lt, prev_apply_v=prev_apply_v):
                        t.wait_ge(sV, prev_apply_v)
                        for hb in range(NT):
                            bk = hb % 8
                            bank_wait(t, bk)
                            for j in range(2):
                                inc(t.transpose(
                                    pb[bk][:, j * 128:(j + 1) * 128],
                                    in_=tbf2[:, j * H + hb * 128:
                                             j * H + (hb + 1) * 128],
                                    identity=ident_f[:]), sT)
                            t_here = val(sT)

                            @blk.scalar
                            def _(scl, lt=lt, hb=hb, bk=bk, t_here=t_here):
                                scl.wait_ge(sT, t_here)
                                if lt == 1 and hb == 0:
                                    scl.wait_ge(sV, p0b_mod_v)
                                    scl.wait_ge(sDo, p0d_spill)
                                c = inc(scl.activation(
                                    out=h1[:, hb * L + (lt - 1) * 128:
                                           hb * L + (lt + 1) * 128],
                                    in_=pb[bk][:, 0:256], func=AF.Identity,
                                    scale=ca1[:, hb:hb + 1],
                                    bias=sa_c[:, hb:hb + 1]), sS)
                                bank_guard[bk] = (sS, c)
                    prev_tp_t = val(sT)
            p1_s = val(sS)

            wstg = [sb("wstg0", [128, H], BF16, scope=sc3),
                    sb("wstg1", [128, H], BF16, scope=sc3)]
            gstg = [sb("gstg0", [128, H], BF16, scope=sc3),
                    sb("gstg1", [128, H], BF16, scope=sc3)]
            hpre = sb("hpre", [128, 3 + L], BF16, scope=sc3)
            hst = [sb("hst0", [128, L], BF16, scope=sc3),
                   sb("hst1", [128, L], BF16, scope=sc3)]
            hsf = [sb("hsf0", [128, 512], scope=sc3),
                   sb("hsf1", [128, 512], scope=sc3)]
            sgbb = [sb("sgbb0", [128, 512], BF16, scope=sc3),
                    sb("sgbb1", [128, 512], BF16, scope=sc3)]
            tsw = sb("tsw", [128, L], scope=sc3)
            bcw = sb("bcw", [32, L], scope=sc3)

            @blk.vector
            def _(v):
                v.wait_ge(sV, p0_diag_v)
                v.wait_ge(sDo, p0d_spill)
                inc(v.memset(hpre[:, 0:3], 0.0), sV)
                inc(v.memset(tsw[:], 0.0), sV)
                inc(v.memset(bcw[:], 0.0), sV)
            p2_pad_v = val(sV)

            w_cons = [0, 0]
            prev_conv_t = 0
            hs_dma = [0, 0]
            sil_dma = [0, 0]
            hst_cons = [0, 0]
            tsbc_add_v = 0
            for dt_i in range(NT):
                @blk.sync
                def _(sy, dt_i=dt_i):
                    sy.wait_ge(sDw, val(sDw))
                    sy.wait_ge(sT, w_cons[dt_i % 2])
                    if dt_i == 0:
                        sy.wait_ge(sDo, p0d_spill)
                    inc(sy.dma_start(out=wstg[dt_i % 2][:],
                                     in_=wisT_sp[dt_i * 128:(dt_i + 1) * 128, :]),
                        sDw, 16)
                    inc(sy.dma_start(out=gstg[dt_i % 2][:],
                                     in_=wigT_sp[dt_i * 128:(dt_i + 1) * 128, :]),
                        sDw, 16)
                d_here = val(sDw)
                w0, w1 = wstg[dt_i % 2], gstg[dt_i % 2]

                for fi in range(4):
                    @blk.tensor
                    def _(t, dt_i=dt_i, fi=fi, d_here=d_here):
                        t.wait_ge(sDw, d_here)
                        if dt_i == 0 and fi == 0:
                            t.wait_ge(sS, p1_s)
                        bank_wait(t, fi % 2)
                        for kb in range(NT):
                            inc(t.matmul(
                                pb[fi % 2][:, 0:512],
                                lhsT=w0[:, kb * 128:(kb + 1) * 128],
                                rhs=h1[:, kb * L + fi * 512:kb * L + (fi + 1) * 512],
                                start=(kb == 0), stop=(kb == NT - 1)), sT)
                    t_here = val(sT)

                    @blk.scalar
                    def _(scl, dt_i=dt_i, fi=fi, t_here=t_here,
                          prev_conv_t=prev_conv_t):
                        scl.wait_ge(sT, t_here)
                        scl.wait_ge(sT, prev_conv_t)
                        if dt_i == 0 and fi == 0:
                            scl.wait_ge(sV, p2_pad_v)
                            scl.wait_ge(sDo, p0d_spill)
                        c = inc(scl.activation(
                            out=hpre[:, 3 + fi * 512:3 + (fi + 1) * 512],
                            in_=pb[fi % 2][:, 0:512], func=AF.Copy), sS)
                        bank_guard[fi % 2] = (sS, c)
                p2_hpre_s = val(sS)

                for fi in range(4):
                    @blk.tensor
                    def _(t, dt_i=dt_i, fi=fi, p2_hpre_s=p2_hpre_s):
                        t.wait_ge(sS, p2_hpre_s)
                        if dt_i == 0 and fi == 0:
                            t.wait_ge(sV, p2_pad_v)
                        bank_wait(t, 2 + fi % 2)
                        for j in range(KC):
                            inc(t.matmul(
                                pb[2 + fi % 2][:, 0:512],
                                lhsT=diagw[:, (dt_i * KC + j) * 128:
                                           (dt_i * KC + j + 1) * 128],
                                rhs=hpre[:, fi * 512 + j:fi * 512 + j + 512],
                                start=(j == 0), stop=(j == KC - 1)), sT)
                    t_here = val(sT)
                    if fi == 3:
                        prev_conv_t = t_here

                    @blk.scalar
                    def _(scl, dt_i=dt_i, fi=fi, t_here=t_here, hsd=hs_dma[fi % 2],
                          hc=hst_cons[dt_i % 2]):
                        scl.wait_ge(sT, t_here)
                        scl.wait_ge(sDp[fi % 2], hsd)
                        scl.wait_ge(sT, hc)
                        inc(scl.activation(out=hsf[fi % 2][:, 0:512],
                                           in_=pb[2 + fi % 2][:, 0:512],
                                           func=AF.Silu, bias=cbcol[:, dt_i:dt_i + 1]),
                            sS)
                        scl.wait_ge(sS, val(sS))
                        c = inc(scl.activation(
                            out=hst[dt_i % 2][:, fi * 512:(fi + 1) * 512],
                            in_=pb[2 + fi % 2][:, 0:512],
                            func=AF.Silu, bias=cbcol[:, dt_i:dt_i + 1]), sS)
                        bank_guard[2 + fi % 2] = (sS, c)
                    s_here = val(sS)

                    @blk.sync
                    def _(sy, dt_i=dt_i, fi=fi, s_here=s_here):
                        sem = sDp[fi % 2]
                        sy.wait_ge(sem, val(sem))
                        sy.wait_ge(sS, s_here)
                        inc(sy.dma_start(
                            out=hs_sp[dt_i * 128:(dt_i + 1) * 128,
                                      fi * 512:(fi + 1) * 512],
                            in_=hsf[fi % 2][:, 0:512]), sem, 16)
                    hs_dma[fi % 2] = val(sDp[fi % 2])
                p2_hst_s = val(sS)

                for fi in range(4):
                    @blk.tensor
                    def _(t, dt_i=dt_i, fi=fi, d_here=d_here):
                        t.wait_ge(sDw, d_here)
                        bank_wait(t, 4 + fi % 2)
                        for kb in range(NT):
                            inc(t.matmul(
                                pb[4 + fi % 2][:, 0:512],
                                lhsT=w1[:, kb * 128:(kb + 1) * 128],
                                rhs=h1[:, kb * L + fi * 512:kb * L + (fi + 1) * 512],
                                start=(kb == 0), stop=(kb == NT - 1)), sT)
                    t_here = val(sT)
                    if fi == 3:
                        w_cons[dt_i % 2] = t_here

                    @blk.scalar
                    def _(scl, fi=fi, t_here=t_here, sid=sil_dma[fi % 2]):
                        scl.wait_ge(sT, t_here)
                        scl.wait_ge(sDq[fi % 2], sid)
                        c = inc(scl.activation(out=sgbb[fi % 2][:, 0:512],
                                               in_=pb[4 + fi % 2][:, 0:512],
                                               func=AF.Silu), sS)
                        bank_guard[4 + fi % 2] = (sS, c)
                    s_here = val(sS)

                    @blk.sync
                    def _(sy, dt_i=dt_i, fi=fi, s_here=s_here):
                        sem = sDq[fi % 2]
                        sy.wait_ge(sem, val(sem))
                        sy.wait_ge(sS, s_here)
                        inc(sy.dma_start(
                            out=silu_sp[dt_i * 128:(dt_i + 1) * 128,
                                        fi * 512:(fi + 1) * 512],
                            in_=sgbb[fi % 2][:, 0:512]), sem, 16)
                    sil_dma[fi % 2] = val(sDq[fi % 2])

                for fi in range(4):
                    @blk.tensor
                    def _(t, dt_i=dt_i, fi=fi, p2_hst_s=p2_hst_s):
                        t.wait_ge(sS, p2_hst_s)
                        bank_wait(t, 6 + fi % 2)
                        inc(t.matmul(pb[6 + fi % 2][:, 0:512],
                                     lhsT=wdtT[:, dt_i * 128:(dt_i + 1) * 128],
                                     rhs=hst[dt_i % 2][:, fi * 512:(fi + 1) * 512],
                                     start=True, stop=True), sT)
                    t_here = val(sT)

                    @blk.vector
                    def _(v, fi=fi, t_here=t_here):
                        v.wait_ge(sT, t_here)
                        c = inc(v.tensor_tensor(out=tsw[:, fi * 512:(fi + 1) * 512],
                                                in0=tsw[:, fi * 512:(fi + 1) * 512],
                                                in1=pb[6 + fi % 2][:, 0:512],
                                                op=ADD), sV)
                        bank_guard[6 + fi % 2] = (sV, c)
                for fi in range(4):
                    @blk.tensor
                    def _(t, dt_i=dt_i, fi=fi):
                        bank_wait(t, 6 + fi % 2)
                        inc(t.matmul(pb[6 + fi % 2][0:32, 0:512],
                                     lhsT=wbcT[:, dt_i * 32:(dt_i + 1) * 32],
                                     rhs=hst[dt_i % 2][:, fi * 512:(fi + 1) * 512],
                                     start=True, stop=True), sT)
                    t_here = val(sT)

                    @blk.vector
                    def _(v, fi=fi, t_here=t_here):
                        v.wait_ge(sT, t_here)
                        c = inc(v.tensor_tensor(out=bcw[:, fi * 512:(fi + 1) * 512],
                                                in0=bcw[:, fi * 512:(fi + 1) * 512],
                                                in1=pb[6 + fi % 2][0:32, 0:512],
                                                op=ADD), sV)
                        bank_guard[6 + fi % 2] = (sV, c)
                hst_cons[dt_i % 2] = val(sT)
                tsbc_add_v = val(sV)
            p2_s = val(sS)

            @blk.sync
            def _(sy):
                sy.wait_ge(sDo, val(sDo))
                sy.wait_ge(sV, tsbc_add_v)
                inc(sy.dma_start(out=dtbc_i[0:R, :], in_=tsw[:]), sDo, 16)
                inc(sy.dma_start(out=dtbc_i[R:R + 2 * N, :], in_=bcw[:]), sDo, 16)
            p2b_d = val(sDo)

            @blk.gpsimd
            def _(g):
                g.wait_ge(sDo, p2b_d)
                inc(g.collective_compute("AllReduce", ADD, replica_groups=PAIRS,
                                         ins=[dtbc_i.opt()], outs=[dtbc_o.opt()]), sG)
            p2b_g = val(sG)

            @blk.sync
            def _(sy):
                sy.wait_ge(sDx, val(sDx))
                sy.wait_ge(sG, p2b_g)
                inc(sy.dma_start(out=tsw[:], in_=dtbc_o[0:R, :]), sDx, 16)
                inc(sy.dma_start(out=bcw[:], in_=dtbc_o[R:R + 2 * N, :]), sDx, 16)
            p2b_d2 = val(sDx)

            @blk.vector
            def _(v):
                v.wait_ge(sDx, p2b_d2)
                inc(v.tensor_copy(out=ts_b[:], in_=tsw[:]), sV)
                inc(v.tensor_copy(out=bcb[:], in_=bcw[:]), sV)
            p2b_v2 = val(sV)

            p2b_d3 = p2b_d2

            zb = [sb("zb0", [128, L], scope=sc3), sb("zb1", [128, L], scope=sc3)]
            az = [sb("az0", [128, L], scope=sc3), sb("az1", [128, L], scope=sc3)]

            zb_dma = [0, 0]
            for dt_i in range(NT):
                pz = dt_i % 2
                for fi in range(4):
                    @blk.tensor
                    def _(t, dt_i=dt_i, fi=fi):
                        if dt_i == 0 and fi == 0:
                            t.wait_ge(sV, p2b_v2)
                        bank_wait(t, fi % 2)
                        inc(t.matmul(pb[fi % 2][:, 0:512],
                                     lhsT=wdpT[:, dt_i * 128:(dt_i + 1) * 128],
                                     rhs=ts_b[:, fi * 512:(fi + 1) * 512],
                                     start=True, stop=True), sT)
                    t_here = val(sT)

                    @blk.scalar
                    def _(scl, dt_i=dt_i, fi=fi, t_here=t_here, pz=pz,
                          zdp=zb_dma[dt_i % 2]):
                        scl.wait_ge(sT, t_here)
                        scl.wait_ge(sDo, zdp)
                        c = inc(scl.activation(out=zb[pz][:, fi * 512:(fi + 1) * 512],
                                               in_=pb[fi % 2][:, 0:512], func=AF.Identity,
                                               bias=bdcol[:, dt_i:dt_i + 1]), sS)
                        bank_guard[fi % 2] = (sS, c)
                s_here = val(sS)

                @blk.vector
                def _(v, s_here=s_here, pz=pz):
                    v.wait_ge(sS, s_here)
                    inc(v.tensor_scalar(out=az[pz][:], in0=zb[pz][:], scalar1=-1.0,
                                        scalar2=None, op0=MUL, op1=BYP), sV)
                    v.wait_ge(sV, val(sV))
                    inc(v.tensor_tensor(out=az[pz][:], in0=az[pz][:], in1=zb[pz][:],
                                        op=MAXO), sV)
                v2 = val(sV)

                @blk.scalar
                def _(scl, v2=v2, pz=pz):
                    scl.wait_ge(sV, v2)
                    inc(scl.activation(out=az[pz][:], in_=az[pz][:], func=AF.Exp,
                                       scale=-1.0), sS)
                    scl.wait_ge(sS, val(sS))
                    inc(scl.activation(out=az[pz][:], in_=az[pz][:], func=AF.Ln,
                                       bias=1.0), sS)
                s3 = val(sS)

                @blk.vector
                def _(v, s3=s3, pz=pz):
                    v.wait_ge(sS, s3)
                    inc(v.scalar_tensor_tensor(out=zb[pz][:], in0=zb[pz][:], scalar=0.0,
                                               in1=az[pz][:], op0=MAXO, op1=ADD), sV)
                v3 = val(sV)

                @blk.sync
                def _(sy, dt_i=dt_i, v3=v3, pz=pz):
                    sy.wait_ge(sDo, val(sDo))
                    sy.wait_ge(sV, v3)
                    inc(sy.dma_start(out=dt_sp[dt_i * 128:(dt_i + 1) * 128, :],
                                     in_=zb[pz][:]), sDo, 16)
                zb_dma[pz] = val(sDo)
            p3_d = val(sDo)

        with ExitStack() as sc4:
            sc4a = st.enter_context(ExitStack())
            s_int = sb("s_int", [128, LC * N], BF16, scope=sc4a)
            b_bc = sb("b_bc", [128, N * LC], BF16, scope=sc4a)
            c_bc = sb("c_bc", [128, LC * N], BF16, scope=sc4a)
            dAb = [sb("dA0", [128, LC], scope=sc4a), sb("dA1", [128, LC], scope=sc4a)]
            dtc = [sb("dtc0", [128, LC], scope=sc4a), sb("dtc1", [128, LC], scope=sc4a)]
            hsc = [sb("hsc0", [128, LC], scope=sc4a), sb("hsc1", [128, LC], scope=sc4a)]
            sic = [sb("sic0", [128, LC], BF16, scope=sc4a),
                   sb("sic1", [128, LC], BF16, scope=sc4a)]
            u_b = sb("u_b", [128, LC], BF16, scope=sc4a)
            dbu = sb("dbu", [128, LC], BF16, scope=sc4a)
            ych = sb("ych", [128, LC], scope=sc4a)
            y3c = [sb("y3c0", [128, LC], BF16, scope=sc4a),
                   sb("y3c1", [128, LC], BF16, scope=sc4a)]

            bstage = sb("bstage", [1, LC], BF16, scope=sc4a)
            cstage = sb("cstage", [1, LC], BF16, scope=sc4a)

            p2_spills = [(sDp[0], val(sDp[0])), (sDp[1], val(sDp[1])),
                         (sDq[0], val(sDq[0])), (sDq[1], val(sDq[1]))]
            IDX = [(c_, d_) for c_ in range(2) for d_ in range(NT)]
            pcons = {0: (0, 0), 1: (0, 0)}
            y3dma = {0: 0, 1: 0}
            loads = {}

            def p4_load(i):
                ch_, dt_ = IDX[i]
                p = i % 2
                sem = sDs[p]
                i_ = i

                @blk.sync
                def _(sy, ch_=ch_, dt_=dt_, p=p, sem=sem, pc=pcons[p], i=i):
                    sy.wait_ge(sem, val(sem))
                    if i < 2:
                        sy.wait_ge(sDo, p3_d)
                        for s_, c_ in p2_spills:
                            sy.wait_ge(s_, c_)
                    sy.wait_ge(sV, pc[0])
                    sy.wait_ge(sS, pc[1])
                    inc(sy.dma_start(out=dtc[p][:],
                                     in_=dt_sp[dt_ * 128:(dt_ + 1) * 128,
                                               ch_ * LC:(ch_ + 1) * LC]), sem, 16)
                    inc(sy.dma_start(out=hsc[p][:],
                                     in_=hs_sp[dt_ * 128:(dt_ + 1) * 128,
                                               ch_ * LC:(ch_ + 1) * LC]), sem, 16)
                    inc(sy.dma_start(out=sic[p][:],
                                     in_=silu_sp[dt_ * 128:(dt_ + 1) * 128,
                                                 ch_ * LC:(ch_ + 1) * LC]), sem, 16)
                loads[i] = val(sDs[i % 2])

            bcv = {}
            prev_tts = None
            p4_load(0)
            for ch in range(2):
                chunk_entry_v = val(sV)
                bst_mm = 0
                for n in range(N):
                    @blk.sync
                    def _(sy, ch=ch, n=n, bst_mm=bst_mm, chunk_entry_v=chunk_entry_v):
                        sy.wait_ge(sDx, val(sDx))
                        sy.wait_ge(sV, p2b_v2)
                        sy.wait_ge(sT, bst_mm)
                        if n == 0:
                            sy.wait_ge(sV, chunk_entry_v)
                        inc(sy.dma_start(out=bstage[:],
                                         in_=bcb[n:n + 1, ch * LC:(ch + 1) * LC]),
                            sDx, 16)
                        inc(sy.dma_start(out=cstage[:],
                                         in_=bcb[N + n:N + n + 1,
                                                 ch * LC:(ch + 1) * LC]), sDx, 16)
                    d_here = val(sDx)

                    @blk.tensor
                    def _(t, n=n, d_here=d_here):
                        t.wait_ge(sDx, d_here)
                        for fb in range(LC // 512):
                            bk = (n * 2 + fb) % 4
                            bank_wait(t, bk)
                            inc(t.matmul(pb[bk][:, 0:512], lhsT=ones_rb[:],
                                         rhs=bstage[0:1, fb * 512:(fb + 1) * 512],
                                         start=True, stop=True), sT)
                            t_here = val(sT)

                            @blk.vector
                            def _(v, n=n, fb=fb, bk=bk, t_here=t_here, ch=ch):
                                v.wait_ge(sT, t_here)
                                if ch == 0 and n == 0 and fb == 0:
                                    v.wait_ge(sDo, p3_d)
                                c = inc(v.tensor_copy(
                                    out=b_bc[:, n * LC + fb * 512:
                                             n * LC + (fb + 1) * 512],
                                    in_=pb[bk][:, 0:512]), sV)
                                bank_guard[bk] = (sV, c)
                        for fb in range(LC // 512):
                            bk = 4 + (n * 2 + fb) % 2
                            bank_wait(t, bk)
                            inc(t.matmul(pb[bk][:, 0:512], lhsT=ones_rb[:],
                                         rhs=cstage[0:1, fb * 512:(fb + 1) * 512],
                                         start=True, stop=True), sT)
                            t_here = val(sT)

                            @blk.vector
                            def _(v, n=n, fb=fb, bk=bk, t_here=t_here):
                                v.wait_ge(sT, t_here)
                                c = inc(v.tensor_copy(
                                    out=c_bc[:, n * LC + fb * 512:
                                             n * LC + (fb + 1) * 512],
                                    in_=pb[bk][:, 0:512]), sV)
                                bank_guard[bk] = (sV, c)
                    bst_mm = val(sT)
                bcv[ch] = val(sV)

                for dt_i in range(NT):
                    i = ch * NT + dt_i
                    if i + 1 < 2 * NT:
                        p4_load(i + 1)
                    p = i % 2
                    sem = sDs[p]
                    d_here = loads[i]

                    @blk.vector
                    def _(v, p=p, sem=sem, d_here=d_here, bcv_ch=bcv[ch]):
                        v.wait_ge(sem, d_here)
                        v.wait_ge(sV, bcv_ch)
                        inc(v.tensor_tensor(out=u_b[:], in0=dtc[p][:], in1=hsc[p][:],
                                            op=MUL), sV)
                    p4_u_v = val(sV)

                    tts_c = {}
                    for n in range(N):
                        @blk.scalar
                        def _(scl, dt_i=dt_i, n=n, p=p, sem=sem, d_here=d_here,
                              tts_c=tts_c, i=i, ptts=prev_tts):
                            scl.wait_ge(sem, d_here)
                            if i == 0 and n == 0:
                                scl.wait_ge(sDo, p3_d)
                            if n >= 2:
                                scl.wait_ge(sV, tts_c[n - 2])
                            elif ptts is not None:
                                scl.wait_ge(sV, ptts[N - 2 + n])
                            inc(scl.activation(
                                out=dAb[n % 2][:], in_=dtc[p][:], func=AF.Exp,
                                scale=acol[:, dt_i * N + n:dt_i * N + n + 1]), sS)
                        s_dA = val(sS)

                        @blk.vector
                        def _(v, ch=ch, dt_i=dt_i, n=n, p4_u_v=p4_u_v, s_dA=s_dA,
                              tts_c=tts_c):
                            v.wait_ge(sV, p4_u_v)
                            v.wait_ge(sS, s_dA)
                            if n > 0:
                                v.wait_ge(sV, tts_c[n - 1])
                            inc(v.tensor_tensor(out=dbu[:], in0=u_b[:],
                                                in1=b_bc[:, n * LC:(n + 1) * LC],
                                                op=MUL), sV)
                            v.wait_ge(sV, val(sV))
                            init = (0.0 if ch == 0
                                    else state[:, dt_i * N + n:dt_i * N + n + 1])
                            inc(v.tensor_tensor_scan(
                                out=s_int[:, n * LC:(n + 1) * LC],
                                data0=dAb[n % 2][:], data1=dbu[:], initial=init,
                                op0=MUL, op1=ADD), sV)
                            if ch == 0:
                                v.wait_ge(sV, val(sV))
                                inc(v.tensor_copy(
                                    out=state[:, dt_i * N + n:dt_i * N + n + 1],
                                    in_=s_int[:, (n + 1) * LC - 1:(n + 1) * LC]), sV)
                        tts_c[n] = val(sV)

                        @blk.gpsimd
                        def _(g, n=n, tn=tts_c[n]):
                            g.wait_ge(sV, tn)
                            inc(g.tensor_tensor(out=s_int[:, n * LC:(n + 1) * LC],
                                                in0=s_int[:, n * LC:(n + 1) * LC],
                                                in1=c_bc[:, n * LC:(n + 1) * LC],
                                                op=MUL), sGP)
                        gp_last = val(sGP)

                    @blk.vector
                    def _(v, ch=ch, dt_i=dt_i, p=p, gp_c=gp_last, y3p=y3dma[p]):
                        v.wait_ge(sGP, gp_c)
                        inc(v.tensor_tensor(out=ych[:], in0=s_int[:, 0:LC],
                                            in1=s_int[:, LC:2 * LC], op=ADD), sV)
                        for n in range(2, N):
                            v.wait_ge(sV, val(sV))
                            inc(v.tensor_tensor(out=ych[:], in0=ych[:],
                                                in1=s_int[:, n * LC:(n + 1) * LC],
                                                op=ADD), sV)
                        v.wait_ge(sV, val(sV))
                        inc(v.scalar_tensor_tensor(out=ych[:], in0=hsc[p][:],
                                                   scalar=dcol[:, dt_i:dt_i + 1],
                                                   in1=ych[:], op0=MUL, op1=ADD), sV)
                        v.wait_ge(sV, val(sV))
                        v.wait_ge(sDo, y3p)
                        inc(v.tensor_tensor(out=y3c[p][:], in0=ych[:], in1=sic[p][:],
                                            op=MUL), sV)
                    y3v = val(sV)
                    pcons[p] = (y3v, val(sS))
                    prev_tts = dict(tts_c)

                    @blk.sync
                    def _(sy, ch=ch, dt_i=dt_i, p=p, y3v=y3v):
                        sy.wait_ge(sDo, val(sDo))
                        sy.wait_ge(sV, y3v)
                        inc(sy.dma_start(
                            out=y3_sp[dt_i * 128:(dt_i + 1) * 128,
                                      ch * LC:(ch + 1) * LC],
                            in_=y3c[p][:]), sDo, 16)
                    y3dma[p] = val(sDo)
            p4_v = val(sV)
            p4_y3_d = val(sDo)

            sc4a.close()
            opw = [sb("opw0", [128, H], BF16, scope=sc4),
                   sb("opw1", [128, H], BF16, scope=sc4)]
            oev = [sb("oev0", [128, L], scope=sc4),
                   sb("oev1", [128, L], scope=sc4)]
            yst = [sb(f"yst{j}", [128, L], BF16, scope=sc4) for j in range(4)]
            ksem5 = [sDs[0], sDs[1], sDp[0], sDp[1]]

            prev_grp_t = 0
            oev_dma = [0, 0]
            yst_cons = [0, 0, 0, 0]
            for grp in range(8):
                @blk.sync
                def _(sy, grp=grp, prev_grp_t=prev_grp_t):
                    sy.wait_ge(sDw, val(sDw))
                    sy.wait_ge(sT, prev_grp_t)
                    if grp == 0:
                        sy.wait_ge(sDo, p4_y3_d)
                    inc(sy.dma_start(out=opw[0][:],
                                     in_=woutT_sp[(2 * grp) * 128:(2 * grp + 1) * 128,
                                                  :]), sDw, 16)
                    inc(sy.dma_start(out=opw[1][:],
                                     in_=woutT_sp[(2 * grp + 1) * 128:
                                                  (2 * grp + 2) * 128, :]), sDw, 16)
                d_here = val(sDw)

                for kb in range(NT):
                    sl = kb % 4

                    @blk.sync
                    def _(sy, grp=grp, kb=kb, sl=sl):
                        sem = ksem5[sl]
                        sy.wait_ge(sem, val(sem))
                        sy.wait_ge(sT, yst_cons[sl])
                        if grp == 0 and kb < 4:
                            sy.wait_ge(sDo, p4_y3_d)
                        inc(sy.dma_start(out=yst[sl][:],
                                         in_=y3_sp[kb * 128:(kb + 1) * 128, :]),
                            sem, 16)
                    dk_here = val(ksem5[sl])

                    @blk.tensor
                    def _(t, grp=grp, kb=kb, sl=sl, d_here=d_here, dk_here=dk_here):
                        t.wait_ge(sDw, d_here)
                        t.wait_ge(ksem5[sl], dk_here)
                        if grp == 0 and kb == 0:
                            t.wait_ge(sV, p4_v)
                        if kb == 0:
                            for i in range(8):
                                bank_wait(t, i)
                        for hh in range(2):
                            for fi in range(4):
                                inc(t.matmul(
                                    pb[hh * 4 + fi][:, 0:512],
                                    lhsT=opw[hh][:, kb * 128:(kb + 1) * 128],
                                    rhs=yst[sl][:, fi * 512:(fi + 1) * 512],
                                    start=(kb == 0), stop=(kb == NT - 1)), sT)
                    yst_cons[sl] = val(sT)
                prev_grp_t = val(sT)
                mm_end = prev_grp_t

                for hh in range(2):
                    @blk.vector
                    def _(v, hh=hh, mm_end=mm_end, od=oev_dma[hh], grp=grp):
                        v.wait_ge(sT, mm_end)
                        v.wait_ge(sDp[hh], od)
                        if grp == 0:
                            v.wait_ge(sDo, p4_y3_d)
                        for fi in range(4):
                            c = inc(v.tensor_copy(
                                out=oev[hh][:, fi * 512:(fi + 1) * 512],
                                in_=pb[hh * 4 + fi][:, 0:512]), sV)
                            bank_guard[hh * 4 + fi] = (sV, c)
                    v_here = val(sV)

                    @blk.sync
                    def _(sy, grp=grp, hh=hh, v_here=v_here):
                        sem = sDp[hh]
                        sy.wait_ge(sem, val(sem))
                        sy.wait_ge(sV, v_here)
                        ht = grp * 2 + hh
                        inc(sy.dma_start(out=op_i[ht * 128:(ht + 1) * 128, :],
                                         in_=oev[hh][:, 0:LH]), sem, 16)
                        inc(sy.dma_start(out=op_i[H + ht * 128:H + (ht + 1) * 128, :],
                                         in_=oev[hh][:, LH:L]), sem, 16)
                    oev_dma[hh] = val(sDp[hh])
            p5a_d = [(sDp[0], val(sDp[0])), (sDp[1], val(sDp[1]))]

            @blk.gpsimd
            def _(g):
                for s_, c_ in p5a_d:
                    g.wait_ge(s_, c_)
                inc(g.collective_compute("ReduceScatter", ADD, replica_groups=PAIRS,
                                         ins=[op_i.opt()], outs=[op_o.opt()]), sG)
            p5a_g = val(sG)

        with ExitStack() as sc5:
            x2 = sb("x2", [128, NT * LH], scope=sc5)
            xs2 = tbf2[:, 0:H]
            att = xs[:, 0:LH]

            prev_xt = 0
            for lt in range(8):
                @blk.sync
                def _(sy, lt=lt, prev_xt=prev_xt):
                    sy.wait_ge(sDx, val(sDx))
                    sy.wait_ge(sT, prev_xt)
                    if lt == 0:
                        for s_, c_ in p5a_d:
                            sy.wait_ge(s_, c_)
                    inc(sy.dma_start(out=xs2[:], in_=x_h[lt * 128:(lt + 1) * 128, :]),
                        sDx, 16)
                d_here = val(sDx)

                @blk.tensor
                def _(t, lt=lt, d_here=d_here):
                    t.wait_ge(sDx, d_here)
                    for hb in range(NT):
                        bank_wait(t, hb % 2)
                        inc(t.transpose(pb[hb % 2][:, 0:128],
                                        in_=xs2[:, hb * 128:(hb + 1) * 128],
                                        identity=ident_f[:]), sT)
                        t_here = val(sT)

                        @blk.vector
                        def _(v, lt=lt, hb=hb, t_here=t_here):
                            v.wait_ge(sT, t_here)
                            if lt == 0 and hb == 0:
                                for s_, c_ in p5a_d:
                                    v.wait_ge(s_, c_)
                            c = inc(v.tensor_copy(
                                out=x2[:, hb * LH + lt * 128:hb * LH + (lt + 1) * 128],
                                in_=pb[hb % 2][:, 0:128]), sV)
                            bank_guard[hb % 2] = (sV, c)
                prev_xt = val(sT)
            p5b_v = val(sV)

            for ht in range(NT):
                @blk.sync
                def _(sy, ht=ht):
                    sy.wait_ge(sDx, val(sDx))
                    sy.wait_ge(sG, p5a_g)
                    if ht > 0:
                        sy.wait_ge(sV, val(sV))
                    inc(sy.dma_start(out=att[:],
                                     in_=op_o[ht * 128:(ht + 1) * 128, :]), sDx, 16)
                d_here = val(sDx)

                @blk.vector
                def _(v, ht=ht, d_here=d_here):
                    v.wait_ge(sDx, d_here)
                    v.wait_ge(sV, p5b_v)
                    inc(v.scalar_tensor_tensor(
                        out=x2[:, ht * LH:(ht + 1) * LH], in0=att[:],
                        scalar=ga_c[:, ht:ht + 1],
                        in1=x2[:, ht * LH:(ht + 1) * LH], op0=MUL, op1=ADD), sV)
            p5b_v2 = val(sV)

            h2 = sb("h2", [128, NT * LH], BF16, scope=sc5)
            mrow = sb("mrow", [1, LH], scope=sc5)
            rrow = sb("rrow", [1, LH], scope=sc5)
            m_bc = sb("m_bc", [128, LH], scope=sc5)
            r_bc = sb("r_bc", [128, LH], scope=sc5)
            tb2 = sb("tb2", [128, LH], scope=sc5)
            sqf = tb2

            stat_t = 0
            for ht in range(NT):
                @blk.scalar
                def _(scl, ht=ht, stat_t=stat_t):
                    scl.wait_ge(sV, p5b_v2)
                    scl.wait_ge(sT, stat_t)
                    inc(scl.activation(out=sqf[:], in_=x2[:, ht * LH:(ht + 1) * LH],
                                       func=AF.Square), sS)
                s_here = val(sS)

                @blk.tensor
                def _(t, ht=ht, s_here=s_here):
                    t.wait_ge(sS, s_here)
                    if ht == 0:
                        for i in (4, 5, 6, 7):
                            bank_wait(t, i)
                    for fi in range(2):
                        inc(t.matmul(pb[4 + fi][0:1, 0:512],
                                     lhsT=ones_cf[:],
                                     rhs=x2[:, ht * LH + fi * 512:
                                            ht * LH + (fi + 1) * 512],
                                     start=(ht == 0), stop=(ht == NT - 1)), sT)
                        inc(t.matmul(pb[6 + fi][0:1, 0:512],
                                     lhsT=ones_cf[:],
                                     rhs=sqf[:, fi * 512:(fi + 1) * 512],
                                     start=(ht == 0), stop=(ht == NT - 1)), sT)
                stat_t = val(sT)
            p6_t = val(sT)

            @blk.vector
            def _(v):
                v.wait_ge(sT, p6_t)
                for fi in range(2):
                    c = inc(v.tensor_copy(out=mrow[0:1, fi * 512:(fi + 1) * 512],
                                          in_=pb[4 + fi][0:1, 0:512]), sV)
                    bank_guard[4 + fi] = (sV, c)
                    c = inc(v.tensor_copy(out=rrow[0:1, fi * 512:(fi + 1) * 512],
                                          in_=pb[6 + fi][0:1, 0:512]), sV)
                    bank_guard[6 + fi] = (sV, c)
                v.wait_ge(sV, val(sV))
                inc(v.tensor_scalar(out=mrow[:], in0=mrow[:], scalar1=1.0 / H,
                                    scalar2=None, op0=MUL, op1=BYP), sV)
                v.wait_ge(sV, val(sV))
                inc(v.tensor_tensor(out=tb2[0:1, 0:LH], in0=mrow[:], in1=mrow[:],
                                    op=MUL), sV)
                v.wait_ge(sV, val(sV))
                inc(v.scalar_tensor_tensor(out=rrow[:], in0=rrow[:], scalar=1.0 / H,
                                           in1=tb2[0:1, 0:LH], op0=MUL, op1=SUB), sV)
                v.wait_ge(sV, val(sV))
                inc(v.tensor_scalar(out=rrow[:], in0=rrow[:], scalar1=LN_EPS,
                                    scalar2=None, op0=ADD, op1=BYP), sV)
            p6_v = val(sV)

            @blk.scalar
            def _(scl):
                scl.wait_ge(sV, p6_v)
                inc(scl.activation(out=rrow[:], in_=rrow[:], func=AF.Sqrt), sS)
            p6_s = val(sS)

            @blk.vector
            def _(v):
                v.wait_ge(sS, p6_s)
                inc(v.reciprocal(out=rrow[:], in_=rrow[:]), sV)
            p6_v2 = val(sV)

            @blk.tensor
            def _(t):
                t.wait_ge(sV, p6_v2)
                for fi in range(2):
                    bank_wait(t, 6)
                    inc(t.matmul(pb[6][:, 0:512], lhsT=ones_rf[:],
                                 rhs=mrow[0:1, fi * 512:(fi + 1) * 512],
                                 start=True, stop=True), sT)
                    t_here = val(sT)

                    @blk.vector
                    def _(v, fi=fi, t_here=t_here):
                        v.wait_ge(sT, t_here)
                        c = inc(v.tensor_copy(out=m_bc[:, fi * 512:(fi + 1) * 512],
                                              in_=pb[6][:, 0:512]), sV)
                        bank_guard[6] = (sV, c)
                for fi in range(2):
                    bank_wait(t, 7)
                    inc(t.matmul(pb[7][:, 0:512], lhsT=ones_rf[:],
                                 rhs=rrow[0:1, fi * 512:(fi + 1) * 512],
                                 start=True, stop=True), sT)
                    t_here = val(sT)

                    @blk.vector
                    def _(v, fi=fi, t_here=t_here):
                        v.wait_ge(sT, t_here)
                        c = inc(v.tensor_copy(out=r_bc[:, fi * 512:(fi + 1) * 512],
                                              in_=pb[7][:, 0:512]), sV)
                        bank_guard[7] = (sV, c)
            p6_bc_v = val(sV)

            prev_h2_s = 0
            for ht in range(NT):
                @blk.vector
                def _(v, ht=ht, prev_h2_s=prev_h2_s):
                    v.wait_ge(sV, p6_bc_v)
                    v.wait_ge(sS, prev_h2_s)
                    inc(v.tensor_tensor(out=tb2[:], in0=x2[:, ht * LH:(ht + 1) * LH],
                                        in1=m_bc[:], op=SUB), sV)
                    v.wait_ge(sV, val(sV))
                    inc(v.tensor_tensor(out=tb2[:], in0=tb2[:], in1=r_bc[:], op=MUL),
                        sV)
                v_here = val(sV)

                @blk.scalar
                def _(scl, ht=ht, v_here=v_here):
                    scl.wait_ge(sV, v_here)
                    inc(scl.activation(out=h2[:, ht * LH:(ht + 1) * LH], in_=tb2[:],
                                       func=AF.Identity, scale=cb1[:, ht:ht + 1],
                                       bias=sb_c[:, ht:ht + 1]), sS)
                prev_h2_s = val(sS)
            p6_h2_s = val(sS)

            fw = [sb("fw0", [128, H], BF16, scope=sc5),
                  sb("fw1", [128, H], BF16, scope=sc5)]
            hbb = [sb("hbb0", [128, LH], BF16, scope=sc5),
                   sb("hbb1", [128, LH], BF16, scope=sc5)]
            hq = [sb("hq0", [128, LH], BF16, scope=sc5),
                  sb("hq1", [128, LH], BF16, scope=sc5)]
            msr = mrow

            fw_cons = [0, 0]
            hb_dma = [0, 0]
            ms_t = 0
            hq_s = [0, 0]
            for m in range(HID // 128):
                @blk.sync
                def _(sy, m=m):
                    sy.wait_ge(sDw, val(sDw))
                    sy.wait_ge(sT, fw_cons[m % 2])
                    if m == 0:
                        sy.wait_ge(sG, p0c_g)
                    inc(sy.dma_start(
                        out=fw[m % 2][:],
                        in_=wfcT_o[m * 128:(m + 1) * 128, :]), sDw, 16)
                d_here = val(sDw)

                @blk.tensor
                def _(t, m=m, d_here=d_here):
                    t.wait_ge(sDw, d_here)
                    if m == 0:
                        t.wait_ge(sS, p6_h2_s)
                    bank_wait(t, (m % 2) * 2)
                    bank_wait(t, 1 + (m % 2) * 2)
                    for kb in range(NT):
                        for fi in range(2):
                            inc(t.matmul(
                                pb[fi + (m % 2) * 2][:, 0:512],
                                lhsT=fw[m % 2][:, kb * 128:(kb + 1) * 128],
                                rhs=h2[:, kb * LH + fi * 512:kb * LH + (fi + 1) * 512],
                                start=(kb == 0), stop=(kb == NT - 1)), sT)
                fw_cons[m % 2] = val(sT)
                t_here = val(sT)

                @blk.scalar
                def _(scl, m=m, t_here=t_here, hbd=hb_dma[m % 2], ms_t=ms_t):
                    scl.wait_ge(sT, t_here)
                    scl.wait_ge(sDs[m % 2], hbd)
                    scl.wait_ge(sT, ms_t)
                    for fi in range(2):
                        inc(scl.activation(out=sqf[:, fi * 512:(fi + 1) * 512],
                                           in_=pb[fi + (m % 2) * 2][:, 0:512],
                                           func=AF.Relu), sS)
                    scl.wait_ge(sS, val(sS))
                    c = None
                    for fi in range(2):
                        c = inc(scl.activation(
                            out=hbb[m % 2][:, fi * 512:(fi + 1) * 512],
                            in_=sqf[:, fi * 512:(fi + 1) * 512],
                            func=AF.Square), sS)
                    for fi in range(2):
                        bank_guard[fi + (m % 2) * 2] = (sS, c)
                    scl.wait_ge(sS, val(sS))
                    inc(scl.activation(out=hq[m % 2][:], in_=hbb[m % 2][:],
                                       func=AF.Square), sS)
                s_here = val(sS)
                hq_s[m % 2] = s_here

                if m > 0:
                    @blk.tensor
                    def _(t, mprev=m - 1, sprev=hq_s[(m - 1) % 2]):
                        t.wait_ge(sS, sprev)
                        if mprev == 0:
                            bank_wait(t, 6)
                            bank_wait(t, 7)
                        for fi in range(2):
                            inc(t.matmul(pb[6 + fi][0:1, 0:512],
                                         lhsT=ones_cb[:],
                                         rhs=hq[mprev % 2][:, fi * 512:(fi + 1) * 512],
                                         start=(mprev == 0), stop=False), sT)
                    ms_t = val(sT)

                @blk.sync
                def _(sy, m=m, s_here=s_here):
                    sem = sDs[m % 2]
                    sy.wait_ge(sem, val(sem))
                    sy.wait_ge(sS, s_here)
                    inc(sy.dma_start(out=h_sp[m * 128:(m + 1) * 128, :],
                                     in_=hbb[m % 2][:]), sem, 16)
                hb_dma[m % 2] = val(sDs[m % 2])

            @blk.tensor
            def _(t, mprev=HID // 128 - 1, sprev=hq_s[(HID // 128 - 1) % 2]):
                t.wait_ge(sS, sprev)
                for fi in range(2):
                    inc(t.matmul(pb[6 + fi][0:1, 0:512],
                                 lhsT=ones_cb[:],
                                 rhs=hq[mprev % 2][:, fi * 512:(fi + 1) * 512],
                                 start=False, stop=True), sT)
            ms_t = val(sT)
            p7_fc_d = [(sDs[0], val(sDs[0])), (sDs[1], val(sDs[1]))]

            @blk.vector
            def _(v):
                v.wait_ge(sT, ms_t)
                for fi in range(2):
                    c = inc(v.tensor_copy(out=msr[0:1, fi * 512:(fi + 1) * 512],
                                          in_=pb[6 + fi][0:1, 0:512]), sV)
                    bank_guard[6 + fi] = (sV, c)
                v.wait_ge(sV, val(sV))
                inc(v.tensor_scalar(out=msr[:], in0=msr[:], scalar1=1.0 / HID,
                                    scalar2=RMS_EPS, op0=MUL, op1=ADD), sV)
            p7_ms_v = val(sV)

            @blk.scalar
            def _(scl):
                scl.wait_ge(sV, p7_ms_v)
                inc(scl.activation(out=msr[:], in_=msr[:], func=AF.Sqrt), sS)
            p7_s = val(sS)

            @blk.vector
            def _(v):
                v.wait_ge(sS, p7_s)
                inc(v.reciprocal(out=msr[:], in_=msr[:]), sV)
            p7_v = val(sV)

            @blk.tensor
            def _(t):
                t.wait_ge(sV, p7_v)
                for fi in range(2):
                    bank_wait(t, 6)
                    inc(t.matmul(pb[6][:, 0:512], lhsT=ones_rf[:],
                                 rhs=msr[0:1, fi * 512:(fi + 1) * 512],
                                 start=True, stop=True), sT)
                    t_here = val(sT)

                    @blk.vector
                    def _(v, fi=fi, t_here=t_here):
                        v.wait_ge(sT, t_here)
                        c = inc(v.tensor_copy(out=m_bc[:, fi * 512:(fi + 1) * 512],
                                              in_=pb[6][:, 0:512]), sV)
                        bank_guard[6] = (sV, c)
            p7_rbc_v = val(sV)

            hw = [sb(f"hw{j}", [128, LH], BF16, scope=sc5) for j in range(4)]
            pw = [sb(f"pw{j}", [128, 4 * 128], BF16, scope=sc5) for j in range(4)]
            ksem = [sDs[0], sDs[1], sDp[0], sDp[1]]

            hw_cons = [0, 0, 0, 0]
            for dg in range(4):
                mm_end = None
                for kb in range(HID // 128):
                    cshard, hb8 = kb // 8, kb % 8
                    sl = kb % 4

                    @blk.sync
                    def _(sy, dg=dg, kb=kb, cshard=cshard, hb8=hb8, sl=sl):
                        sem = ksem[sl]
                        sy.wait_ge(sem, val(sem))
                        sy.wait_ge(sT, hw_cons[sl])
                        if dg == 0 and kb < 4:
                            for s_, c_ in p7_fc_d:
                                sy.wait_ge(s_, c_)
                        inc(sy.dma_start(out=hw[sl][:],
                                         in_=h_sp[kb * 128:(kb + 1) * 128, :]),
                            sem, 16)
                        inc(sy.dma_start(
                            out=pw[sl].ap().rearrange("p (mm c) -> p mm c", c=128),
                            in_=wpjT_o.rearrange("(g p) c -> p g c", p=128)[
                                :, cshard * NT + dg * 4:cshard * NT + dg * 4 + 4,
                                hb8 * 128:(hb8 + 1) * 128]), sem, 16)
                    d_here = val(ksem[sl])

                    @blk.tensor
                    def _(t, dg=dg, kb=kb, sl=sl, d_here=d_here):
                        t.wait_ge(ksem[sl], d_here)
                        if kb == 0:
                            t.wait_ge(sV, p7_rbc_v)
                            for i in range(8):
                                bank_wait(t, i)
                        for mm in range(4):
                            for fi in range(2):
                                inc(t.matmul(
                                    pb[mm * 2 + fi][:, 0:512],
                                    lhsT=pw[sl][:, mm * 128:(mm + 1) * 128],
                                    rhs=hw[sl][:, fi * 512:(fi + 1) * 512],
                                    start=(kb == 0), stop=(kb == HID // 128 - 1)), sT)
                    mm_end = val(sT)
                    hw_cons[sl] = mm_end

                for mm in range(4):
                    ht = dg * 4 + mm

                    @blk.vector
                    def _(v, mm=mm, ht=ht, mm_end=mm_end, ventry=val(sV)):
                        v.wait_ge(sT, mm_end)
                        v.wait_ge(sV, ventry)
                        for fi in range(2):
                            c = inc(v.scalar_tensor_tensor(
                                out=tb2[:, fi * 512:(fi + 1) * 512],
                                in0=pb[mm * 2 + fi][:, 0:512],
                                scalar=gb_c[:, ht:ht + 1],
                                in1=m_bc[:, fi * 512:(fi + 1) * 512],
                                op0=MUL, op1=MUL), sV)
                            bank_guard[mm * 2 + fi] = (sV, c)
                        v.wait_ge(sV, val(sV))
                        inc(v.tensor_tensor(out=x2[:, ht * LH:(ht + 1) * LH],
                                            in0=tb2[:],
                                            in1=x2[:, ht * LH:(ht + 1) * LH],
                                            op=ADD), sV)
            p7_v2 = val(sV)

            out_dma = 0
            for lt in range(8):
                @blk.tensor
                def _(t, lt=lt, out_dma=out_dma):
                    t.wait_ge(sV, p7_v2)
                    t.wait_ge(sDo, out_dma)
                    for hb in range(NT):
                        bank_wait(t, hb % 2)
                        inc(t.transpose(pb[hb % 2][:, 0:128],
                                        in_=x2[:, hb * LH + lt * 128:
                                               hb * LH + (lt + 1) * 128],
                                        identity=ident_f[:]), sT)
                        t_here = val(sT)

                        @blk.vector
                        def _(v, hb=hb, t_here=t_here):
                            v.wait_ge(sT, t_here)
                            c = inc(v.tensor_copy(out=xs2[:, hb * 128:(hb + 1) * 128],
                                                  in_=pb[hb % 2][:, 0:128]), sV)
                            bank_guard[hb % 2] = (sV, c)
                v_here = val(sV)

                @blk.sync
                def _(sy, lt=lt, v_here=v_here):
                    sy.wait_ge(sDo, val(sDo))
                    sy.wait_ge(sV, v_here)
                    inc(sy.dma_start(out=out_my[lt * 128:(lt + 1) * 128, :],
                                     in_=xs2[:]), sDo, 16)
                out_dma = val(sDo)
            p8_d = val(sDo)

            @blk.sync
            def _(sy):
                sy.wait_ge(sDo, p8_d)

    return nc


_NC_CACHE = {}


def _get_nc():
    if "nc" not in _NC_CACHE:
        _NC_CACHE["nc"] = build_program()
    return _NC_CACHE["nc"]


def kernel(L0_attn_x, L0_attn_W_in_states, L0_attn_W_in_gates, L0_attn_conv_w,
           L0_attn_conv_b, L0_attn_W_dt, L0_attn_W_b, L0_attn_W_c,
           L0_attn_W_dtproj, L0_attn_b_dtproj, L0_attn_A_log, L0_attn_D,
           L0_attn_W_out, L0_ffn_W_fc, L0_ffn_W_proj, L0_c, L0_wmod):
    import ml_dtypes
    f = lambda a: np.ascontiguousarray(np.asarray(a, dtype=np.float32))
    X = f(L0_attn_x)
    Wis, Wig = f(L0_attn_W_in_states), f(L0_attn_W_in_gates)
    Cw, Cb = f(L0_attn_conv_w), f(L0_attn_conv_b)
    Wdt, Wb, Wc = f(L0_attn_W_dt), f(L0_attn_W_b), f(L0_attn_W_c)
    Wdp, Bdp = f(L0_attn_W_dtproj), f(L0_attn_b_dtproj)
    Alog, Dv = f(L0_attn_A_log), f(L0_attn_D)
    Wout, Wfc, Wpj = f(L0_attn_W_out), f(L0_ffn_W_fc), f(L0_ffn_W_proj)
    C0, Wmod = f(L0_c), f(L0_wmod)
    eyef = np.eye(128, dtype=np.float32)
    eyeb = np.eye(128, dtype=ml_dtypes.bfloat16)

    in_maps = []
    for c in range(NC):
        b, hf = c // 2, c % 2
        sd = slice(hf * DH, (hf + 1) * DH)
        in_maps.append({
            "x_b": X[b],
            "x_h": np.ascontiguousarray(X[b, hf * LH:(hf + 1) * LH]),
            "wis": np.ascontiguousarray(Wis[sd, :]),
            "wig": np.ascontiguousarray(Wig[sd, :]),
            "cw": np.ascontiguousarray(Cw[sd, :]),
            "cbia": np.ascontiguousarray(Cb[sd, None]),
            "wdt": np.ascontiguousarray(Wdt[:, sd]),
            "wbc": np.ascontiguousarray(
                np.concatenate([Wb[:, sd], Wc[:, sd]], axis=0)),
            "wdp": np.ascontiguousarray(Wdp[sd, :]),
            "bdp": np.ascontiguousarray(Bdp[sd, None]),
            "alog": np.ascontiguousarray(Alog[sd, :]),
            "dvec": np.ascontiguousarray(Dv[sd, None]),
            "wout_h": np.ascontiguousarray(Wout[:, sd]),
            "wfc_s": np.ascontiguousarray(
                Wfc[c * (HID // NC):(c + 1) * (HID // NC), :]),
            "wpj_s": np.ascontiguousarray(
                Wpj[:, c * (HID // NC):(c + 1) * (HID // NC)]),
            "wmod_s": np.ascontiguousarray(Wmod[:, hf * 6144:(hf + 1) * 6144]),
            "c0_my": np.ascontiguousarray(C0[b:b + 1, :]),
            "eyef": eyef,
            "eyeb": eyeb,
        })

    res = run_bass_kernel_spmd(_get_nc(), in_maps, core_ids=list(range(NC)))
    out = np.empty((B, L, H), dtype=np.float32)
    for c in range(NC):
        b, hf = c // 2, c % 2
        out[b, hf * LH:(hf + 1) * LH, :] = np.asarray(res.results[c]["out_my"])
    return out


# version 07
# speedup vs baseline: 1.0835x; 1.0835x over previous
import numpy as np
import concourse.bass as bass
import concourse.mybir as mybir
from concourse.alu_op_type import AluOpType
from concourse.bass_utils import run_bass_kernel_spmd
from contextlib import ExitStack

F32 = mybir.dt.float32
BF16 = mybir.dt.bfloat16
AF = mybir.ActivationFunctionType
AX = mybir.AxisListType

B, L, H, DI, N, R = 4, 2048, 2048, 4096, 16, 128
KC = 4
DH = DI // 2
HID = 8192
NC = 8
LH = L // 2
LC = 1024
NT = 16
RINT_C = 12582912.0
LN_EPS = 1e-5
RMS_EPS = 1.1920929e-07
PAIRS = [[0, 1], [2, 3], [4, 5], [6, 7]]
ALLG = [list(range(NC))]
MUL, ADD, SUB, MAXO, MINO, BYP = (AluOpType.mult, AluOpType.add, AluOpType.subtract,
                                  AluOpType.max, AluOpType.min, AluOpType.bypass)


def build_program():
    nc = bass.Bass("TRN2", target_bir_lowering=False, debug=False)

    def din(name, shape, dt=F32):
        return nc.dram_tensor(name, shape, dt, kind="ExternalInput").ap()

    def dtmp(name, shape, dt=F32, shared=False):
        if shared:
            return nc.dram_tensor(name, shape, dt, addr_space="Shared").ap()
        return nc.dram_tensor(name, shape, dt).ap()

    x_b = din("x_b", [L, H])
    x_h = din("x_h", [LH, H])
    wis = din("wis", [DH, H])
    wig = din("wig", [DH, H])
    cw = din("cw", [DH, KC])
    cbia = din("cbia", [DH, 1])
    wdt = din("wdt", [R, DH])
    wbc = din("wbc", [2 * N, DH])
    wdp = din("wdp", [DH, R])
    bdp = din("bdp", [DH, 1])
    alog = din("alog", [DH, N])
    dvec = din("dvec", [DH, 1])
    wout_h = din("wout_h", [H, DH])
    wfc_s = din("wfc_s", [HID // NC, H])
    wpj_s = din("wpj_s", [H, HID // NC])
    wmod_s = din("wmod_s", [H, 6144])
    c0_my = din("c0_my", [1, H])
    eyef = din("eyef", [128, 128])
    eyeb = din("eyeb", [128, 128], BF16)

    out_my = nc.dram_tensor("out_my", [LH, H], F32, kind="ExternalOutput").ap()

    hs_sp = dtmp("hs_sp", [DH, L])
    silu_sp = dtmp("silu_sp", [DH, L], BF16)
    dt_sp = dtmp("dt_sp", [DH, L])
    y3_sp = dtmp("y3_sp", [DH, L], BF16)
    h_sp = dtmp("h_sp", [HID, LH], BF16)
    wisT_sp = dtmp("wisT_sp", [DH, H], BF16)
    wigT_sp = dtmp("wigT_sp", [DH, H], BF16)
    woutT_sp = dtmp("woutT_sp", [H, DH], BF16)
    dtbc_i = dtmp("dtbc_i", [R + 2 * N, L])
    dtbc_o = dtmp("dtbc_o", [R + 2 * N, L])
    op_i = dtmp("op_i", [2 * H, LH])
    op_o = dtmp("op_o", [H, LH])
    mod_i = dtmp("mod_i", [1, 6144])
    mod_o = dtmp("mod_o", [2, 6144])
    wfcT_i = dtmp("wfcT_i", [HID // NC, H], BF16)
    wfcT_o = dtmp("wfcT_o", [HID, H], BF16, shared=True)
    wpjT_i = dtmp("wpjT_i", [H, HID // NC], BF16)
    wpjT_o = dtmp("wpjT_o", [NC * H, HID // NC], BF16, shared=True)

    cnt = {}

    def inc(instr, sem, n=1):
        instr.then_inc(sem, n)
        cnt[sem.num] = cnt.get(sem.num, 0) + n
        return cnt[sem.num]

    def val(sem):
        return cnt.get(sem.num, 0)

    st = ExitStack()
    E = st.enter_context
    with st:
        E(nc.allow_non_contiguous_dma(
            reason="small column-vector loads and interleaved row assembly"))
        sD = E(nc.semaphore("sD"))
        sDw = E(nc.semaphore("sDw"))
        sDx = E(nc.semaphore("sDx"))
        sDo = E(nc.semaphore("sDo"))
        sDs = [E(nc.semaphore("sDs0")), E(nc.semaphore("sDs1"))]
        sDp = [E(nc.semaphore("sDp0")), E(nc.semaphore("sDp1"))]
        sDq = [E(nc.semaphore("sDq0")), E(nc.semaphore("sDq1"))]
        sGP = E(nc.semaphore("sGP"))
        sV = E(nc.semaphore("sV"))
        sS = E(nc.semaphore("sS"))
        sT = E(nc.semaphore("sT"))
        sG = E(nc.semaphore("sG"))

        def sb(name, shape, dt=F32, scope=None):
            return (scope or st).enter_context(nc.sbuf_tensor(name, shape, dt))

        ident_f = sb("ident_f", [128, 128])
        ident_b = sb("ident_b", [128, 128], BF16)
        ones_cb = sb("ones_cb", [128, 1], BF16)
        ones_cf = sb("ones_cf", [128, 1])
        ones_rb = sb("ones_rb", [1, 128], BF16)
        ones_rf = sb("ones_rf", [1, 128])
        acol = sb("acol", [128, NT * N])
        dcol = sb("dcol", [128, NT])
        cbcol = sb("cbcol", [128, NT])
        bdcol = sb("bdcol", [128, NT])
        state = sb("state", [128, NT * N])
        ca1 = sb("ca1", [128, NT])
        sa_c = sb("sa_c", [128, NT])
        ga_c = sb("ga_c", [128, NT])
        cb1 = sb("cb1", [128, NT])
        sb_c = sb("sb_c", [128, NT])
        gb_c = sb("gb_c", [128, NT])
        wdtT = sb("wdtT", [128, NT * 128], BF16)
        wbcT = sb("wbcT", [128, NT * 32], BF16)
        wdpT = sb("wdpT", [128, NT * 128], BF16)
        diagw = sb("diagw", [128, NT * KC * 128], BF16)
        bcb = sb("bcb", [32, L], BF16)
        ts_b = sb("ts_b", [R, L], BF16)
        xs = sb("xs", [128, H])
        tbf2 = sb("tbf2", [128, 2 * H])
        colw = sb("colw", [128, 8])

        pb = [E(nc.psum_tensor(f"pb{i}", [128, 512], F32)) for i in range(8)]
        bank_guard = {i: None for i in range(8)}

        def bank_wait(t, i):
            if bank_guard[i] is not None:
                s, c = bank_guard[i]
                t.wait_ge(s, c)

        blk = E(nc.Block())

        def tp_blocks_bf16(src_sb, wcols, dst_sb, coloff=0, wait_v=None,
                           extra_waits=()):
            nb = wcols // 128

            @blk.tensor
            def _(t):
                if wait_v is not None:
                    t.wait_ge(sV, wait_v)
                for g0 in range(0, nb, 4):
                    gn = min(4, nb - g0)
                    bk = 6 + (g0 // 4) % 2
                    bank_wait(t, bk)
                    for j in range(gn):
                        cb = g0 + j
                        inc(t.transpose(
                            pb[bk][:, j * 64:(j + 1) * 64].bitcast(BF16),
                            in_=src_sb[:, cb * 128:(cb + 1) * 128],
                            identity=ident_b[:]), sT)
                    t_here = val(sT)

                    @blk.vector
                    def _(v, g0=g0, gn=gn, bk=bk, t_here=t_here):
                        v.wait_ge(sT, t_here)
                        for s_, c_ in extra_waits:
                            v.wait_ge(s_, c_)
                        c = inc(v.tensor_copy(
                            out=dst_sb[:, coloff + g0 * 128:
                                       coloff + (g0 + gn) * 128],
                            in_=pb[bk][:, 0:gn * 64].bitcast(BF16)), sV)
                        bank_guard[bk] = (sV, c)
            return val(sV)

        @blk.sync
        def _(sy):
            sy.wait_ge(sDw, val(sDw))
            inc(sy.dma_start(out=ident_f[:], in_=eyef), sDw, 16)
            inc(sy.dma_start(out=ident_b[:], in_=eyeb), sDw, 16)
            inc(sy.dma_start(out=acol.ap().rearrange("p (t n) -> p t n", n=N),
                             in_=alog.rearrange("(t p) n -> p t n", p=128)), sDw, 16)
            inc(sy.dma_start(out=dcol.ap(),
                             in_=dvec.rearrange("(t p) o -> p (t o)", p=128)), sDw, 16)
            inc(sy.dma_start(out=cbcol.ap(),
                             in_=cbia.rearrange("(t p) o -> p (t o)", p=128)), sDw, 16)
            inc(sy.dma_start(out=bdcol.ap(),
                             in_=bdp.rearrange("(t p) o -> p (t o)", p=128)), sDw, 16)
        p0a_d = val(sDw)

        @blk.vector
        def _(v):
            inc(v.memset(ones_cb[:], 1.0), sV)
            inc(v.memset(ones_cf[:], 1.0), sV)
            inc(v.memset(ones_rb[:], 1.0), sV)
            inc(v.memset(ones_rf[:], 1.0), sV)
        p0_ones_v = val(sV)

        @blk.scalar
        def _(sc):
            sc.wait_ge(sDw, p0a_d)
            inc(sc.activation(out=acol[:], in_=acol[:], func=AF.Exp), sS)
        p0_aexp_s = val(sS)

        @blk.vector
        def _(v):
            v.wait_ge(sS, p0_aexp_s)
            inc(v.tensor_scalar(out=acol[:], in0=acol[:], scalar1=-1.0, scalar2=None,
                                op0=MUL, op1=BYP), sV)
        p0_acol_v = val(sV)

        with ExitStack() as sc0:
            c0s = sb("c0s", [1, H], scope=sc0)
            silT = sb("silT", [128, NT], BF16, scope=sc0)
            wmstg = sb("wmstg", [128, 3072], scope=sc0)
            wmb = sb("wmb", [128, 3072], BF16, scope=sc0)
            mvrow = sb("mvrow", [1, 3072], scope=sc0)
            modr = sb("modr", [1, 6 * H], scope=sc0)
            modc = sb("modc", [128, 96], scope=sc0)
            tmpc = sb("tmpc", [128, 2 * NT], scope=sc0)

            @blk.sync
            def _(sy):
                sy.wait_ge(sDx, val(sDx))
                inc(sy.dma_start(out=c0s[:], in_=c0_my), sDx, 16)
            d0 = val(sDx)

            @blk.scalar
            def _(scl):
                scl.wait_ge(sDx, d0)
                inc(scl.activation(out=c0s[:], in_=c0s[:], func=AF.Silu), sS)
            s0 = val(sS)

            @blk.tensor
            def _(t):
                t.wait_ge(sS, s0)
                t.wait_ge(sDw, p0a_d)
                for j in range(NT):
                    bank_wait(t, 6)
                    inc(t.transpose(pb[6][:, j:j + 1],
                                    in_=c0s[0:1, j * 128:(j + 1) * 128],
                                    identity=ident_f[0:1, 0:1]), sT)
            t0 = val(sT)

            @blk.vector
            def _(v):
                v.wait_ge(sT, t0)
                c = inc(v.tensor_copy(out=silT[:], in_=pb[6][:, 0:NT]), sV)
                bank_guard[6] = (sV, c)
            v_silT = val(sV)

            mvrow_dma = 0
            for piece in range(2):
                mm_last = None
                for kt in range(NT):
                    @blk.sync
                    def _(sy, piece=piece, kt=kt):
                        sy.wait_ge(sDw, val(sDw))
                        sy.wait_ge(sV, val(sV))
                        inc(sy.dma_start(
                            out=wmstg[:],
                            in_=wmod_s[kt * 128:(kt + 1) * 128,
                                       piece * 3072:(piece + 1) * 3072]), sDw, 16)
                    d1 = val(sDw)

                    @blk.vector
                    def _(v, d1=d1, mm_prev=mm_last):
                        v.wait_ge(sDw, d1)
                        v.wait_ge(sV, v_silT)
                        if mm_prev is not None:
                            v.wait_ge(sT, mm_prev)
                        inc(v.tensor_copy(out=wmb[:], in_=wmstg[:]), sV)
                    v1 = val(sV)

                    @blk.tensor
                    def _(t, kt=kt, v1=v1):
                        t.wait_ge(sV, v1)
                        for fb in range(6):
                            if kt == 0:
                                bank_wait(t, fb)
                            inc(t.matmul(pb[fb][0:1, 0:512],
                                         lhsT=silT[:, kt:kt + 1],
                                         rhs=wmb[:, fb * 512:(fb + 1) * 512],
                                         start=(kt == 0), stop=(kt == NT - 1)), sT)
                    mm_last = val(sT)

                @blk.vector
                def _(v, mm_last=mm_last, mvrow_dma=mvrow_dma):
                    v.wait_ge(sT, mm_last)
                    v.wait_ge(sDo, mvrow_dma)
                    for fb in range(6):
                        c = inc(v.tensor_copy(out=mvrow[0:1, fb * 512:(fb + 1) * 512],
                                              in_=pb[fb][0:1, 0:512]), sV)
                        bank_guard[fb] = (sV, c)
                v2 = val(sV)

                @blk.sync
                def _(sy, piece=piece, v2=v2):
                    sy.wait_ge(sDo, val(sDo))
                    sy.wait_ge(sV, v2)
                    inc(sy.dma_start(out=mod_i[0:1, piece * 3072:(piece + 1) * 3072],
                                     in_=mvrow[:]), sDo, 16)
                mvrow_dma = val(sDo)
            p0b_d = val(sDo)

            @blk.gpsimd
            def _(g):
                g.wait_ge(sDo, p0b_d)
                inc(g.collective_compute("AllGather", BYP, replica_groups=PAIRS,
                                         ins=[mod_i.opt()], outs=[mod_o.opt()]), sG)
            g0 = val(sG)

            @blk.sync
            def _(sy):
                sy.wait_ge(sDx, val(sDx))
                sy.wait_ge(sG, g0)
                inc(sy.dma_start(out=modr[0:1, 0:6144], in_=mod_o[0:1, :]), sDx, 16)
                inc(sy.dma_start(out=modr[0:1, 6144:12288], in_=mod_o[1:2, :]), sDx, 16)
            d2 = val(sDx)

            @blk.tensor
            def _(t):
                t.wait_ge(sDx, d2)
                for j in range(96):
                    bank_wait(t, 6)
                    inc(t.transpose(pb[6][:, j:j + 1],
                                    in_=modr[0:1, j * 128:(j + 1) * 128],
                                    identity=ident_f[0:1, 0:1]), sT)
            t1 = val(sT)

            @blk.vector
            def _(v):
                v.wait_ge(sT, t1)
                c = inc(v.tensor_copy(out=modc[:], in_=pb[6][:, 0:96]), sV)
                bank_guard[6] = (sV, c)
                v.wait_ge(sV, c)
                inc(v.tensor_copy(out=sa_c[:], in_=modc[:, 0:NT]), sV)
                inc(v.tensor_scalar(out=ca1[:], in0=modc[:, NT:2 * NT], scalar1=1.0,
                                    scalar2=None, op0=ADD, op1=BYP), sV)
                inc(v.tensor_copy(out=sb_c[:], in_=modc[:, 3 * NT:4 * NT]), sV)
                inc(v.tensor_scalar(out=cb1[:], in0=modc[:, 4 * NT:5 * NT], scalar1=1.0,
                                    scalar2=None, op0=ADD, op1=BYP), sV)
                for src_o, dst in ((2 * NT, ga_c), (5 * NT, gb_c)):
                    g0ap = modc[:, src_o:src_o + NT]
                    v.wait_ge(sV, val(sV))
                    inc(v.tensor_scalar(out=tmpc[:, 0:NT], in0=g0ap, scalar1=0.0,
                                        scalar2=2.0, op0=AluOpType.is_ge, op1=MUL), sV)
                    v.wait_ge(sV, val(sV))
                    inc(v.tensor_scalar(out=tmpc[:, 0:NT], in0=tmpc[:, 0:NT],
                                        scalar1=1.0, scalar2=None, op0=SUB, op1=BYP),
                        sV)
                    v.wait_ge(sV, val(sV))
                    inc(v.tensor_tensor(out=tmpc[:, NT:2 * NT], in0=g0ap,
                                        in1=tmpc[:, 0:NT], op=MUL), sV)
                    v.wait_ge(sV, val(sV))
                    inc(v.tensor_scalar(out=tmpc[:, NT:2 * NT], in0=tmpc[:, NT:2 * NT],
                                        scalar1=0.25, scalar2=None, op0=MAXO, op1=BYP),
                        sV)
                    v.wait_ge(sV, val(sV))
                    inc(v.tensor_tensor(out=dst[:], in0=tmpc[:, NT:2 * NT],
                                        in1=tmpc[:, 0:NT], op=MUL), sV)
            p0b_mod_v = val(sV)

        with ExitStack() as sc1:
            qstg = sb("qstg", [128, H], scope=sc1)
            qwb = sb("qwb", [128, H], BF16, scope=sc1)
            qsum = sb("qsum", [128, 32], scope=sc1)
            qrec = sb("qrec", [128, 32], scope=sc1)
            qscb = sb("qscb", [128, 32], BF16, scope=sc1)
            qrt = sb("qrt", [128, H], BF16, scope=sc1)
            qq = sb("qq", [128, H], scope=sc1)
            qqb = sb("qqb", [128, H], BF16, scope=sc1)
            qwide = sb("qwide", [128, H], BF16, scope=sc1)

            def quant_tile(ncols, d_wait):
                ng = ncols // 64

                @blk.vector
                def _(v):
                    v.wait_ge(sDw, d_wait)
                    inc(v.tensor_copy(out=qwb[:, 0:ncols], in_=qstg[:, 0:ncols]), sV)
                    v.wait_ge(sV, val(sV))
                    inc(v.tensor_reduce(out=qsum[:, 0:ng],
                                        in_=qwb.ap()[:, 0:ncols].rearrange(
                                            "p (g k) -> p g k", k=64),
                                        axis=AX.X, op=ADD, apply_absolute_value=True),
                        sV)
                    v.wait_ge(sV, val(sV))
                    inc(v.tensor_scalar(out=qscb[:, 0:ng], in0=qsum[:, 0:ng],
                                        scalar1=1.0 / 64, scalar2=1e-8,
                                        op0=MUL, op1=MAXO), sV)
                    v.wait_ge(sV, val(sV))
                    inc(v.reciprocal(out=qrec[:, 0:ng], in_=qscb[:, 0:ng]), sV)
                    v.wait_ge(sV, val(sV))
                    inc(v.tensor_tensor(
                        out=qrt.ap()[:, 0:ncols].rearrange("p (g k) -> p g k", k=64),
                        in0=qwb.ap()[:, 0:ncols].rearrange("p (g k) -> p g k", k=64),
                        in1=qrec.ap()[:, 0:ng].unsqueeze(2).broadcast_to([128, ng, 64]),
                        op=MUL), sV)
                    v.wait_ge(sV, val(sV))
                    inc(v.tensor_scalar(out=qq[:, 0:ncols], in0=qrt[:, 0:ncols],
                                        scalar1=RINT_C, scalar2=RINT_C,
                                        op0=ADD, op1=SUB), sV)
                    v.wait_ge(sV, val(sV))
                    inc(v.tensor_scalar(out=qq[:, 0:ncols], in0=qq[:, 0:ncols],
                                        scalar1=1.0, scalar2=-1.0,
                                        op0=MINO, op1=MAXO), sV)
                    v.wait_ge(sV, val(sV))
                    inc(v.tensor_tensor(
                        out=qqb.ap()[:, 0:ncols].rearrange("p (g k) -> p g k", k=64),
                        in0=qq.ap()[:, 0:ncols].rearrange("p (g k) -> p g k", k=64),
                        in1=qscb.ap()[:, 0:ng].unsqueeze(2).broadcast_to([128, ng, 64]),
                        op=MUL), sV)
                return val(sV)

            qdma = 0
            for rt in range(8):
                @blk.sync
                def _(sy, rt=rt):
                    sy.wait_ge(sDw, val(sDw))
                    sy.wait_ge(sV, val(sV))
                    if rt == 0:
                        sy.wait_ge(sV, p0b_mod_v)
                    inc(sy.dma_start(out=qstg[:],
                                     in_=wfc_s[rt * 128:(rt + 1) * 128, :]), sDw, 16)
                vq = quant_tile(H, val(sDw))
                v2 = tp_blocks_bf16(qqb, H, qwide, wait_v=vq,
                                    extra_waits=[(sDo, qdma)])

                @blk.sync
                def _(sy, rt=rt, v2=v2):
                    sy.wait_ge(sDo, val(sDo))
                    sy.wait_ge(sV, v2)
                    inc(sy.dma_start(out=wfcT_i[rt * 128:(rt + 1) * 128, :],
                                     in_=qwide[:]), sDo, 16)
                qdma = val(sDo)

            for rt in range(NT):
                @blk.sync
                def _(sy, rt=rt):
                    sy.wait_ge(sDw, val(sDw))
                    sy.wait_ge(sV, val(sV))
                    inc(sy.dma_start(out=qstg[:, 0:1024],
                                     in_=wpj_s[rt * 128:(rt + 1) * 128, :]), sDw, 16)
                vq = quant_tile(1024, val(sDw))
                v2 = tp_blocks_bf16(qqb, 1024, qwide, wait_v=vq,
                                    extra_waits=[(sDo, qdma)])

                @blk.sync
                def _(sy, rt=rt, v2=v2):
                    sy.wait_ge(sDo, val(sDo))
                    sy.wait_ge(sV, v2)
                    inc(sy.dma_start(out=wpjT_i[rt * 128:(rt + 1) * 128, :],
                                     in_=qwide[:, 0:1024]), sDo, 16)
                qdma = val(sDo)
            p0c_d = val(sDo)

            @blk.gpsimd
            def _(g):
                g.wait_ge(sDo, p0c_d)
                inc(g.collective_compute("AllGather", BYP, replica_groups=ALLG,
                                         ins=[wfcT_i.opt()], outs=[wfcT_o.opt()]), sG)
                inc(g.collective_compute("AllGather", BYP, replica_groups=ALLG,
                                         ins=[wpjT_i.opt()], outs=[wpjT_o.opt()]), sG)
            p0c_g = val(sG)

        with ExitStack() as sc2:
            tstg = sb("tstg", [128, H], scope=sc2)
            tstg1 = sb("tstg1", [128, H], scope=sc2)
            tb16 = sb("tb16", [128, H], BF16, scope=sc2)
            tb161 = sb("tb161", [128, H], BF16, scope=sc2)
            twide = sb("twide", [128, H], BF16, scope=sc2)
            twide1 = sb("twide1", [128, H], BF16, scope=sc2)
            tstgs, tb16s, twides = [tstg, tstg1], [tb16, tb161], [twide, twide1]
            cts_i = [0]
            cast_cons = [0, 0]
            cast_done = [0, 0]

            sp_dma = [0, 0]
            p0c_entry = (val(sV), val(sDo))

            def cts_strips(src, nrt, cols, dst_sp):
                for rt in range(nrt):
                    p = cts_i[0] % 2
                    cts_i[0] += 1

                    @blk.sync
                    def _(sy, rt=rt, p=p, cc=cast_cons[p], cd=cast_done[p]):
                        sem = sDs[p]
                        sy.wait_ge(sem, val(sem))
                        sy.wait_ge(sV, cd)
                        sy.wait_ge(sT, cc)
                        sy.wait_ge(sDo, p0c_entry[1])
                        inc(sy.dma_start(out=tstgs[p][:, 0:cols],
                                         in_=src[rt * 128:(rt + 1) * 128, :]),
                            sem, 16)
                    d_here = val(sDs[p])

                    @blk.vector
                    def _(v, d_here=d_here, p=p):
                        v.wait_ge(sDs[p], d_here)
                        inc(v.tensor_copy(out=tb16s[p][:, 0:cols],
                                          in_=tstgs[p][:, 0:cols]), sV)
                    v1 = val(sV)
                    cast_done[p] = v1
                    v2 = tp_blocks_bf16(tb16s[p], cols, twides[p], wait_v=v1,
                                        extra_waits=[(sDo, sp_dma[p])])
                    cast_cons[p] = val(sT)

                    @blk.sync
                    def _(sy, rt=rt, v2=v2, p=p):
                        sy.wait_ge(sDo, val(sDo))
                        sy.wait_ge(sV, v2)
                        inc(sy.dma_start(out=dst_sp[rt * 128:(rt + 1) * 128, 0:cols],
                                         in_=twide1[:, 0:cols] if p else twide[:, 0:cols]),
                            sDo, 16)
                    sp_dma[p] = val(sDo)

            cts_strips(wis, NT, H, wisT_sp)
            cts_strips(wig, NT, H, wigT_sp)
            cts_strips(wout_h, NT, DH, woutT_sp)

            @blk.sync
            def _(sy):
                sy.wait_ge(sDw, val(sDw))
                sy.wait_ge(sV, val(sV))
                inc(sy.dma_start(out=tstg[:, 0:DH], in_=wdt), sDw, 16)
            d1 = val(sDw)

            @blk.vector
            def _(v, d1=d1):
                v.wait_ge(sDw, d1)
                inc(v.tensor_copy(out=tb16[:, 0:DH], in_=tstg[:, 0:DH]), sV)
            v1 = val(sV)
            tp_blocks_bf16(tb16, DH, wdtT, wait_v=v1)

            @blk.sync
            def _(sy):
                sy.wait_ge(sDw, val(sDw))
                sy.wait_ge(sV, val(sV))
                inc(sy.dma_start(out=tstg[0:2 * N, 0:DH], in_=wbc), sDw, 16)
            d2 = val(sDw)

            @blk.vector
            def _(v, d2=d2):
                v.wait_ge(sDw, d2)
                inc(v.tensor_copy(out=tb16[0:2 * N, 0:DH], in_=tstg[0:2 * N, 0:DH]),
                    sV)
            v2 = val(sV)

            @blk.tensor
            def _(t, v2=v2):
                t.wait_ge(sV, v2)
                for cb in range(NT):
                    bank_wait(t, 6 + cb % 2)
                    inc(t.transpose(pb[6 + cb % 2][:, 0:16].bitcast(BF16),
                                    in_=tb16[0:2 * N, cb * 128:(cb + 1) * 128],
                                    identity=ident_b[0:2 * N, 0:2 * N]), sT)
                    t_here = val(sT)

                    @blk.vector
                    def _(v, cb=cb, t_here=t_here):
                        v.wait_ge(sT, t_here)
                        c = inc(v.tensor_copy(
                            out=wbcT[:, cb * 32:(cb + 1) * 32],
                            in_=pb[6 + cb % 2][:, 0:16].bitcast(BF16)), sV)
                        bank_guard[6 + cb % 2] = (sV, c)

            @blk.sync
            def _(sy):
                sy.wait_ge(sDw, val(sDw))
                sy.wait_ge(sV, val(sV))
                inc(sy.dma_start(
                    out=tstg.ap()[:, 0:DH].rearrange("p (t r) -> p t r", r=128),
                    in_=wdp.rearrange("(t p) r -> p t r", p=128)), sDw, 16)
            d3 = val(sDw)

            @blk.vector
            def _(v, d3=d3):
                v.wait_ge(sDw, d3)
                inc(v.tensor_copy(out=tb16[:, 0:DH], in_=tstg[:, 0:DH]), sV)
            v3 = val(sV)
            tp_blocks_bf16(tb16, DH, wdpT, wait_v=v3)

            @blk.sync
            def _(sy):
                sy.wait_ge(sDw, val(sDw))
                sy.wait_ge(sV, val(sV))
                inc(sy.dma_start(
                    out=tstg.ap()[:, 0:NT * KC].rearrange("p (t k) -> p t k", k=KC),
                    in_=cw.rearrange("(t p) k -> p t k", p=128)), sDw, 16)
            d4 = val(sDw)

            @blk.vector
            def _(v, d4=d4):
                v.wait_ge(sDw, d4)
                for tt in range(NT):
                    for j in range(KC):
                        inc(v.tensor_scalar(
                            out=diagw[:, (tt * KC + j) * 128:(tt * KC + j + 1) * 128],
                            in0=ident_b[:],
                            scalar1=tstg[:, tt * KC + j:tt * KC + j + 1],
                            scalar2=None, op0=MUL, op1=BYP), sV)
            p0_diag_v = val(sV)

        with ExitStack() as sc3:
            h1 = sb("h1", [128, NT * L], BF16, scope=sc3)

            p0d_spill = val(sDo)
            p0d_tp = val(sT)
            prev_tp_t = 0
            prev_apply_v = 0
            for lt in range(NT):
                half = lt % 2

                @blk.sync
                def _(sy, lt=lt, prev_apply_v=prev_apply_v):
                    sy.wait_ge(sDx, val(sDx))
                    if lt > 0:
                        sy.wait_ge(sV, prev_apply_v)
                    inc(sy.dma_start(out=xs[:], in_=x_b[lt * 128:(lt + 1) * 128, :]),
                        sDx, 16)
                d_here = val(sDx)

                @blk.scalar
                def _(scl, d_here=d_here, half=half, prev_tp_t=prev_tp_t,
                      prev_apply_v=prev_apply_v):
                    scl.wait_ge(sDx, d_here)
                    scl.wait_ge(sT, prev_tp_t)
                    scl.wait_ge(sV, prev_apply_v)
                    inc(scl.activation(out=tbf2[:, half * H:(half + 1) * H],
                                       in_=xs[:], func=AF.Square,
                                       accum_out=colw[:, 0:1]), sS)
                    scl.wait_ge(sS, val(sS))
                    inc(scl.activation(out=tbf2[:, half * H:(half + 1) * H],
                                       in_=xs[:], func=AF.Copy,
                                       accum_out=colw[:, 1:2]), sS)
                s_here = val(sS)

                @blk.vector
                def _(v, s_here=s_here):
                    v.wait_ge(sS, s_here)
                    inc(v.tensor_scalar(out=colw[:, 2:3], in0=colw[:, 1:2],
                                        scalar1=1.0 / H, scalar2=None, op0=MUL,
                                        op1=BYP), sV)
                    v.wait_ge(sV, val(sV))
                    inc(v.tensor_tensor(out=colw[:, 3:4], in0=colw[:, 2:3],
                                        in1=colw[:, 2:3], op=MUL), sV)
                    v.wait_ge(sV, val(sV))
                    inc(v.scalar_tensor_tensor(out=colw[:, 4:5], in0=colw[:, 0:1],
                                               scalar=1.0 / H, in1=colw[:, 3:4],
                                               op0=MUL, op1=SUB), sV)
                    v.wait_ge(sV, val(sV))
                    inc(v.tensor_scalar(out=colw[:, 4:5], in0=colw[:, 4:5],
                                        scalar1=LN_EPS, scalar2=None,
                                        op0=ADD, op1=BYP), sV)
                v_here = val(sV)

                @blk.scalar
                def _(scl, v_here=v_here):
                    scl.wait_ge(sV, v_here)
                    inc(scl.activation(out=colw[:, 5:6], in_=colw[:, 4:5],
                                       func=AF.Sqrt), sS)
                s2 = val(sS)

                @blk.vector
                def _(v, s2=s2, half=half):
                    v.wait_ge(sS, s2)
                    inc(v.reciprocal(out=colw[:, 6:7], in_=colw[:, 5:6]), sV)
                    v.wait_ge(sV, val(sV))
                    inc(v.tensor_scalar(out=tbf2[:, half * H:(half + 1) * H],
                                        in0=xs[:], scalar1=colw[:, 2:3],
                                        scalar2=colw[:, 6:7], op0=SUB, op1=MUL), sV)
                prev_apply_v = val(sV)

                if half == 1:
                    @blk.tensor
                    def _(t, lt=lt, prev_apply_v=prev_apply_v):
                        t.wait_ge(sV, prev_apply_v)
                        for hb in range(NT):
                            bk = hb % 8
                            bank_wait(t, bk)
                            for j in range(2):
                                inc(t.transpose(
                                    pb[bk][:, j * 128:(j + 1) * 128],
                                    in_=tbf2[:, j * H + hb * 128:
                                             j * H + (hb + 1) * 128],
                                    identity=ident_f[:]), sT)
                            t_here = val(sT)

                            @blk.scalar
                            def _(scl, lt=lt, hb=hb, bk=bk, t_here=t_here):
                                scl.wait_ge(sT, t_here)
                                if lt == 1 and hb == 0:
                                    scl.wait_ge(sV, p0b_mod_v)
                                    scl.wait_ge(sDo, p0d_spill)
                                c = inc(scl.activation(
                                    out=h1[:, hb * L + (lt - 1) * 128:
                                           hb * L + (lt + 1) * 128],
                                    in_=pb[bk][:, 0:256], func=AF.Identity,
                                    scale=ca1[:, hb:hb + 1],
                                    bias=sa_c[:, hb:hb + 1]), sS)
                                bank_guard[bk] = (sS, c)
                    prev_tp_t = val(sT)
            p1_s = val(sS)

            wstg = [sb("wstg0", [128, H], BF16, scope=sc3),
                    sb("wstg1", [128, H], BF16, scope=sc3)]
            gstg = [sb("gstg0", [128, H], BF16, scope=sc3),
                    sb("gstg1", [128, H], BF16, scope=sc3)]
            hpre = sb("hpre", [128, 3 + L], BF16, scope=sc3)
            hst = [sb("hst0", [128, L], BF16, scope=sc3),
                   sb("hst1", [128, L], BF16, scope=sc3)]
            hsf = [sb("hsf0", [128, 512], scope=sc3),
                   sb("hsf1", [128, 512], scope=sc3)]
            sgbb = [sb("sgbb0", [128, 512], BF16, scope=sc3),
                    sb("sgbb1", [128, 512], BF16, scope=sc3)]
            tsw = sb("tsw", [128, L], scope=sc3)
            bcw = sb("bcw", [32, L], scope=sc3)

            @blk.vector
            def _(v):
                v.wait_ge(sV, p0_diag_v)
                v.wait_ge(sDo, p0d_spill)
                inc(v.memset(hpre[:, 0:3], 0.0), sV)
                inc(v.memset(tsw[:], 0.0), sV)
                inc(v.memset(bcw[:], 0.0), sV)
            p2_pad_v = val(sV)

            w_cons = [0, 0]
            prev_conv_t = 0
            hs_dma = [0, 0]
            sil_dma = [0, 0]
            hst_cons = [0, 0]
            tsbc_add_v = 0
            for dt_i in range(NT):
                @blk.sync
                def _(sy, dt_i=dt_i):
                    sy.wait_ge(sDw, val(sDw))
                    sy.wait_ge(sT, w_cons[dt_i % 2])
                    if dt_i == 0:
                        sy.wait_ge(sDo, p0d_spill)
                    inc(sy.dma_start(out=wstg[dt_i % 2][:],
                                     in_=wisT_sp[dt_i * 128:(dt_i + 1) * 128, :]),
                        sDw, 16)
                    inc(sy.dma_start(out=gstg[dt_i % 2][:],
                                     in_=wigT_sp[dt_i * 128:(dt_i + 1) * 128, :]),
                        sDw, 16)
                d_here = val(sDw)
                w0, w1 = wstg[dt_i % 2], gstg[dt_i % 2]

                for fi in range(4):
                    @blk.tensor
                    def _(t, dt_i=dt_i, fi=fi, d_here=d_here):
                        t.wait_ge(sDw, d_here)
                        if dt_i == 0 and fi == 0:
                            t.wait_ge(sS, p1_s)
                        bank_wait(t, fi % 2)
                        for kb in range(NT):
                            inc(t.matmul(
                                pb[fi % 2][:, 0:512],
                                lhsT=w0[:, kb * 128:(kb + 1) * 128],
                                rhs=h1[:, kb * L + fi * 512:kb * L + (fi + 1) * 512],
                                start=(kb == 0), stop=(kb == NT - 1)), sT)
                    t_here = val(sT)

                    @blk.scalar
                    def _(scl, dt_i=dt_i, fi=fi, t_here=t_here,
                          prev_conv_t=prev_conv_t):
                        scl.wait_ge(sT, t_here)
                        scl.wait_ge(sT, prev_conv_t)
                        if dt_i == 0 and fi == 0:
                            scl.wait_ge(sV, p2_pad_v)
                            scl.wait_ge(sDo, p0d_spill)
                        c = inc(scl.activation(
                            out=hpre[:, 3 + fi * 512:3 + (fi + 1) * 512],
                            in_=pb[fi % 2][:, 0:512], func=AF.Copy), sS)
                        bank_guard[fi % 2] = (sS, c)
                p2_hpre_s = val(sS)

                for fi in range(4):
                    @blk.tensor
                    def _(t, dt_i=dt_i, fi=fi, p2_hpre_s=p2_hpre_s):
                        t.wait_ge(sS, p2_hpre_s)
                        if dt_i == 0 and fi == 0:
                            t.wait_ge(sV, p2_pad_v)
                        bank_wait(t, 2 + fi % 2)
                        for j in range(KC):
                            inc(t.matmul(
                                pb[2 + fi % 2][:, 0:512],
                                lhsT=diagw[:, (dt_i * KC + j) * 128:
                                           (dt_i * KC + j + 1) * 128],
                                rhs=hpre[:, fi * 512 + j:fi * 512 + j + 512],
                                start=(j == 0), stop=(j == KC - 1)), sT)
                    t_here = val(sT)
                    if fi == 3:
                        prev_conv_t = t_here

                    @blk.scalar
                    def _(scl, dt_i=dt_i, fi=fi, t_here=t_here, hsd=hs_dma[fi % 2],
                          hc=hst_cons[dt_i % 2]):
                        scl.wait_ge(sT, t_here)
                        scl.wait_ge(sDp[fi % 2], hsd)
                        scl.wait_ge(sT, hc)
                        inc(scl.activation(out=hsf[fi % 2][:, 0:512],
                                           in_=pb[2 + fi % 2][:, 0:512],
                                           func=AF.Silu, bias=cbcol[:, dt_i:dt_i + 1]),
                            sS)
                        scl.wait_ge(sS, val(sS))
                        c = inc(scl.activation(
                            out=hst[dt_i % 2][:, fi * 512:(fi + 1) * 512],
                            in_=pb[2 + fi % 2][:, 0:512],
                            func=AF.Silu, bias=cbcol[:, dt_i:dt_i + 1]), sS)
                        bank_guard[2 + fi % 2] = (sS, c)
                    s_here = val(sS)

                    @blk.sync
                    def _(sy, dt_i=dt_i, fi=fi, s_here=s_here):
                        sem = sDp[fi % 2]
                        sy.wait_ge(sem, val(sem))
                        sy.wait_ge(sS, s_here)
                        inc(sy.dma_start(
                            out=hs_sp[dt_i * 128:(dt_i + 1) * 128,
                                      fi * 512:(fi + 1) * 512],
                            in_=hsf[fi % 2][:, 0:512]), sem, 16)
                    hs_dma[fi % 2] = val(sDp[fi % 2])
                p2_hst_s = val(sS)

                for fi in range(4):
                    @blk.tensor
                    def _(t, dt_i=dt_i, fi=fi, d_here=d_here):
                        t.wait_ge(sDw, d_here)
                        bank_wait(t, 4 + fi % 2)
                        for kb in range(NT):
                            inc(t.matmul(
                                pb[4 + fi % 2][:, 0:512],
                                lhsT=w1[:, kb * 128:(kb + 1) * 128],
                                rhs=h1[:, kb * L + fi * 512:kb * L + (fi + 1) * 512],
                                start=(kb == 0), stop=(kb == NT - 1)), sT)
                    t_here = val(sT)
                    if fi == 3:
                        w_cons[dt_i % 2] = t_here

                    @blk.scalar
                    def _(scl, fi=fi, t_here=t_here, sid=sil_dma[fi % 2]):
                        scl.wait_ge(sT, t_here)
                        scl.wait_ge(sDq[fi % 2], sid)
                        c = inc(scl.activation(out=sgbb[fi % 2][:, 0:512],
                                               in_=pb[4 + fi % 2][:, 0:512],
                                               func=AF.Silu), sS)
                        bank_guard[4 + fi % 2] = (sS, c)
                    s_here = val(sS)

                    @blk.sync
                    def _(sy, dt_i=dt_i, fi=fi, s_here=s_here):
                        sem = sDq[fi % 2]
                        sy.wait_ge(sem, val(sem))
                        sy.wait_ge(sS, s_here)
                        inc(sy.dma_start(
                            out=silu_sp[dt_i * 128:(dt_i + 1) * 128,
                                        fi * 512:(fi + 1) * 512],
                            in_=sgbb[fi % 2][:, 0:512]), sem, 16)
                    sil_dma[fi % 2] = val(sDq[fi % 2])

                for fi in range(4):
                    @blk.tensor
                    def _(t, dt_i=dt_i, fi=fi, p2_hst_s=p2_hst_s):
                        t.wait_ge(sS, p2_hst_s)
                        bank_wait(t, 6 + fi % 2)
                        inc(t.matmul(pb[6 + fi % 2][:, 0:512],
                                     lhsT=wdtT[:, dt_i * 128:(dt_i + 1) * 128],
                                     rhs=hst[dt_i % 2][:, fi * 512:(fi + 1) * 512],
                                     start=True, stop=True), sT)
                    t_here = val(sT)

                    @blk.vector
                    def _(v, fi=fi, t_here=t_here):
                        v.wait_ge(sT, t_here)
                        c = inc(v.tensor_tensor(out=tsw[:, fi * 512:(fi + 1) * 512],
                                                in0=tsw[:, fi * 512:(fi + 1) * 512],
                                                in1=pb[6 + fi % 2][:, 0:512],
                                                op=ADD), sV)
                        bank_guard[6 + fi % 2] = (sV, c)
                for fi in range(4):
                    @blk.tensor
                    def _(t, dt_i=dt_i, fi=fi):
                        bank_wait(t, 6 + fi % 2)
                        inc(t.matmul(pb[6 + fi % 2][0:32, 0:512],
                                     lhsT=wbcT[:, dt_i * 32:(dt_i + 1) * 32],
                                     rhs=hst[dt_i % 2][:, fi * 512:(fi + 1) * 512],
                                     start=True, stop=True), sT)
                    t_here = val(sT)

                    @blk.vector
                    def _(v, fi=fi, t_here=t_here):
                        v.wait_ge(sT, t_here)
                        c = inc(v.tensor_tensor(out=bcw[:, fi * 512:(fi + 1) * 512],
                                                in0=bcw[:, fi * 512:(fi + 1) * 512],
                                                in1=pb[6 + fi % 2][0:32, 0:512],
                                                op=ADD), sV)
                        bank_guard[6 + fi % 2] = (sV, c)
                hst_cons[dt_i % 2] = val(sT)
                tsbc_add_v = val(sV)
            p2_s = val(sS)

            @blk.sync
            def _(sy):
                sy.wait_ge(sDo, val(sDo))
                sy.wait_ge(sV, tsbc_add_v)
                inc(sy.dma_start(out=dtbc_i[0:R, :], in_=tsw[:]), sDo, 16)
                inc(sy.dma_start(out=dtbc_i[R:R + 2 * N, :], in_=bcw[:]), sDo, 16)
            p2b_d = val(sDo)

            @blk.gpsimd
            def _(g):
                g.wait_ge(sDo, p2b_d)
                inc(g.collective_compute("AllReduce", ADD, replica_groups=PAIRS,
                                         ins=[dtbc_i.opt()], outs=[dtbc_o.opt()]), sG)
            p2b_g = val(sG)

            @blk.sync
            def _(sy):
                sy.wait_ge(sDx, val(sDx))
                sy.wait_ge(sG, p2b_g)
                inc(sy.dma_start(out=tsw[:], in_=dtbc_o[0:R, :]), sDx, 16)
                inc(sy.dma_start(out=bcw[:], in_=dtbc_o[R:R + 2 * N, :]), sDx, 16)
            p2b_d2 = val(sDx)

            @blk.vector
            def _(v):
                v.wait_ge(sDx, p2b_d2)
                inc(v.tensor_copy(out=ts_b[:], in_=tsw[:]), sV)
                inc(v.tensor_copy(out=bcb[:], in_=bcw[:]), sV)
            p2b_v2 = val(sV)

            p2b_d3 = p2b_d2

            zb = [sb("zb0", [128, L], scope=sc3), sb("zb1", [128, L], scope=sc3)]
            az = [sb("az0", [128, L], scope=sc3), sb("az1", [128, L], scope=sc3)]

            zb_dma = [0, 0]
            for dt_i in range(NT):
                pz = dt_i % 2
                for fi in range(4):
                    @blk.tensor
                    def _(t, dt_i=dt_i, fi=fi):
                        if dt_i == 0 and fi == 0:
                            t.wait_ge(sV, p2b_v2)
                        bank_wait(t, fi % 2)
                        inc(t.matmul(pb[fi % 2][:, 0:512],
                                     lhsT=wdpT[:, dt_i * 128:(dt_i + 1) * 128],
                                     rhs=ts_b[:, fi * 512:(fi + 1) * 512],
                                     start=True, stop=True), sT)
                    t_here = val(sT)

                    @blk.scalar
                    def _(scl, dt_i=dt_i, fi=fi, t_here=t_here, pz=pz,
                          zdp=zb_dma[dt_i % 2]):
                        scl.wait_ge(sT, t_here)
                        scl.wait_ge(sDo, zdp)
                        c = inc(scl.activation(out=zb[pz][:, fi * 512:(fi + 1) * 512],
                                               in_=pb[fi % 2][:, 0:512], func=AF.Identity,
                                               bias=bdcol[:, dt_i:dt_i + 1]), sS)
                        bank_guard[fi % 2] = (sS, c)
                s_here = val(sS)

                @blk.vector
                def _(v, s_here=s_here, pz=pz):
                    v.wait_ge(sS, s_here)
                    inc(v.tensor_scalar(out=az[pz][:], in0=zb[pz][:], scalar1=-1.0,
                                        scalar2=None, op0=MUL, op1=BYP), sV)
                    v.wait_ge(sV, val(sV))
                    inc(v.tensor_tensor(out=az[pz][:], in0=az[pz][:], in1=zb[pz][:],
                                        op=MAXO), sV)
                v2 = val(sV)

                @blk.scalar
                def _(scl, v2=v2, pz=pz):
                    scl.wait_ge(sV, v2)
                    inc(scl.activation(out=az[pz][:], in_=az[pz][:], func=AF.Exp,
                                       scale=-1.0), sS)
                    scl.wait_ge(sS, val(sS))
                    inc(scl.activation(out=az[pz][:], in_=az[pz][:], func=AF.Ln,
                                       bias=1.0), sS)
                s3 = val(sS)

                @blk.vector
                def _(v, s3=s3, pz=pz):
                    v.wait_ge(sS, s3)
                    inc(v.scalar_tensor_tensor(out=zb[pz][:], in0=zb[pz][:], scalar=0.0,
                                               in1=az[pz][:], op0=MAXO, op1=ADD), sV)
                v3 = val(sV)

                @blk.sync
                def _(sy, dt_i=dt_i, v3=v3, pz=pz):
                    sy.wait_ge(sDo, val(sDo))
                    sy.wait_ge(sV, v3)
                    inc(sy.dma_start(out=dt_sp[dt_i * 128:(dt_i + 1) * 128, :],
                                     in_=zb[pz][:]), sDo, 16)
                zb_dma[pz] = val(sDo)
            p3_d = val(sDo)

        with ExitStack() as sc4:
            sc4a = st.enter_context(ExitStack())
            s_int = sb("s_int", [128, LC * N], BF16, scope=sc4a)
            b_bc = sb("b_bc", [128, N * LC], BF16, scope=sc4a)
            c_bc = sb("c_bc", [128, LC * N], BF16, scope=sc4a)
            dAb = [sb("dA0", [128, LC], scope=sc4a), sb("dA1", [128, LC], scope=sc4a)]
            dtc = [sb("dtc0", [128, LC], scope=sc4a), sb("dtc1", [128, LC], scope=sc4a)]
            hsc = [sb("hsc0", [128, LC], scope=sc4a), sb("hsc1", [128, LC], scope=sc4a)]
            sic = [sb("sic0", [128, LC], BF16, scope=sc4a),
                   sb("sic1", [128, LC], BF16, scope=sc4a)]
            u_b = sb("u_b", [128, LC], BF16, scope=sc4a)
            dbu = sb("dbu", [128, LC], BF16, scope=sc4a)
            ych = sb("ych", [128, LC], scope=sc4a)
            y3c = [sb("y3c0", [128, LC], BF16, scope=sc4a),
                   sb("y3c1", [128, LC], BF16, scope=sc4a)]

            bstage = sb("bstage", [1, LC], BF16, scope=sc4a)
            cstage = sb("cstage", [1, LC], BF16, scope=sc4a)

            p2_spills = [(sDp[0], val(sDp[0])), (sDp[1], val(sDp[1])),
                         (sDq[0], val(sDq[0])), (sDq[1], val(sDq[1]))]
            IDX = [(c_, d_) for c_ in range(2) for d_ in range(NT)]
            pcons = {0: (0, 0), 1: (0, 0)}
            y3dma = {0: 0, 1: 0}
            loads = {}

            def p4_load(i):
                ch_, dt_ = IDX[i]
                p = i % 2
                sem = sDs[p]
                i_ = i

                @blk.sync
                def _(sy, ch_=ch_, dt_=dt_, p=p, sem=sem, pc=pcons[p], i=i):
                    sy.wait_ge(sem, val(sem))
                    if i < 2:
                        sy.wait_ge(sDo, p3_d)
                        for s_, c_ in p2_spills:
                            sy.wait_ge(s_, c_)
                    sy.wait_ge(sV, pc[0])
                    sy.wait_ge(sS, pc[1])
                    inc(sy.dma_start(out=dtc[p][:],
                                     in_=dt_sp[dt_ * 128:(dt_ + 1) * 128,
                                               ch_ * LC:(ch_ + 1) * LC]), sem, 16)
                    inc(sy.dma_start(out=hsc[p][:],
                                     in_=hs_sp[dt_ * 128:(dt_ + 1) * 128,
                                               ch_ * LC:(ch_ + 1) * LC]), sem, 16)
                    inc(sy.dma_start(out=sic[p][:],
                                     in_=silu_sp[dt_ * 128:(dt_ + 1) * 128,
                                                 ch_ * LC:(ch_ + 1) * LC]), sem, 16)
                loads[i] = val(sDs[i % 2])

            bcv = {}
            prev_tts = None
            p4_load(0)
            for ch in range(2):
                chunk_entry_v = val(sV)
                bst_mm = 0
                for n in range(N):
                    @blk.sync
                    def _(sy, ch=ch, n=n, bst_mm=bst_mm, chunk_entry_v=chunk_entry_v):
                        sy.wait_ge(sDx, val(sDx))
                        sy.wait_ge(sV, p2b_v2)
                        sy.wait_ge(sT, bst_mm)
                        if n == 0:
                            sy.wait_ge(sV, chunk_entry_v)
                        inc(sy.dma_start(out=bstage[:],
                                         in_=bcb[n:n + 1, ch * LC:(ch + 1) * LC]),
                            sDx, 16)
                        inc(sy.dma_start(out=cstage[:],
                                         in_=bcb[N + n:N + n + 1,
                                                 ch * LC:(ch + 1) * LC]), sDx, 16)
                    d_here = val(sDx)

                    @blk.tensor
                    def _(t, n=n, d_here=d_here):
                        t.wait_ge(sDx, d_here)
                        for fb in range(LC // 512):
                            bk = (n * 2 + fb) % 4
                            bank_wait(t, bk)
                            inc(t.matmul(pb[bk][:, 0:512], lhsT=ones_rb[:],
                                         rhs=bstage[0:1, fb * 512:(fb + 1) * 512],
                                         start=True, stop=True), sT)
                            t_here = val(sT)

                            @blk.vector
                            def _(v, n=n, fb=fb, bk=bk, t_here=t_here, ch=ch):
                                v.wait_ge(sT, t_here)
                                if ch == 0 and n == 0 and fb == 0:
                                    v.wait_ge(sDo, p3_d)
                                c = inc(v.tensor_copy(
                                    out=b_bc[:, n * LC + fb * 512:
                                             n * LC + (fb + 1) * 512],
                                    in_=pb[bk][:, 0:512]), sV)
                                bank_guard[bk] = (sV, c)
                        for fb in range(LC // 512):
                            bk = 4 + (n * 2 + fb) % 2
                            bank_wait(t, bk)
                            inc(t.matmul(pb[bk][:, 0:512], lhsT=ones_rb[:],
                                         rhs=cstage[0:1, fb * 512:(fb + 1) * 512],
                                         start=True, stop=True), sT)
                            t_here = val(sT)

                            @blk.vector
                            def _(v, n=n, fb=fb, bk=bk, t_here=t_here):
                                v.wait_ge(sT, t_here)
                                c = inc(v.tensor_copy(
                                    out=c_bc[:, n * LC + fb * 512:
                                             n * LC + (fb + 1) * 512],
                                    in_=pb[bk][:, 0:512]), sV)
                                bank_guard[bk] = (sV, c)
                    bst_mm = val(sT)
                bcv[ch] = val(sV)

                for dt_i in range(NT):
                    i = ch * NT + dt_i
                    if i + 1 < 2 * NT:
                        p4_load(i + 1)
                    p = i % 2
                    sem = sDs[p]
                    d_here = loads[i]

                    @blk.vector
                    def _(v, p=p, sem=sem, d_here=d_here, bcv_ch=bcv[ch]):
                        v.wait_ge(sem, d_here)
                        v.wait_ge(sV, bcv_ch)
                        inc(v.tensor_tensor(out=u_b[:], in0=dtc[p][:], in1=hsc[p][:],
                                            op=MUL), sV)
                    p4_u_v = val(sV)

                    tts_c = {}
                    for n in range(N):
                        @blk.scalar
                        def _(scl, dt_i=dt_i, n=n, p=p, sem=sem, d_here=d_here,
                              tts_c=tts_c, i=i, ptts=prev_tts):
                            scl.wait_ge(sem, d_here)
                            if i == 0 and n == 0:
                                scl.wait_ge(sDo, p3_d)
                            if n >= 2:
                                scl.wait_ge(sV, tts_c[n - 2])
                            elif ptts is not None:
                                scl.wait_ge(sV, ptts[N - 2 + n])
                            inc(scl.activation(
                                out=dAb[n % 2][:], in_=dtc[p][:], func=AF.Exp,
                                scale=acol[:, dt_i * N + n:dt_i * N + n + 1]), sS)
                        s_dA = val(sS)

                        @blk.vector
                        def _(v, ch=ch, dt_i=dt_i, n=n, p4_u_v=p4_u_v, s_dA=s_dA,
                              tts_c=tts_c):
                            v.wait_ge(sV, p4_u_v)
                            v.wait_ge(sS, s_dA)
                            if n > 0:
                                v.wait_ge(sV, tts_c[n - 1])
                            inc(v.tensor_tensor(out=dbu[:], in0=u_b[:],
                                                in1=b_bc[:, n * LC:(n + 1) * LC],
                                                op=MUL), sV)
                            v.wait_ge(sV, val(sV))
                            init = (0.0 if ch == 0
                                    else state[:, dt_i * N + n:dt_i * N + n + 1])
                            inc(v.tensor_tensor_scan(
                                out=s_int[:, n * LC:(n + 1) * LC],
                                data0=dAb[n % 2][:], data1=dbu[:], initial=init,
                                op0=MUL, op1=ADD), sV)
                        tts_c[n] = val(sV)

                    @blk.vector
                    def _(v, ch=ch, dt_i=dt_i, p=p, tts_c=tts_c, y3p=y3dma[p]):
                        v.wait_ge(sV, tts_c[N - 1])
                        if ch == 0:
                            sview = s_int.ap().rearrange("p (n l) -> p n l", n=N)
                            inc(v.tensor_copy(
                                out=state.ap()[:, dt_i * N:(dt_i + 1) * N],
                                in_=sview[:, :, LC - 1:LC]), sV)
                            v.wait_ge(sV, val(sV))
                        inc(v.tensor_tensor(out=s_int[:], in0=s_int[:], in1=c_bc[:],
                                            op=MUL), sV)
                        v.wait_ge(sV, val(sV))
                        inc(v.tensor_tensor(out=ych[:], in0=s_int[:, 0:LC],
                                            in1=s_int[:, LC:2 * LC], op=ADD), sV)
                        for n in range(2, N):
                            v.wait_ge(sV, val(sV))
                            inc(v.tensor_tensor(out=ych[:], in0=ych[:],
                                                in1=s_int[:, n * LC:(n + 1) * LC],
                                                op=ADD), sV)
                        v.wait_ge(sV, val(sV))
                        inc(v.scalar_tensor_tensor(out=ych[:], in0=hsc[p][:],
                                                   scalar=dcol[:, dt_i:dt_i + 1],
                                                   in1=ych[:], op0=MUL, op1=ADD), sV)
                        v.wait_ge(sV, val(sV))
                        v.wait_ge(sDo, y3p)
                        inc(v.tensor_tensor(out=y3c[p][:], in0=ych[:], in1=sic[p][:],
                                            op=MUL), sV)
                    y3v = val(sV)
                    pcons[p] = (y3v, val(sS))
                    prev_tts = dict(tts_c)

                    @blk.sync
                    def _(sy, ch=ch, dt_i=dt_i, p=p, y3v=y3v):
                        sy.wait_ge(sDo, val(sDo))
                        sy.wait_ge(sV, y3v)
                        inc(sy.dma_start(
                            out=y3_sp[dt_i * 128:(dt_i + 1) * 128,
                                      ch * LC:(ch + 1) * LC],
                            in_=y3c[p][:]), sDo, 16)
                    y3dma[p] = val(sDo)
            p4_v = val(sV)
            p4_y3_d = val(sDo)

            sc4a.close()
            opw = [sb("opw0", [128, H], BF16, scope=sc4),
                   sb("opw1", [128, H], BF16, scope=sc4)]
            oev = [sb("oev0", [128, L], scope=sc4),
                   sb("oev1", [128, L], scope=sc4)]
            yst = [sb(f"yst{j}", [128, L], BF16, scope=sc4) for j in range(4)]
            ksem5 = [sDs[0], sDs[1], sDp[0], sDp[1]]

            prev_grp_t = 0
            oev_dma = [0, 0]
            yst_cons = [0, 0, 0, 0]
            for grp in range(8):
                @blk.sync
                def _(sy, grp=grp, prev_grp_t=prev_grp_t):
                    sy.wait_ge(sDw, val(sDw))
                    sy.wait_ge(sT, prev_grp_t)
                    if grp == 0:
                        sy.wait_ge(sDo, p4_y3_d)
                    inc(sy.dma_start(out=opw[0][:],
                                     in_=woutT_sp[(2 * grp) * 128:(2 * grp + 1) * 128,
                                                  :]), sDw, 16)
                    inc(sy.dma_start(out=opw[1][:],
                                     in_=woutT_sp[(2 * grp + 1) * 128:
                                                  (2 * grp + 2) * 128, :]), sDw, 16)
                d_here = val(sDw)

                for kb in range(NT):
                    sl = kb % 4

                    @blk.sync
                    def _(sy, grp=grp, kb=kb, sl=sl):
                        sem = ksem5[sl]
                        sy.wait_ge(sem, val(sem))
                        sy.wait_ge(sT, yst_cons[sl])
                        if grp == 0 and kb < 4:
                            sy.wait_ge(sDo, p4_y3_d)
                        inc(sy.dma_start(out=yst[sl][:],
                                         in_=y3_sp[kb * 128:(kb + 1) * 128, :]),
                            sem, 16)
                    dk_here = val(ksem5[sl])

                    @blk.tensor
                    def _(t, grp=grp, kb=kb, sl=sl, d_here=d_here, dk_here=dk_here):
                        t.wait_ge(sDw, d_here)
                        t.wait_ge(ksem5[sl], dk_here)
                        if grp == 0 and kb == 0:
                            t.wait_ge(sV, p4_v)
                        if kb == 0:
                            for i in range(8):
                                bank_wait(t, i)
                        for hh in range(2):
                            for fi in range(4):
                                inc(t.matmul(
                                    pb[hh * 4 + fi][:, 0:512],
                                    lhsT=opw[hh][:, kb * 128:(kb + 1) * 128],
                                    rhs=yst[sl][:, fi * 512:(fi + 1) * 512],
                                    start=(kb == 0), stop=(kb == NT - 1)), sT)
                    yst_cons[sl] = val(sT)
                prev_grp_t = val(sT)
                mm_end = prev_grp_t

                for hh in range(2):
                    @blk.vector
                    def _(v, hh=hh, mm_end=mm_end, od=oev_dma[hh], grp=grp):
                        v.wait_ge(sT, mm_end)
                        v.wait_ge(sDp[hh], od)
                        if grp == 0:
                            v.wait_ge(sDo, p4_y3_d)
                        for fi in range(4):
                            c = inc(v.tensor_copy(
                                out=oev[hh][:, fi * 512:(fi + 1) * 512],
                                in_=pb[hh * 4 + fi][:, 0:512]), sV)
                            bank_guard[hh * 4 + fi] = (sV, c)
                    v_here = val(sV)

                    @blk.sync
                    def _(sy, grp=grp, hh=hh, v_here=v_here):
                        sem = sDp[hh]
                        sy.wait_ge(sem, val(sem))
                        sy.wait_ge(sV, v_here)
                        ht = grp * 2 + hh
                        inc(sy.dma_start(out=op_i[ht * 128:(ht + 1) * 128, :],
                                         in_=oev[hh][:, 0:LH]), sem, 16)
                        inc(sy.dma_start(out=op_i[H + ht * 128:H + (ht + 1) * 128, :],
                                         in_=oev[hh][:, LH:L]), sem, 16)
                    oev_dma[hh] = val(sDp[hh])
            p5a_d = [(sDp[0], val(sDp[0])), (sDp[1], val(sDp[1]))]

            @blk.gpsimd
            def _(g):
                for s_, c_ in p5a_d:
                    g.wait_ge(s_, c_)
                inc(g.collective_compute("ReduceScatter", ADD, replica_groups=PAIRS,
                                         ins=[op_i.opt()], outs=[op_o.opt()]), sG)
            p5a_g = val(sG)

        with ExitStack() as sc5:
            x2 = sb("x2", [128, NT * LH], scope=sc5)
            xs2 = tbf2[:, 0:H]
            att = xs[:, 0:LH]

            prev_xt = 0
            for lt in range(8):
                @blk.sync
                def _(sy, lt=lt, prev_xt=prev_xt):
                    sy.wait_ge(sDx, val(sDx))
                    sy.wait_ge(sT, prev_xt)
                    if lt == 0:
                        for s_, c_ in p5a_d:
                            sy.wait_ge(s_, c_)
                    inc(sy.dma_start(out=xs2[:], in_=x_h[lt * 128:(lt + 1) * 128, :]),
                        sDx, 16)
                d_here = val(sDx)

                @blk.tensor
                def _(t, lt=lt, d_here=d_here):
                    t.wait_ge(sDx, d_here)
                    for hb in range(NT):
                        bank_wait(t, hb % 2)
                        inc(t.transpose(pb[hb % 2][:, 0:128],
                                        in_=xs2[:, hb * 128:(hb + 1) * 128],
                                        identity=ident_f[:]), sT)
                        t_here = val(sT)

                        @blk.vector
                        def _(v, lt=lt, hb=hb, t_here=t_here):
                            v.wait_ge(sT, t_here)
                            if lt == 0 and hb == 0:
                                for s_, c_ in p5a_d:
                                    v.wait_ge(s_, c_)
                            c = inc(v.tensor_copy(
                                out=x2[:, hb * LH + lt * 128:hb * LH + (lt + 1) * 128],
                                in_=pb[hb % 2][:, 0:128]), sV)
                            bank_guard[hb % 2] = (sV, c)
                prev_xt = val(sT)
            p5b_v = val(sV)

            for ht in range(NT):
                @blk.sync
                def _(sy, ht=ht):
                    sy.wait_ge(sDx, val(sDx))
                    sy.wait_ge(sG, p5a_g)
                    if ht > 0:
                        sy.wait_ge(sV, val(sV))
                    inc(sy.dma_start(out=att[:],
                                     in_=op_o[ht * 128:(ht + 1) * 128, :]), sDx, 16)
                d_here = val(sDx)

                @blk.vector
                def _(v, ht=ht, d_here=d_here):
                    v.wait_ge(sDx, d_here)
                    v.wait_ge(sV, p5b_v)
                    inc(v.scalar_tensor_tensor(
                        out=x2[:, ht * LH:(ht + 1) * LH], in0=att[:],
                        scalar=ga_c[:, ht:ht + 1],
                        in1=x2[:, ht * LH:(ht + 1) * LH], op0=MUL, op1=ADD), sV)
            p5b_v2 = val(sV)

            h2 = sb("h2", [128, NT * LH], BF16, scope=sc5)
            mrow = sb("mrow", [1, LH], scope=sc5)
            rrow = sb("rrow", [1, LH], scope=sc5)
            m_bc = sb("m_bc", [128, LH], scope=sc5)
            r_bc = sb("r_bc", [128, LH], scope=sc5)
            tb2 = sb("tb2", [128, LH], scope=sc5)
            sqf = tb2

            stat_t = 0
            for ht in range(NT):
                @blk.scalar
                def _(scl, ht=ht, stat_t=stat_t):
                    scl.wait_ge(sV, p5b_v2)
                    scl.wait_ge(sT, stat_t)
                    inc(scl.activation(out=sqf[:], in_=x2[:, ht * LH:(ht + 1) * LH],
                                       func=AF.Square), sS)
                s_here = val(sS)

                @blk.tensor
                def _(t, ht=ht, s_here=s_here):
                    t.wait_ge(sS, s_here)
                    if ht == 0:
                        for i in (4, 5, 6, 7):
                            bank_wait(t, i)
                    for fi in range(2):
                        inc(t.matmul(pb[4 + fi][0:1, 0:512],
                                     lhsT=ones_cf[:],
                                     rhs=x2[:, ht * LH + fi * 512:
                                            ht * LH + (fi + 1) * 512],
                                     start=(ht == 0), stop=(ht == NT - 1)), sT)
                        inc(t.matmul(pb[6 + fi][0:1, 0:512],
                                     lhsT=ones_cf[:],
                                     rhs=sqf[:, fi * 512:(fi + 1) * 512],
                                     start=(ht == 0), stop=(ht == NT - 1)), sT)
                stat_t = val(sT)
            p6_t = val(sT)

            @blk.vector
            def _(v):
                v.wait_ge(sT, p6_t)
                for fi in range(2):
                    c = inc(v.tensor_copy(out=mrow[0:1, fi * 512:(fi + 1) * 512],
                                          in_=pb[4 + fi][0:1, 0:512]), sV)
                    bank_guard[4 + fi] = (sV, c)
                    c = inc(v.tensor_copy(out=rrow[0:1, fi * 512:(fi + 1) * 512],
                                          in_=pb[6 + fi][0:1, 0:512]), sV)
                    bank_guard[6 + fi] = (sV, c)
                v.wait_ge(sV, val(sV))
                inc(v.tensor_scalar(out=mrow[:], in0=mrow[:], scalar1=1.0 / H,
                                    scalar2=None, op0=MUL, op1=BYP), sV)
                v.wait_ge(sV, val(sV))
                inc(v.tensor_tensor(out=tb2[0:1, 0:LH], in0=mrow[:], in1=mrow[:],
                                    op=MUL), sV)
                v.wait_ge(sV, val(sV))
                inc(v.scalar_tensor_tensor(out=rrow[:], in0=rrow[:], scalar=1.0 / H,
                                           in1=tb2[0:1, 0:LH], op0=MUL, op1=SUB), sV)
                v.wait_ge(sV, val(sV))
                inc(v.tensor_scalar(out=rrow[:], in0=rrow[:], scalar1=LN_EPS,
                                    scalar2=None, op0=ADD, op1=BYP), sV)
            p6_v = val(sV)

            @blk.scalar
            def _(scl):
                scl.wait_ge(sV, p6_v)
                inc(scl.activation(out=rrow[:], in_=rrow[:], func=AF.Sqrt), sS)
            p6_s = val(sS)

            @blk.vector
            def _(v):
                v.wait_ge(sS, p6_s)
                inc(v.reciprocal(out=rrow[:], in_=rrow[:]), sV)
            p6_v2 = val(sV)

            @blk.tensor
            def _(t):
                t.wait_ge(sV, p6_v2)
                for fi in range(2):
                    bank_wait(t, 6)
                    inc(t.matmul(pb[6][:, 0:512], lhsT=ones_rf[:],
                                 rhs=mrow[0:1, fi * 512:(fi + 1) * 512],
                                 start=True, stop=True), sT)
                    t_here = val(sT)

                    @blk.vector
                    def _(v, fi=fi, t_here=t_here):
                        v.wait_ge(sT, t_here)
                        c = inc(v.tensor_copy(out=m_bc[:, fi * 512:(fi + 1) * 512],
                                              in_=pb[6][:, 0:512]), sV)
                        bank_guard[6] = (sV, c)
                for fi in range(2):
                    bank_wait(t, 7)
                    inc(t.matmul(pb[7][:, 0:512], lhsT=ones_rf[:],
                                 rhs=rrow[0:1, fi * 512:(fi + 1) * 512],
                                 start=True, stop=True), sT)
                    t_here = val(sT)

                    @blk.vector
                    def _(v, fi=fi, t_here=t_here):
                        v.wait_ge(sT, t_here)
                        c = inc(v.tensor_copy(out=r_bc[:, fi * 512:(fi + 1) * 512],
                                              in_=pb[7][:, 0:512]), sV)
                        bank_guard[7] = (sV, c)
            p6_bc_v = val(sV)

            prev_h2_s = 0
            for ht in range(NT):
                @blk.vector
                def _(v, ht=ht, prev_h2_s=prev_h2_s):
                    v.wait_ge(sV, p6_bc_v)
                    v.wait_ge(sS, prev_h2_s)
                    inc(v.tensor_tensor(out=tb2[:], in0=x2[:, ht * LH:(ht + 1) * LH],
                                        in1=m_bc[:], op=SUB), sV)
                    v.wait_ge(sV, val(sV))
                    inc(v.tensor_tensor(out=tb2[:], in0=tb2[:], in1=r_bc[:], op=MUL),
                        sV)
                v_here = val(sV)

                @blk.scalar
                def _(scl, ht=ht, v_here=v_here):
                    scl.wait_ge(sV, v_here)
                    inc(scl.activation(out=h2[:, ht * LH:(ht + 1) * LH], in_=tb2[:],
                                       func=AF.Identity, scale=cb1[:, ht:ht + 1],
                                       bias=sb_c[:, ht:ht + 1]), sS)
                prev_h2_s = val(sS)
            p6_h2_s = val(sS)

            fw = [sb("fw0", [128, H], BF16, scope=sc5),
                  sb("fw1", [128, H], BF16, scope=sc5)]
            hbb = [sb("hbb0", [128, LH], BF16, scope=sc5),
                   sb("hbb1", [128, LH], BF16, scope=sc5)]
            hq = [sb("hq0", [128, LH], BF16, scope=sc5),
                  sb("hq1", [128, LH], BF16, scope=sc5)]
            msr = mrow

            fw_cons = [0, 0]
            hb_dma = [0, 0]
            ms_t = 0
            hq_s = [0, 0]
            for m in range(HID // 128):
                @blk.sync
                def _(sy, m=m):
                    sy.wait_ge(sDw, val(sDw))
                    sy.wait_ge(sT, fw_cons[m % 2])
                    if m == 0:
                        sy.wait_ge(sG, p0c_g)
                    inc(sy.dma_start(
                        out=fw[m % 2][:],
                        in_=wfcT_o[m * 128:(m + 1) * 128, :]), sDw, 16)
                d_here = val(sDw)

                @blk.tensor
                def _(t, m=m, d_here=d_here):
                    t.wait_ge(sDw, d_here)
                    if m == 0:
                        t.wait_ge(sS, p6_h2_s)
                    bank_wait(t, (m % 2) * 2)
                    bank_wait(t, 1 + (m % 2) * 2)
                    for kb in range(NT):
                        for fi in range(2):
                            inc(t.matmul(
                                pb[fi + (m % 2) * 2][:, 0:512],
                                lhsT=fw[m % 2][:, kb * 128:(kb + 1) * 128],
                                rhs=h2[:, kb * LH + fi * 512:kb * LH + (fi + 1) * 512],
                                start=(kb == 0), stop=(kb == NT - 1)), sT)
                fw_cons[m % 2] = val(sT)
                t_here = val(sT)

                @blk.scalar
                def _(scl, m=m, t_here=t_here, hbd=hb_dma[m % 2], ms_t=ms_t):
                    scl.wait_ge(sT, t_here)
                    scl.wait_ge(sDs[m % 2], hbd)
                    scl.wait_ge(sT, ms_t)
                    for fi in range(2):
                        inc(scl.activation(out=sqf[:, fi * 512:(fi + 1) * 512],
                                           in_=pb[fi + (m % 2) * 2][:, 0:512],
                                           func=AF.Relu), sS)
                    scl.wait_ge(sS, val(sS))
                    c = None
                    for fi in range(2):
                        c = inc(scl.activation(
                            out=hbb[m % 2][:, fi * 512:(fi + 1) * 512],
                            in_=sqf[:, fi * 512:(fi + 1) * 512],
                            func=AF.Square), sS)
                    for fi in range(2):
                        bank_guard[fi + (m % 2) * 2] = (sS, c)
                    scl.wait_ge(sS, val(sS))
                    inc(scl.activation(out=hq[m % 2][:], in_=hbb[m % 2][:],
                                       func=AF.Square), sS)
                s_here = val(sS)
                hq_s[m % 2] = s_here

                if m > 0:
                    @blk.tensor
                    def _(t, mprev=m - 1, sprev=hq_s[(m - 1) % 2]):
                        t.wait_ge(sS, sprev)
                        if mprev == 0:
                            bank_wait(t, 6)
                            bank_wait(t, 7)
                        for fi in range(2):
                            inc(t.matmul(pb[6 + fi][0:1, 0:512],
                                         lhsT=ones_cb[:],
                                         rhs=hq[mprev % 2][:, fi * 512:(fi + 1) * 512],
                                         start=(mprev == 0), stop=False), sT)
                    ms_t = val(sT)

                @blk.sync
                def _(sy, m=m, s_here=s_here):
                    sem = sDs[m % 2]
                    sy.wait_ge(sem, val(sem))
                    sy.wait_ge(sS, s_here)
                    inc(sy.dma_start(out=h_sp[m * 128:(m + 1) * 128, :],
                                     in_=hbb[m % 2][:]), sem, 16)
                hb_dma[m % 2] = val(sDs[m % 2])

            @blk.tensor
            def _(t, mprev=HID // 128 - 1, sprev=hq_s[(HID // 128 - 1) % 2]):
                t.wait_ge(sS, sprev)
                for fi in range(2):
                    inc(t.matmul(pb[6 + fi][0:1, 0:512],
                                 lhsT=ones_cb[:],
                                 rhs=hq[mprev % 2][:, fi * 512:(fi + 1) * 512],
                                 start=False, stop=True), sT)
            ms_t = val(sT)
            p7_fc_d = [(sDs[0], val(sDs[0])), (sDs[1], val(sDs[1]))]

            @blk.vector
            def _(v):
                v.wait_ge(sT, ms_t)
                for fi in range(2):
                    c = inc(v.tensor_copy(out=msr[0:1, fi * 512:(fi + 1) * 512],
                                          in_=pb[6 + fi][0:1, 0:512]), sV)
                    bank_guard[6 + fi] = (sV, c)
                v.wait_ge(sV, val(sV))
                inc(v.tensor_scalar(out=msr[:], in0=msr[:], scalar1=1.0 / HID,
                                    scalar2=RMS_EPS, op0=MUL, op1=ADD), sV)
            p7_ms_v = val(sV)

            @blk.scalar
            def _(scl):
                scl.wait_ge(sV, p7_ms_v)
                inc(scl.activation(out=msr[:], in_=msr[:], func=AF.Sqrt), sS)
            p7_s = val(sS)

            @blk.vector
            def _(v):
                v.wait_ge(sS, p7_s)
                inc(v.reciprocal(out=msr[:], in_=msr[:]), sV)
            p7_v = val(sV)

            @blk.tensor
            def _(t):
                t.wait_ge(sV, p7_v)
                for fi in range(2):
                    bank_wait(t, 6)
                    inc(t.matmul(pb[6][:, 0:512], lhsT=ones_rf[:],
                                 rhs=msr[0:1, fi * 512:(fi + 1) * 512],
                                 start=True, stop=True), sT)
                    t_here = val(sT)

                    @blk.vector
                    def _(v, fi=fi, t_here=t_here):
                        v.wait_ge(sT, t_here)
                        c = inc(v.tensor_copy(out=m_bc[:, fi * 512:(fi + 1) * 512],
                                              in_=pb[6][:, 0:512]), sV)
                        bank_guard[6] = (sV, c)
            p7_rbc_v = val(sV)

            hw = [sb(f"hw{j}", [128, LH], BF16, scope=sc5) for j in range(4)]
            pw = [sb(f"pw{j}", [128, 4 * 128], BF16, scope=sc5) for j in range(4)]
            ksem = [sDs[0], sDs[1], sDp[0], sDp[1]]

            hw_cons = [0, 0, 0, 0]
            for dg in range(4):
                mm_end = None
                for kb in range(HID // 128):
                    cshard, hb8 = kb // 8, kb % 8
                    sl = kb % 4

                    @blk.sync
                    def _(sy, dg=dg, kb=kb, cshard=cshard, hb8=hb8, sl=sl):
                        sem = ksem[sl]
                        sy.wait_ge(sem, val(sem))
                        sy.wait_ge(sT, hw_cons[sl])
                        if dg == 0 and kb < 4:
                            for s_, c_ in p7_fc_d:
                                sy.wait_ge(s_, c_)
                        inc(sy.dma_start(out=hw[sl][:],
                                         in_=h_sp[kb * 128:(kb + 1) * 128, :]),
                            sem, 16)
                        inc(sy.dma_start(
                            out=pw[sl].ap().rearrange("p (mm c) -> p mm c", c=128),
                            in_=wpjT_o.rearrange("(g p) c -> p g c", p=128)[
                                :, cshard * NT + dg * 4:cshard * NT + dg * 4 + 4,
                                hb8 * 128:(hb8 + 1) * 128]), sem, 16)
                    d_here = val(ksem[sl])

                    @blk.tensor
                    def _(t, dg=dg, kb=kb, sl=sl, d_here=d_here):
                        t.wait_ge(ksem[sl], d_here)
                        if kb == 0:
                            t.wait_ge(sV, p7_rbc_v)
                            for i in range(8):
                                bank_wait(t, i)
                        for mm in range(4):
                            for fi in range(2):
                                inc(t.matmul(
                                    pb[mm * 2 + fi][:, 0:512],
                                    lhsT=pw[sl][:, mm * 128:(mm + 1) * 128],
                                    rhs=hw[sl][:, fi * 512:(fi + 1) * 512],
                                    start=(kb == 0), stop=(kb == HID // 128 - 1)), sT)
                    mm_end = val(sT)
                    hw_cons[sl] = mm_end

                for mm in range(4):
                    ht = dg * 4 + mm

                    @blk.vector
                    def _(v, mm=mm, ht=ht, mm_end=mm_end, ventry=val(sV)):
                        v.wait_ge(sT, mm_end)
                        v.wait_ge(sV, ventry)
                        for fi in range(2):
                            c = inc(v.scalar_tensor_tensor(
                                out=tb2[:, fi * 512:(fi + 1) * 512],
                                in0=pb[mm * 2 + fi][:, 0:512],
                                scalar=gb_c[:, ht:ht + 1],
                                in1=m_bc[:, fi * 512:(fi + 1) * 512],
                                op0=MUL, op1=MUL), sV)
                            bank_guard[mm * 2 + fi] = (sV, c)
                        v.wait_ge(sV, val(sV))
                        inc(v.tensor_tensor(out=x2[:, ht * LH:(ht + 1) * LH],
                                            in0=tb2[:],
                                            in1=x2[:, ht * LH:(ht + 1) * LH],
                                            op=ADD), sV)
            p7_v2 = val(sV)

            out_dma = 0
            for lt in range(8):
                @blk.tensor
                def _(t, lt=lt, out_dma=out_dma):
                    t.wait_ge(sV, p7_v2)
                    t.wait_ge(sDo, out_dma)
                    for hb in range(NT):
                        bank_wait(t, hb % 2)
                        inc(t.transpose(pb[hb % 2][:, 0:128],
                                        in_=x2[:, hb * LH + lt * 128:
                                               hb * LH + (lt + 1) * 128],
                                        identity=ident_f[:]), sT)
                        t_here = val(sT)

                        @blk.vector
                        def _(v, hb=hb, t_here=t_here):
                            v.wait_ge(sT, t_here)
                            c = inc(v.tensor_copy(out=xs2[:, hb * 128:(hb + 1) * 128],
                                                  in_=pb[hb % 2][:, 0:128]), sV)
                            bank_guard[hb % 2] = (sV, c)
                v_here = val(sV)

                @blk.sync
                def _(sy, lt=lt, v_here=v_here):
                    sy.wait_ge(sDo, val(sDo))
                    sy.wait_ge(sV, v_here)
                    inc(sy.dma_start(out=out_my[lt * 128:(lt + 1) * 128, :],
                                     in_=xs2[:]), sDo, 16)
                out_dma = val(sDo)
            p8_d = val(sDo)

            @blk.sync
            def _(sy):
                sy.wait_ge(sDo, p8_d)

    return nc


_NC_CACHE = {}


def _get_nc():
    if "nc" not in _NC_CACHE:
        _NC_CACHE["nc"] = build_program()
    return _NC_CACHE["nc"]


def kernel(L0_attn_x, L0_attn_W_in_states, L0_attn_W_in_gates, L0_attn_conv_w,
           L0_attn_conv_b, L0_attn_W_dt, L0_attn_W_b, L0_attn_W_c,
           L0_attn_W_dtproj, L0_attn_b_dtproj, L0_attn_A_log, L0_attn_D,
           L0_attn_W_out, L0_ffn_W_fc, L0_ffn_W_proj, L0_c, L0_wmod):
    import ml_dtypes
    f = lambda a: np.ascontiguousarray(np.asarray(a, dtype=np.float32))
    X = f(L0_attn_x)
    Wis, Wig = f(L0_attn_W_in_states), f(L0_attn_W_in_gates)
    Cw, Cb = f(L0_attn_conv_w), f(L0_attn_conv_b)
    Wdt, Wb, Wc = f(L0_attn_W_dt), f(L0_attn_W_b), f(L0_attn_W_c)
    Wdp, Bdp = f(L0_attn_W_dtproj), f(L0_attn_b_dtproj)
    Alog, Dv = f(L0_attn_A_log), f(L0_attn_D)
    Wout, Wfc, Wpj = f(L0_attn_W_out), f(L0_ffn_W_fc), f(L0_ffn_W_proj)
    C0, Wmod = f(L0_c), f(L0_wmod)
    eyef = np.eye(128, dtype=np.float32)
    eyeb = np.eye(128, dtype=ml_dtypes.bfloat16)

    in_maps = []
    for c in range(NC):
        b, hf = c // 2, c % 2
        sd = slice(hf * DH, (hf + 1) * DH)
        in_maps.append({
            "x_b": X[b],
            "x_h": np.ascontiguousarray(X[b, hf * LH:(hf + 1) * LH]),
            "wis": np.ascontiguousarray(Wis[sd, :]),
            "wig": np.ascontiguousarray(Wig[sd, :]),
            "cw": np.ascontiguousarray(Cw[sd, :]),
            "cbia": np.ascontiguousarray(Cb[sd, None]),
            "wdt": np.ascontiguousarray(Wdt[:, sd]),
            "wbc": np.ascontiguousarray(
                np.concatenate([Wb[:, sd], Wc[:, sd]], axis=0)),
            "wdp": np.ascontiguousarray(Wdp[sd, :]),
            "bdp": np.ascontiguousarray(Bdp[sd, None]),
            "alog": np.ascontiguousarray(Alog[sd, :]),
            "dvec": np.ascontiguousarray(Dv[sd, None]),
            "wout_h": np.ascontiguousarray(Wout[:, sd]),
            "wfc_s": np.ascontiguousarray(
                Wfc[c * (HID // NC):(c + 1) * (HID // NC), :]),
            "wpj_s": np.ascontiguousarray(
                Wpj[:, c * (HID // NC):(c + 1) * (HID // NC)]),
            "wmod_s": np.ascontiguousarray(Wmod[:, hf * 6144:(hf + 1) * 6144]),
            "c0_my": np.ascontiguousarray(C0[b:b + 1, :]),
            "eyef": eyef,
            "eyeb": eyeb,
        })

    res = run_bass_kernel_spmd(_get_nc(), in_maps, core_ids=list(range(NC)))
    out = np.empty((B, L, H), dtype=np.float32)
    for c in range(NC):
        b, hf = c // 2, c % 2
        out[b, hf * LH:(hf + 1) * LH, :] = np.asarray(res.results[c]["out_my"])
    return out


# version 08 (final)
# speedup vs baseline: 1.1900x; 1.0983x over previous
import numpy as np
import concourse.bass as bass
import concourse.mybir as mybir
from concourse.alu_op_type import AluOpType
from concourse.bass_utils import run_bass_kernel_spmd
from contextlib import ExitStack

F32 = mybir.dt.float32
BF16 = mybir.dt.bfloat16
AF = mybir.ActivationFunctionType
AX = mybir.AxisListType

B, L, H, DI, N, R = 4, 2048, 2048, 4096, 16, 128
KC = 4
DH = DI // 2
HID = 8192
NC = 8
LH = L // 2
LC = 1024
NT = 16
RINT_C = 12582912.0
LN_EPS = 1e-5
RMS_EPS = 1.1920929e-07
PAIRS = [[0, 1], [2, 3], [4, 5], [6, 7]]
ALLG = [list(range(NC))]
MUL, ADD, SUB, MAXO, MINO, BYP = (AluOpType.mult, AluOpType.add, AluOpType.subtract,
                                  AluOpType.max, AluOpType.min, AluOpType.bypass)


def build_program():
    nc = bass.Bass("TRN2", target_bir_lowering=False, debug=False)

    def din(name, shape, dt=F32):
        return nc.dram_tensor(name, shape, dt, kind="ExternalInput").ap()

    def dtmp(name, shape, dt=F32, shared=False):
        if shared:
            return nc.dram_tensor(name, shape, dt, addr_space="Shared").ap()
        return nc.dram_tensor(name, shape, dt).ap()

    x_b = din("x_b", [L, H])
    x_h = din("x_h", [LH, H])
    wis = din("wis", [DH, H])
    wig = din("wig", [DH, H])
    cw = din("cw", [DH, KC])
    cbia = din("cbia", [DH, 1])
    wdt = din("wdt", [R, DH])
    wbc = din("wbc", [2 * N, DH])
    wdp = din("wdp", [DH, R])
    bdp = din("bdp", [DH, 1])
    alog = din("alog", [DH, N])
    dvec = din("dvec", [DH, 1])
    wout_h = din("wout_h", [H, DH])
    wfc_s = din("wfc_s", [HID // NC, H])
    wpj_s = din("wpj_s", [H, HID // NC])
    wmod_s = din("wmod_s", [H, 6144])
    c0_my = din("c0_my", [1, H])
    eyef = din("eyef", [128, 128])
    eyeb = din("eyeb", [128, 128], BF16)

    out_my = nc.dram_tensor("out_my", [LH, H], F32, kind="ExternalOutput").ap()

    hs_sp = dtmp("hs_sp", [DH, L])
    silu_sp = dtmp("silu_sp", [DH, L], BF16)
    dt_sp = dtmp("dt_sp", [DH, L])
    y3_sp = dtmp("y3_sp", [DH, L], BF16)
    h_sp = dtmp("h_sp", [HID, LH], BF16)
    wisT_sp = dtmp("wisT_sp", [DH, H], BF16)
    wigT_sp = dtmp("wigT_sp", [DH, H], BF16)
    woutT_sp = dtmp("woutT_sp", [H, DH], BF16)
    dtbc_i = dtmp("dtbc_i", [R + 2 * N, L])
    dtbc_o = dtmp("dtbc_o", [R + 2 * N, L])
    op_i = dtmp("op_i", [2 * H, LH])
    op_o = dtmp("op_o", [H, LH])
    mod_i = dtmp("mod_i", [1, 6144])
    mod_o = dtmp("mod_o", [2, 6144])
    wfcT_i = dtmp("wfcT_i", [HID // NC, H], BF16)
    wfcT_o = dtmp("wfcT_o", [HID, H], BF16, shared=True)
    wpjT_i = dtmp("wpjT_i", [H, HID // NC], BF16)
    wpjT_o = dtmp("wpjT_o", [NC * H, HID // NC], BF16, shared=True)

    cnt = {}

    def inc(instr, sem, n=1):
        instr.then_inc(sem, n)
        cnt[sem.num] = cnt.get(sem.num, 0) + n
        return cnt[sem.num]

    def val(sem):
        return cnt.get(sem.num, 0)

    st = ExitStack()
    E = st.enter_context
    with st:
        E(nc.allow_non_contiguous_dma(
            reason="small column-vector loads and interleaved row assembly"))
        sD = E(nc.semaphore("sD"))
        sDw = E(nc.semaphore("sDw"))
        sDx = E(nc.semaphore("sDx"))
        sDo = E(nc.semaphore("sDo"))
        sDs = [E(nc.semaphore("sDs0")), E(nc.semaphore("sDs1"))]
        sDp = [E(nc.semaphore("sDp0")), E(nc.semaphore("sDp1"))]
        sDq = [E(nc.semaphore("sDq0")), E(nc.semaphore("sDq1"))]
        sGP = E(nc.semaphore("sGP"))
        sV = E(nc.semaphore("sV"))
        sS = E(nc.semaphore("sS"))
        sT = E(nc.semaphore("sT"))
        sG = E(nc.semaphore("sG"))

        def sb(name, shape, dt=F32, scope=None):
            return (scope or st).enter_context(nc.sbuf_tensor(name, shape, dt))

        ident_f = sb("ident_f", [128, 128])
        ident_b = sb("ident_b", [128, 128], BF16)
        ones_cb = sb("ones_cb", [128, 1], BF16)
        ones_cf = sb("ones_cf", [128, 1])
        ones_rb = sb("ones_rb", [1, 128], BF16)
        ones_rf = sb("ones_rf", [1, 128])
        acol = sb("acol", [128, NT * N])
        dcol = sb("dcol", [128, NT])
        cbcol = sb("cbcol", [128, NT])
        bdcol = sb("bdcol", [128, NT])
        state = sb("state", [128, NT * N])
        ca1 = sb("ca1", [128, NT])
        sa_c = sb("sa_c", [128, NT])
        ga_c = sb("ga_c", [128, NT])
        cb1 = sb("cb1", [128, NT])
        sb_c = sb("sb_c", [128, NT])
        gb_c = sb("gb_c", [128, NT])
        wdtT = sb("wdtT", [128, NT * 128], BF16)
        wbcT = sb("wbcT", [128, NT * 32], BF16)
        wdpT = sb("wdpT", [128, NT * 128], BF16)
        diagw = sb("diagw", [128, NT * KC * 128], BF16)
        bcb = sb("bcb", [32, L], BF16)
        ts_b = sb("ts_b", [R, L], BF16)
        xs = sb("xs", [128, H])
        tbf2 = sb("tbf2", [128, 2 * H])
        colw = sb("colw", [128, 8])

        pb = [E(nc.psum_tensor(f"pb{i}", [128, 512], F32)) for i in range(8)]
        bank_guard = {i: None for i in range(8)}

        def bank_wait(t, i):
            if bank_guard[i] is not None:
                s, c = bank_guard[i]
                t.wait_ge(s, c)

        blk = E(nc.Block())

        def tp_blocks_bf16(src_sb, wcols, dst_sb, coloff=0, wait_v=None,
                           extra_waits=()):
            nb = wcols // 128

            @blk.tensor
            def _(t):
                if wait_v is not None:
                    t.wait_ge(sV, wait_v)
                for g0 in range(0, nb, 4):
                    gn = min(4, nb - g0)
                    bk = 6 + (g0 // 4) % 2
                    bank_wait(t, bk)
                    for j in range(gn):
                        cb = g0 + j
                        inc(t.transpose(
                            pb[bk][:, j * 64:(j + 1) * 64].bitcast(BF16),
                            in_=src_sb[:, cb * 128:(cb + 1) * 128],
                            identity=ident_b[:]), sT)
                    t_here = val(sT)

                    @blk.vector
                    def _(v, g0=g0, gn=gn, bk=bk, t_here=t_here):
                        v.wait_ge(sT, t_here)
                        for s_, c_ in extra_waits:
                            v.wait_ge(s_, c_)
                        c = inc(v.tensor_copy(
                            out=dst_sb[:, coloff + g0 * 128:
                                       coloff + (g0 + gn) * 128],
                            in_=pb[bk][:, 0:gn * 64].bitcast(BF16)), sV)
                        bank_guard[bk] = (sV, c)
            return val(sV)

        @blk.sync
        def _(sy):
            sy.wait_ge(sDw, val(sDw))
            inc(sy.dma_start(out=ident_f[:], in_=eyef), sDw, 16)
            inc(sy.dma_start(out=ident_b[:], in_=eyeb), sDw, 16)
            inc(sy.dma_start(out=acol.ap().rearrange("p (t n) -> p t n", n=N),
                             in_=alog.rearrange("(t p) n -> p t n", p=128)), sDw, 16)
            inc(sy.dma_start(out=dcol.ap(),
                             in_=dvec.rearrange("(t p) o -> p (t o)", p=128)), sDw, 16)
            inc(sy.dma_start(out=cbcol.ap(),
                             in_=cbia.rearrange("(t p) o -> p (t o)", p=128)), sDw, 16)
            inc(sy.dma_start(out=bdcol.ap(),
                             in_=bdp.rearrange("(t p) o -> p (t o)", p=128)), sDw, 16)
        p0a_d = val(sDw)

        @blk.vector
        def _(v):
            inc(v.memset(ones_cb[:], 1.0), sV)
            inc(v.memset(ones_cf[:], 1.0), sV)
            inc(v.memset(ones_rb[:], 1.0), sV)
            inc(v.memset(ones_rf[:], 1.0), sV)
        p0_ones_v = val(sV)

        @blk.scalar
        def _(sc):
            sc.wait_ge(sDw, p0a_d)
            inc(sc.activation(out=acol[:], in_=acol[:], func=AF.Exp), sS)
        p0_aexp_s = val(sS)

        @blk.vector
        def _(v):
            v.wait_ge(sS, p0_aexp_s)
            inc(v.tensor_scalar(out=acol[:], in0=acol[:], scalar1=-1.0, scalar2=None,
                                op0=MUL, op1=BYP), sV)
        p0_acol_v = val(sV)

        with ExitStack() as sc0:
            c0s = sb("c0s", [1, H], scope=sc0)
            silT = sb("silT", [128, NT], BF16, scope=sc0)
            wmstg = sb("wmstg", [128, 3072], scope=sc0)
            wmb = sb("wmb", [128, 3072], BF16, scope=sc0)
            mvrow = sb("mvrow", [1, 3072], scope=sc0)
            modr = sb("modr", [1, 6 * H], scope=sc0)
            modc = sb("modc", [128, 96], scope=sc0)
            tmpc = sb("tmpc", [128, 2 * NT], scope=sc0)

            @blk.sync
            def _(sy):
                sy.wait_ge(sDx, val(sDx))
                inc(sy.dma_start(out=c0s[:], in_=c0_my), sDx, 16)
            d0 = val(sDx)

            @blk.scalar
            def _(scl):
                scl.wait_ge(sDx, d0)
                inc(scl.activation(out=c0s[:], in_=c0s[:], func=AF.Silu), sS)
            s0 = val(sS)

            @blk.tensor
            def _(t):
                t.wait_ge(sS, s0)
                t.wait_ge(sDw, p0a_d)
                for j in range(NT):
                    bank_wait(t, 6)
                    inc(t.transpose(pb[6][:, j:j + 1],
                                    in_=c0s[0:1, j * 128:(j + 1) * 128],
                                    identity=ident_f[0:1, 0:1]), sT)
            t0 = val(sT)

            @blk.vector
            def _(v):
                v.wait_ge(sT, t0)
                c = inc(v.tensor_copy(out=silT[:], in_=pb[6][:, 0:NT]), sV)
                bank_guard[6] = (sV, c)
            v_silT = val(sV)

            mvrow_dma = 0
            for piece in range(2):
                mm_last = None
                for kt in range(NT):
                    @blk.sync
                    def _(sy, piece=piece, kt=kt):
                        sy.wait_ge(sDw, val(sDw))
                        sy.wait_ge(sV, val(sV))
                        inc(sy.dma_start(
                            out=wmstg[:],
                            in_=wmod_s[kt * 128:(kt + 1) * 128,
                                       piece * 3072:(piece + 1) * 3072]), sDw, 16)
                    d1 = val(sDw)

                    @blk.vector
                    def _(v, d1=d1, mm_prev=mm_last):
                        v.wait_ge(sDw, d1)
                        v.wait_ge(sV, v_silT)
                        if mm_prev is not None:
                            v.wait_ge(sT, mm_prev)
                        inc(v.tensor_copy(out=wmb[:], in_=wmstg[:]), sV)
                    v1 = val(sV)

                    @blk.tensor
                    def _(t, kt=kt, v1=v1):
                        t.wait_ge(sV, v1)
                        for fb in range(6):
                            if kt == 0:
                                bank_wait(t, fb)
                            inc(t.matmul(pb[fb][0:1, 0:512],
                                         lhsT=silT[:, kt:kt + 1],
                                         rhs=wmb[:, fb * 512:(fb + 1) * 512],
                                         start=(kt == 0), stop=(kt == NT - 1)), sT)
                    mm_last = val(sT)

                @blk.vector
                def _(v, mm_last=mm_last, mvrow_dma=mvrow_dma):
                    v.wait_ge(sT, mm_last)
                    v.wait_ge(sDo, mvrow_dma)
                    for fb in range(6):
                        c = inc(v.tensor_copy(out=mvrow[0:1, fb * 512:(fb + 1) * 512],
                                              in_=pb[fb][0:1, 0:512]), sV)
                        bank_guard[fb] = (sV, c)
                v2 = val(sV)

                @blk.sync
                def _(sy, piece=piece, v2=v2):
                    sy.wait_ge(sDo, val(sDo))
                    sy.wait_ge(sV, v2)
                    inc(sy.dma_start(out=mod_i[0:1, piece * 3072:(piece + 1) * 3072],
                                     in_=mvrow[:]), sDo, 16)
                mvrow_dma = val(sDo)
            p0b_d = val(sDo)

            @blk.gpsimd
            def _(g):
                g.wait_ge(sDo, p0b_d)
                inc(g.collective_compute("AllGather", BYP, replica_groups=PAIRS,
                                         ins=[mod_i.opt()], outs=[mod_o.opt()]), sG)
            g0 = val(sG)

            @blk.sync
            def _(sy):
                sy.wait_ge(sDx, val(sDx))
                sy.wait_ge(sG, g0)
                inc(sy.dma_start(out=modr[0:1, 0:6144], in_=mod_o[0:1, :]), sDx, 16)
                inc(sy.dma_start(out=modr[0:1, 6144:12288], in_=mod_o[1:2, :]), sDx, 16)
            d2 = val(sDx)

            @blk.tensor
            def _(t):
                t.wait_ge(sDx, d2)
                for j in range(96):
                    bank_wait(t, 6)
                    inc(t.transpose(pb[6][:, j:j + 1],
                                    in_=modr[0:1, j * 128:(j + 1) * 128],
                                    identity=ident_f[0:1, 0:1]), sT)
            t1 = val(sT)

            @blk.vector
            def _(v):
                v.wait_ge(sT, t1)
                c = inc(v.tensor_copy(out=modc[:], in_=pb[6][:, 0:96]), sV)
                bank_guard[6] = (sV, c)
                v.wait_ge(sV, c)
                inc(v.tensor_copy(out=sa_c[:], in_=modc[:, 0:NT]), sV)
                inc(v.tensor_scalar(out=ca1[:], in0=modc[:, NT:2 * NT], scalar1=1.0,
                                    scalar2=None, op0=ADD, op1=BYP), sV)
                inc(v.tensor_copy(out=sb_c[:], in_=modc[:, 3 * NT:4 * NT]), sV)
                inc(v.tensor_scalar(out=cb1[:], in0=modc[:, 4 * NT:5 * NT], scalar1=1.0,
                                    scalar2=None, op0=ADD, op1=BYP), sV)
                for src_o, dst in ((2 * NT, ga_c), (5 * NT, gb_c)):
                    g0ap = modc[:, src_o:src_o + NT]
                    v.wait_ge(sV, val(sV))
                    inc(v.tensor_scalar(out=tmpc[:, 0:NT], in0=g0ap, scalar1=0.0,
                                        scalar2=2.0, op0=AluOpType.is_ge, op1=MUL), sV)
                    v.wait_ge(sV, val(sV))
                    inc(v.tensor_scalar(out=tmpc[:, 0:NT], in0=tmpc[:, 0:NT],
                                        scalar1=1.0, scalar2=None, op0=SUB, op1=BYP),
                        sV)
                    v.wait_ge(sV, val(sV))
                    inc(v.tensor_tensor(out=tmpc[:, NT:2 * NT], in0=g0ap,
                                        in1=tmpc[:, 0:NT], op=MUL), sV)
                    v.wait_ge(sV, val(sV))
                    inc(v.tensor_scalar(out=tmpc[:, NT:2 * NT], in0=tmpc[:, NT:2 * NT],
                                        scalar1=0.25, scalar2=None, op0=MAXO, op1=BYP),
                        sV)
                    v.wait_ge(sV, val(sV))
                    inc(v.tensor_tensor(out=dst[:], in0=tmpc[:, NT:2 * NT],
                                        in1=tmpc[:, 0:NT], op=MUL), sV)
            p0b_mod_v = val(sV)

        with ExitStack() as sc1:
            qstg = sb("qstg", [128, H], scope=sc1)
            qwb = sb("qwb", [128, H], BF16, scope=sc1)
            qsum = sb("qsum", [128, 32], scope=sc1)
            qrec = sb("qrec", [128, 32], scope=sc1)
            qscb = sb("qscb", [128, 32], BF16, scope=sc1)
            qrt = sb("qrt", [128, H], BF16, scope=sc1)
            qq = sb("qq", [128, H], scope=sc1)
            qqb = sb("qqb", [128, H], BF16, scope=sc1)
            qwide = sb("qwide", [128, H], BF16, scope=sc1)

            def quant_tile(ncols, d_wait):
                ng = ncols // 64

                @blk.vector
                def _(v):
                    v.wait_ge(sDw, d_wait)
                    inc(v.tensor_copy(out=qwb[:, 0:ncols], in_=qstg[:, 0:ncols]), sV)
                    v.wait_ge(sV, val(sV))
                    inc(v.tensor_reduce(out=qsum[:, 0:ng],
                                        in_=qwb.ap()[:, 0:ncols].rearrange(
                                            "p (g k) -> p g k", k=64),
                                        axis=AX.X, op=ADD, apply_absolute_value=True),
                        sV)
                    v.wait_ge(sV, val(sV))
                    inc(v.tensor_scalar(out=qscb[:, 0:ng], in0=qsum[:, 0:ng],
                                        scalar1=1.0 / 64, scalar2=1e-8,
                                        op0=MUL, op1=MAXO), sV)
                    v.wait_ge(sV, val(sV))
                    inc(v.reciprocal(out=qrec[:, 0:ng], in_=qscb[:, 0:ng]), sV)
                    v.wait_ge(sV, val(sV))
                    inc(v.tensor_tensor(
                        out=qrt.ap()[:, 0:ncols].rearrange("p (g k) -> p g k", k=64),
                        in0=qwb.ap()[:, 0:ncols].rearrange("p (g k) -> p g k", k=64),
                        in1=qrec.ap()[:, 0:ng].unsqueeze(2).broadcast_to([128, ng, 64]),
                        op=MUL), sV)
                    v.wait_ge(sV, val(sV))
                    inc(v.tensor_scalar(out=qq[:, 0:ncols], in0=qrt[:, 0:ncols],
                                        scalar1=RINT_C, scalar2=RINT_C,
                                        op0=ADD, op1=SUB), sV)
                    v.wait_ge(sV, val(sV))
                    inc(v.tensor_scalar(out=qq[:, 0:ncols], in0=qq[:, 0:ncols],
                                        scalar1=1.0, scalar2=-1.0,
                                        op0=MINO, op1=MAXO), sV)
                    v.wait_ge(sV, val(sV))
                    inc(v.tensor_tensor(
                        out=qqb.ap()[:, 0:ncols].rearrange("p (g k) -> p g k", k=64),
                        in0=qq.ap()[:, 0:ncols].rearrange("p (g k) -> p g k", k=64),
                        in1=qscb.ap()[:, 0:ng].unsqueeze(2).broadcast_to([128, ng, 64]),
                        op=MUL), sV)
                return val(sV)

            qdma = 0
            for rt in range(8):
                @blk.sync
                def _(sy, rt=rt):
                    sy.wait_ge(sDw, val(sDw))
                    sy.wait_ge(sV, val(sV))
                    if rt == 0:
                        sy.wait_ge(sV, p0b_mod_v)
                    inc(sy.dma_start(out=qstg[:],
                                     in_=wfc_s[rt * 128:(rt + 1) * 128, :]), sDw, 16)
                vq = quant_tile(H, val(sDw))
                v2 = tp_blocks_bf16(qqb, H, qwide, wait_v=vq,
                                    extra_waits=[(sDo, qdma)])

                @blk.sync
                def _(sy, rt=rt, v2=v2):
                    sy.wait_ge(sDo, val(sDo))
                    sy.wait_ge(sV, v2)
                    inc(sy.dma_start(out=wfcT_i[rt * 128:(rt + 1) * 128, :],
                                     in_=qwide[:]), sDo, 16)
                qdma = val(sDo)

            for rt in range(NT):
                @blk.sync
                def _(sy, rt=rt):
                    sy.wait_ge(sDw, val(sDw))
                    sy.wait_ge(sV, val(sV))
                    inc(sy.dma_start(out=qstg[:, 0:1024],
                                     in_=wpj_s[rt * 128:(rt + 1) * 128, :]), sDw, 16)
                vq = quant_tile(1024, val(sDw))
                v2 = tp_blocks_bf16(qqb, 1024, qwide, wait_v=vq,
                                    extra_waits=[(sDo, qdma)])

                @blk.sync
                def _(sy, rt=rt, v2=v2):
                    sy.wait_ge(sDo, val(sDo))
                    sy.wait_ge(sV, v2)
                    inc(sy.dma_start(out=wpjT_i[rt * 128:(rt + 1) * 128, :],
                                     in_=qwide[:, 0:1024]), sDo, 16)
                qdma = val(sDo)
            p0c_d = val(sDo)

            @blk.gpsimd
            def _(g):
                g.wait_ge(sDo, p0c_d)
                inc(g.collective_compute("AllGather", BYP, replica_groups=ALLG,
                                         ins=[wfcT_i.opt()], outs=[wfcT_o.opt()]), sG)
                inc(g.collective_compute("AllGather", BYP, replica_groups=ALLG,
                                         ins=[wpjT_i.opt()], outs=[wpjT_o.opt()]), sG)
            p0c_g = val(sG)

        with ExitStack() as sc2:
            tstg = sb("tstg", [128, H], scope=sc2)
            tstg1 = sb("tstg1", [128, H], scope=sc2)
            tb16 = sb("tb16", [128, H], BF16, scope=sc2)
            tb161 = sb("tb161", [128, H], BF16, scope=sc2)
            twide = sb("twide", [128, H], BF16, scope=sc2)
            twide1 = sb("twide1", [128, H], BF16, scope=sc2)
            tstgs, tb16s, twides = [tstg, tstg1], [tb16, tb161], [twide, twide1]
            cts_i = [0]
            cast_cons = [0, 0]
            cast_done = [0, 0]

            sp_dma = [0, 0]
            p0c_entry = (val(sV), val(sDo))

            def cts_strips(src, nrt, cols, dst_sp):
                for rt in range(nrt):
                    p = cts_i[0] % 2
                    cts_i[0] += 1

                    @blk.sync
                    def _(sy, rt=rt, p=p, cc=cast_cons[p], cd=cast_done[p]):
                        sem = sDs[p]
                        sy.wait_ge(sem, val(sem))
                        sy.wait_ge(sV, cd)
                        sy.wait_ge(sT, cc)
                        sy.wait_ge(sDo, p0c_entry[1])
                        inc(sy.dma_start(out=tstgs[p][:, 0:cols],
                                         in_=src[rt * 128:(rt + 1) * 128, :]),
                            sem, 16)
                    d_here = val(sDs[p])

                    @blk.vector
                    def _(v, d_here=d_here, p=p):
                        v.wait_ge(sDs[p], d_here)
                        inc(v.tensor_copy(out=tb16s[p][:, 0:cols],
                                          in_=tstgs[p][:, 0:cols]), sV)
                    v1 = val(sV)
                    cast_done[p] = v1
                    v2 = tp_blocks_bf16(tb16s[p], cols, twides[p], wait_v=v1,
                                        extra_waits=[(sDo, sp_dma[p])])
                    cast_cons[p] = val(sT)

                    @blk.sync
                    def _(sy, rt=rt, v2=v2, p=p):
                        sy.wait_ge(sDo, val(sDo))
                        sy.wait_ge(sV, v2)
                        inc(sy.dma_start(out=dst_sp[rt * 128:(rt + 1) * 128, 0:cols],
                                         in_=twide1[:, 0:cols] if p else twide[:, 0:cols]),
                            sDo, 16)
                    sp_dma[p] = val(sDo)

            cts_strips(wis, NT, H, wisT_sp)
            cts_strips(wig, NT, H, wigT_sp)
            cts_strips(wout_h, NT, DH, woutT_sp)

            @blk.sync
            def _(sy):
                sy.wait_ge(sDw, val(sDw))
                sy.wait_ge(sV, val(sV))
                inc(sy.dma_start(out=tstg[:, 0:DH], in_=wdt), sDw, 16)
            d1 = val(sDw)

            @blk.vector
            def _(v, d1=d1):
                v.wait_ge(sDw, d1)
                inc(v.tensor_copy(out=tb16[:, 0:DH], in_=tstg[:, 0:DH]), sV)
            v1 = val(sV)
            tp_blocks_bf16(tb16, DH, wdtT, wait_v=v1)

            @blk.sync
            def _(sy):
                sy.wait_ge(sDw, val(sDw))
                sy.wait_ge(sV, val(sV))
                inc(sy.dma_start(out=tstg[0:2 * N, 0:DH], in_=wbc), sDw, 16)
            d2 = val(sDw)

            @blk.vector
            def _(v, d2=d2):
                v.wait_ge(sDw, d2)
                inc(v.tensor_copy(out=tb16[0:2 * N, 0:DH], in_=tstg[0:2 * N, 0:DH]),
                    sV)
            v2 = val(sV)

            @blk.tensor
            def _(t, v2=v2):
                t.wait_ge(sV, v2)
                for cb in range(NT):
                    bank_wait(t, 6 + cb % 2)
                    inc(t.transpose(pb[6 + cb % 2][:, 0:16].bitcast(BF16),
                                    in_=tb16[0:2 * N, cb * 128:(cb + 1) * 128],
                                    identity=ident_b[0:2 * N, 0:2 * N]), sT)
                    t_here = val(sT)

                    @blk.vector
                    def _(v, cb=cb, t_here=t_here):
                        v.wait_ge(sT, t_here)
                        c = inc(v.tensor_copy(
                            out=wbcT[:, cb * 32:(cb + 1) * 32],
                            in_=pb[6 + cb % 2][:, 0:16].bitcast(BF16)), sV)
                        bank_guard[6 + cb % 2] = (sV, c)

            @blk.sync
            def _(sy):
                sy.wait_ge(sDw, val(sDw))
                sy.wait_ge(sV, val(sV))
                inc(sy.dma_start(
                    out=tstg.ap()[:, 0:DH].rearrange("p (t r) -> p t r", r=128),
                    in_=wdp.rearrange("(t p) r -> p t r", p=128)), sDw, 16)
            d3 = val(sDw)

            @blk.vector
            def _(v, d3=d3):
                v.wait_ge(sDw, d3)
                inc(v.tensor_copy(out=tb16[:, 0:DH], in_=tstg[:, 0:DH]), sV)
            v3 = val(sV)
            tp_blocks_bf16(tb16, DH, wdpT, wait_v=v3)

            @blk.sync
            def _(sy):
                sy.wait_ge(sDw, val(sDw))
                sy.wait_ge(sV, val(sV))
                inc(sy.dma_start(
                    out=tstg.ap()[:, 0:NT * KC].rearrange("p (t k) -> p t k", k=KC),
                    in_=cw.rearrange("(t p) k -> p t k", p=128)), sDw, 16)
            d4 = val(sDw)

            @blk.vector
            def _(v, d4=d4):
                v.wait_ge(sDw, d4)
                for tt in range(NT):
                    for j in range(KC):
                        inc(v.tensor_scalar(
                            out=diagw[:, (tt * KC + j) * 128:(tt * KC + j + 1) * 128],
                            in0=ident_b[:],
                            scalar1=tstg[:, tt * KC + j:tt * KC + j + 1],
                            scalar2=None, op0=MUL, op1=BYP), sV)
            p0_diag_v = val(sV)

        with ExitStack() as sc3:
            h1 = sb("h1", [128, NT * L], BF16, scope=sc3)

            p0d_spill = val(sDo)
            p0d_tp = val(sT)
            prev_tp_t = 0
            prev_apply_v = 0
            for lt in range(NT):
                half = lt % 2

                @blk.sync
                def _(sy, lt=lt, prev_apply_v=prev_apply_v):
                    sy.wait_ge(sDx, val(sDx))
                    if lt > 0:
                        sy.wait_ge(sV, prev_apply_v)
                    inc(sy.dma_start(out=xs[:], in_=x_b[lt * 128:(lt + 1) * 128, :]),
                        sDx, 16)
                d_here = val(sDx)

                @blk.scalar
                def _(scl, d_here=d_here, half=half, prev_tp_t=prev_tp_t,
                      prev_apply_v=prev_apply_v):
                    scl.wait_ge(sDx, d_here)
                    scl.wait_ge(sT, prev_tp_t)
                    scl.wait_ge(sV, prev_apply_v)
                    inc(scl.activation(out=tbf2[:, half * H:(half + 1) * H],
                                       in_=xs[:], func=AF.Square,
                                       accum_out=colw[:, 0:1]), sS)
                    scl.wait_ge(sS, val(sS))
                    inc(scl.activation(out=tbf2[:, half * H:(half + 1) * H],
                                       in_=xs[:], func=AF.Copy,
                                       accum_out=colw[:, 1:2]), sS)
                s_here = val(sS)

                @blk.vector
                def _(v, s_here=s_here):
                    v.wait_ge(sS, s_here)
                    inc(v.tensor_scalar(out=colw[:, 2:3], in0=colw[:, 1:2],
                                        scalar1=1.0 / H, scalar2=None, op0=MUL,
                                        op1=BYP), sV)
                    v.wait_ge(sV, val(sV))
                    inc(v.tensor_tensor(out=colw[:, 3:4], in0=colw[:, 2:3],
                                        in1=colw[:, 2:3], op=MUL), sV)
                    v.wait_ge(sV, val(sV))
                    inc(v.scalar_tensor_tensor(out=colw[:, 4:5], in0=colw[:, 0:1],
                                               scalar=1.0 / H, in1=colw[:, 3:4],
                                               op0=MUL, op1=SUB), sV)
                    v.wait_ge(sV, val(sV))
                    inc(v.tensor_scalar(out=colw[:, 4:5], in0=colw[:, 4:5],
                                        scalar1=LN_EPS, scalar2=None,
                                        op0=ADD, op1=BYP), sV)
                v_here = val(sV)

                @blk.scalar
                def _(scl, v_here=v_here):
                    scl.wait_ge(sV, v_here)
                    inc(scl.activation(out=colw[:, 5:6], in_=colw[:, 4:5],
                                       func=AF.Sqrt), sS)
                s2 = val(sS)

                @blk.vector
                def _(v, s2=s2, half=half):
                    v.wait_ge(sS, s2)
                    inc(v.reciprocal(out=colw[:, 6:7], in_=colw[:, 5:6]), sV)
                    v.wait_ge(sV, val(sV))
                    inc(v.tensor_scalar(out=tbf2[:, half * H:(half + 1) * H],
                                        in0=xs[:], scalar1=colw[:, 2:3],
                                        scalar2=colw[:, 6:7], op0=SUB, op1=MUL), sV)
                prev_apply_v = val(sV)

                if half == 1:
                    @blk.tensor
                    def _(t, lt=lt, prev_apply_v=prev_apply_v):
                        t.wait_ge(sV, prev_apply_v)
                        for hb in range(NT):
                            bk = hb % 8
                            bank_wait(t, bk)
                            for j in range(2):
                                inc(t.transpose(
                                    pb[bk][:, j * 128:(j + 1) * 128],
                                    in_=tbf2[:, j * H + hb * 128:
                                             j * H + (hb + 1) * 128],
                                    identity=ident_f[:]), sT)
                            t_here = val(sT)

                            @blk.scalar
                            def _(scl, lt=lt, hb=hb, bk=bk, t_here=t_here):
                                scl.wait_ge(sT, t_here)
                                if lt == 1 and hb == 0:
                                    scl.wait_ge(sV, p0b_mod_v)
                                    scl.wait_ge(sDo, p0d_spill)
                                c = inc(scl.activation(
                                    out=h1[:, hb * L + (lt - 1) * 128:
                                           hb * L + (lt + 1) * 128],
                                    in_=pb[bk][:, 0:256], func=AF.Identity,
                                    scale=ca1[:, hb:hb + 1],
                                    bias=sa_c[:, hb:hb + 1]), sS)
                                bank_guard[bk] = (sS, c)
                    prev_tp_t = val(sT)
            p1_s = val(sS)

            wstg = [sb("wstg0", [128, H], BF16, scope=sc3),
                    sb("wstg1", [128, H], BF16, scope=sc3)]
            gstg = [sb("gstg0", [128, H], BF16, scope=sc3),
                    sb("gstg1", [128, H], BF16, scope=sc3)]
            hpre = sb("hpre", [128, 3 + L], BF16, scope=sc3)
            hst = [sb("hst0", [128, L], BF16, scope=sc3),
                   sb("hst1", [128, L], BF16, scope=sc3)]
            hsf = [sb("hsf0", [128, 512], scope=sc3),
                   sb("hsf1", [128, 512], scope=sc3)]
            sgbb = [sb("sgbb0", [128, 512], BF16, scope=sc3),
                    sb("sgbb1", [128, 512], BF16, scope=sc3)]
            tsw = sb("tsw", [128, L], scope=sc3)
            bcw = sb("bcw", [32, L], scope=sc3)

            @blk.vector
            def _(v):
                v.wait_ge(sV, p0_diag_v)
                v.wait_ge(sDo, p0d_spill)
                inc(v.memset(hpre[:, 0:3], 0.0), sV)
                inc(v.memset(tsw[:], 0.0), sV)
                inc(v.memset(bcw[:], 0.0), sV)
            p2_pad_v = val(sV)

            w_cons = [0, 0]
            prev_conv_t = 0
            hs_dma = [0, 0]
            sil_dma = [0, 0]
            hst_cons = [0, 0]
            tsbc_add_v = 0
            for dt_i in range(NT):
                @blk.sync
                def _(sy, dt_i=dt_i):
                    sy.wait_ge(sDw, val(sDw))
                    sy.wait_ge(sT, w_cons[dt_i % 2])
                    if dt_i == 0:
                        sy.wait_ge(sDo, p0d_spill)
                    inc(sy.dma_start(out=wstg[dt_i % 2][:],
                                     in_=wisT_sp[dt_i * 128:(dt_i + 1) * 128, :]),
                        sDw, 16)
                    inc(sy.dma_start(out=gstg[dt_i % 2][:],
                                     in_=wigT_sp[dt_i * 128:(dt_i + 1) * 128, :]),
                        sDw, 16)
                d_here = val(sDw)
                w0, w1 = wstg[dt_i % 2], gstg[dt_i % 2]

                for fi in range(4):
                    @blk.tensor
                    def _(t, dt_i=dt_i, fi=fi, d_here=d_here):
                        t.wait_ge(sDw, d_here)
                        if dt_i == 0 and fi == 0:
                            t.wait_ge(sS, p1_s)
                        bank_wait(t, fi % 2)
                        for kb in range(NT):
                            inc(t.matmul(
                                pb[fi % 2][:, 0:512],
                                lhsT=w0[:, kb * 128:(kb + 1) * 128],
                                rhs=h1[:, kb * L + fi * 512:kb * L + (fi + 1) * 512],
                                start=(kb == 0), stop=(kb == NT - 1)), sT)
                    t_here = val(sT)

                    @blk.scalar
                    def _(scl, dt_i=dt_i, fi=fi, t_here=t_here,
                          prev_conv_t=prev_conv_t):
                        scl.wait_ge(sT, t_here)
                        scl.wait_ge(sT, prev_conv_t)
                        if dt_i == 0 and fi == 0:
                            scl.wait_ge(sV, p2_pad_v)
                            scl.wait_ge(sDo, p0d_spill)
                        c = inc(scl.activation(
                            out=hpre[:, 3 + fi * 512:3 + (fi + 1) * 512],
                            in_=pb[fi % 2][:, 0:512], func=AF.Copy), sS)
                        bank_guard[fi % 2] = (sS, c)
                p2_hpre_s = val(sS)

                for fi in range(4):
                    @blk.tensor
                    def _(t, dt_i=dt_i, fi=fi, p2_hpre_s=p2_hpre_s):
                        t.wait_ge(sS, p2_hpre_s)
                        if dt_i == 0 and fi == 0:
                            t.wait_ge(sV, p2_pad_v)
                        bank_wait(t, 2 + fi % 2)
                        for j in range(KC):
                            inc(t.matmul(
                                pb[2 + fi % 2][:, 0:512],
                                lhsT=diagw[:, (dt_i * KC + j) * 128:
                                           (dt_i * KC + j + 1) * 128],
                                rhs=hpre[:, fi * 512 + j:fi * 512 + j + 512],
                                start=(j == 0), stop=(j == KC - 1)), sT)
                    t_here = val(sT)
                    if fi == 3:
                        prev_conv_t = t_here

                    @blk.scalar
                    def _(scl, dt_i=dt_i, fi=fi, t_here=t_here, hsd=hs_dma[fi % 2],
                          hc=hst_cons[dt_i % 2]):
                        scl.wait_ge(sT, t_here)
                        scl.wait_ge(sDp[fi % 2], hsd)
                        scl.wait_ge(sT, hc)
                        inc(scl.activation(out=hsf[fi % 2][:, 0:512],
                                           in_=pb[2 + fi % 2][:, 0:512],
                                           func=AF.Silu, bias=cbcol[:, dt_i:dt_i + 1]),
                            sS)
                        scl.wait_ge(sS, val(sS))
                        c = inc(scl.activation(
                            out=hst[dt_i % 2][:, fi * 512:(fi + 1) * 512],
                            in_=pb[2 + fi % 2][:, 0:512],
                            func=AF.Silu, bias=cbcol[:, dt_i:dt_i + 1]), sS)
                        bank_guard[2 + fi % 2] = (sS, c)
                    s_here = val(sS)

                    @blk.sync
                    def _(sy, dt_i=dt_i, fi=fi, s_here=s_here):
                        sem = sDp[fi % 2]
                        sy.wait_ge(sem, val(sem))
                        sy.wait_ge(sS, s_here)
                        inc(sy.dma_start(
                            out=hs_sp[dt_i * 128:(dt_i + 1) * 128,
                                      fi * 512:(fi + 1) * 512],
                            in_=hsf[fi % 2][:, 0:512]), sem, 16)
                    hs_dma[fi % 2] = val(sDp[fi % 2])
                p2_hst_s = val(sS)

                for fi in range(4):
                    @blk.tensor
                    def _(t, dt_i=dt_i, fi=fi, d_here=d_here):
                        t.wait_ge(sDw, d_here)
                        bank_wait(t, 4 + fi % 2)
                        for kb in range(NT):
                            inc(t.matmul(
                                pb[4 + fi % 2][:, 0:512],
                                lhsT=w1[:, kb * 128:(kb + 1) * 128],
                                rhs=h1[:, kb * L + fi * 512:kb * L + (fi + 1) * 512],
                                start=(kb == 0), stop=(kb == NT - 1)), sT)
                    t_here = val(sT)
                    if fi == 3:
                        w_cons[dt_i % 2] = t_here

                    @blk.scalar
                    def _(scl, fi=fi, t_here=t_here, sid=sil_dma[fi % 2]):
                        scl.wait_ge(sT, t_here)
                        scl.wait_ge(sDq[fi % 2], sid)
                        c = inc(scl.activation(out=sgbb[fi % 2][:, 0:512],
                                               in_=pb[4 + fi % 2][:, 0:512],
                                               func=AF.Silu), sS)
                        bank_guard[4 + fi % 2] = (sS, c)
                    s_here = val(sS)

                    @blk.sync
                    def _(sy, dt_i=dt_i, fi=fi, s_here=s_here):
                        sem = sDq[fi % 2]
                        sy.wait_ge(sem, val(sem))
                        sy.wait_ge(sS, s_here)
                        inc(sy.dma_start(
                            out=silu_sp[dt_i * 128:(dt_i + 1) * 128,
                                        fi * 512:(fi + 1) * 512],
                            in_=sgbb[fi % 2][:, 0:512]), sem, 16)
                    sil_dma[fi % 2] = val(sDq[fi % 2])

                for fi in range(4):
                    @blk.tensor
                    def _(t, dt_i=dt_i, fi=fi, p2_hst_s=p2_hst_s):
                        t.wait_ge(sS, p2_hst_s)
                        bank_wait(t, 6 + fi % 2)
                        inc(t.matmul(pb[6 + fi % 2][:, 0:512],
                                     lhsT=wdtT[:, dt_i * 128:(dt_i + 1) * 128],
                                     rhs=hst[dt_i % 2][:, fi * 512:(fi + 1) * 512],
                                     start=True, stop=True), sT)
                    t_here = val(sT)

                    @blk.vector
                    def _(v, fi=fi, t_here=t_here):
                        v.wait_ge(sT, t_here)
                        c = inc(v.tensor_tensor(out=tsw[:, fi * 512:(fi + 1) * 512],
                                                in0=tsw[:, fi * 512:(fi + 1) * 512],
                                                in1=pb[6 + fi % 2][:, 0:512],
                                                op=ADD), sV)
                        bank_guard[6 + fi % 2] = (sV, c)
                for fi in range(4):
                    @blk.tensor
                    def _(t, dt_i=dt_i, fi=fi):
                        bank_wait(t, 6 + fi % 2)
                        inc(t.matmul(pb[6 + fi % 2][0:32, 0:512],
                                     lhsT=wbcT[:, dt_i * 32:(dt_i + 1) * 32],
                                     rhs=hst[dt_i % 2][:, fi * 512:(fi + 1) * 512],
                                     start=True, stop=True), sT)
                    t_here = val(sT)

                    @blk.vector
                    def _(v, fi=fi, t_here=t_here):
                        v.wait_ge(sT, t_here)
                        c = inc(v.tensor_tensor(out=bcw[:, fi * 512:(fi + 1) * 512],
                                                in0=bcw[:, fi * 512:(fi + 1) * 512],
                                                in1=pb[6 + fi % 2][0:32, 0:512],
                                                op=ADD), sV)
                        bank_guard[6 + fi % 2] = (sV, c)
                hst_cons[dt_i % 2] = val(sT)
                tsbc_add_v = val(sV)
            p2_s = val(sS)

            @blk.sync
            def _(sy):
                sy.wait_ge(sDo, val(sDo))
                sy.wait_ge(sV, tsbc_add_v)
                inc(sy.dma_start(out=dtbc_i[0:R, :], in_=tsw[:]), sDo, 16)
                inc(sy.dma_start(out=dtbc_i[R:R + 2 * N, :], in_=bcw[:]), sDo, 16)
            p2b_d = val(sDo)

            @blk.gpsimd
            def _(g):
                g.wait_ge(sDo, p2b_d)
                inc(g.collective_compute("AllReduce", ADD, replica_groups=PAIRS,
                                         ins=[dtbc_i.opt()], outs=[dtbc_o.opt()]), sG)
            p2b_g = val(sG)

            @blk.sync
            def _(sy):
                sy.wait_ge(sDx, val(sDx))
                sy.wait_ge(sG, p2b_g)
                inc(sy.dma_start(out=tsw[:], in_=dtbc_o[0:R, :]), sDx, 16)
                inc(sy.dma_start(out=bcw[:], in_=dtbc_o[R:R + 2 * N, :]), sDx, 16)
            p2b_d2 = val(sDx)

            @blk.vector
            def _(v):
                v.wait_ge(sDx, p2b_d2)
                inc(v.tensor_copy(out=ts_b[:], in_=tsw[:]), sV)
                inc(v.tensor_copy(out=bcb[:], in_=bcw[:]), sV)
            p2b_v2 = val(sV)

            p2b_d3 = p2b_d2

            zb = [sb("zb0", [128, L], scope=sc3), sb("zb1", [128, L], scope=sc3)]
            az = [sb("az0", [128, L], scope=sc3), sb("az1", [128, L], scope=sc3)]

            zb_dma = [0, 0]
            for dt_i in range(NT):
                pz = dt_i % 2
                for fi in range(4):
                    @blk.tensor
                    def _(t, dt_i=dt_i, fi=fi):
                        if dt_i == 0 and fi == 0:
                            t.wait_ge(sV, p2b_v2)
                        bank_wait(t, fi % 2)
                        inc(t.matmul(pb[fi % 2][:, 0:512],
                                     lhsT=wdpT[:, dt_i * 128:(dt_i + 1) * 128],
                                     rhs=ts_b[:, fi * 512:(fi + 1) * 512],
                                     start=True, stop=True), sT)
                    t_here = val(sT)

                    @blk.scalar
                    def _(scl, dt_i=dt_i, fi=fi, t_here=t_here, pz=pz,
                          zdp=zb_dma[dt_i % 2]):
                        scl.wait_ge(sT, t_here)
                        scl.wait_ge(sDo, zdp)
                        c = inc(scl.activation(out=zb[pz][:, fi * 512:(fi + 1) * 512],
                                               in_=pb[fi % 2][:, 0:512], func=AF.Identity,
                                               bias=bdcol[:, dt_i:dt_i + 1]), sS)
                        bank_guard[fi % 2] = (sS, c)
                s_here = val(sS)

                @blk.vector
                def _(v, s_here=s_here, pz=pz):
                    v.wait_ge(sS, s_here)
                    inc(v.tensor_scalar(out=az[pz][:], in0=zb[pz][:], scalar1=-1.0,
                                        scalar2=None, op0=MUL, op1=BYP), sV)
                    v.wait_ge(sV, val(sV))
                    inc(v.tensor_tensor(out=az[pz][:], in0=az[pz][:], in1=zb[pz][:],
                                        op=MAXO), sV)
                v2 = val(sV)

                @blk.scalar
                def _(scl, v2=v2, pz=pz):
                    scl.wait_ge(sV, v2)
                    inc(scl.activation(out=az[pz][:], in_=az[pz][:], func=AF.Exp,
                                       scale=-1.0), sS)
                    scl.wait_ge(sS, val(sS))
                    inc(scl.activation(out=az[pz][:], in_=az[pz][:], func=AF.Ln,
                                       bias=1.0), sS)
                s3 = val(sS)

                @blk.vector
                def _(v, s3=s3, pz=pz):
                    v.wait_ge(sS, s3)
                    inc(v.scalar_tensor_tensor(out=zb[pz][:], in0=zb[pz][:], scalar=0.0,
                                               in1=az[pz][:], op0=MAXO, op1=ADD), sV)
                v3 = val(sV)

                @blk.sync
                def _(sy, dt_i=dt_i, v3=v3, pz=pz):
                    sy.wait_ge(sDo, val(sDo))
                    sy.wait_ge(sV, v3)
                    inc(sy.dma_start(out=dt_sp[dt_i * 128:(dt_i + 1) * 128, :],
                                     in_=zb[pz][:]), sDo, 16)
                zb_dma[pz] = val(sDo)
            p3_d = val(sDo)

        with ExitStack() as sc4:
            sc4a = st.enter_context(ExitStack())
            s_int = sb("s_int", [128, LC * N], BF16, scope=sc4a)
            b_bc = sb("b_bc", [128, N * LC], BF16, scope=sc4a)
            c_bc = sb("c_bc", [128, LC * N], BF16, scope=sc4a)
            dAb = [sb("dA0", [128, LC], scope=sc4a), sb("dA1", [128, LC], scope=sc4a)]
            dtc = [sb("dtc0", [128, LC], scope=sc4a), sb("dtc1", [128, LC], scope=sc4a)]
            hsc = [sb("hsc0", [128, LC], scope=sc4a), sb("hsc1", [128, LC], scope=sc4a)]
            sic = [sb("sic0", [128, LC], BF16, scope=sc4a),
                   sb("sic1", [128, LC], BF16, scope=sc4a)]
            u_b = sb("u_b", [128, LC], BF16, scope=sc4a)
            dbu = sb("dbu", [128, LC], BF16, scope=sc4a)
            ych = sb("ych", [128, LC], scope=sc4a)
            y3c = [sb("y3c0", [128, LC], BF16, scope=sc4a),
                   sb("y3c1", [128, LC], BF16, scope=sc4a)]

            bstage = sb("bstage", [1, LC], BF16, scope=sc4a)
            cstage = sb("cstage", [1, LC], BF16, scope=sc4a)

            p2_spills = [(sDp[0], val(sDp[0])), (sDp[1], val(sDp[1])),
                         (sDq[0], val(sDq[0])), (sDq[1], val(sDq[1]))]
            IDX = [(c_, d_) for c_ in range(2) for d_ in range(NT)]
            pcons = {0: (0, 0), 1: (0, 0)}
            y3dma = {0: 0, 1: 0}
            loads = {}

            def p4_load(i):
                ch_, dt_ = IDX[i]
                p = i % 2
                sem = sDs[p]
                i_ = i

                @blk.sync
                def _(sy, ch_=ch_, dt_=dt_, p=p, sem=sem, pc=pcons[p], i=i):
                    sy.wait_ge(sem, val(sem))
                    if i < 2:
                        sy.wait_ge(sDo, p3_d)
                        for s_, c_ in p2_spills:
                            sy.wait_ge(s_, c_)
                    sy.wait_ge(sV, pc[0])
                    sy.wait_ge(sS, pc[1])
                    inc(sy.dma_start(out=dtc[p][:],
                                     in_=dt_sp[dt_ * 128:(dt_ + 1) * 128,
                                               ch_ * LC:(ch_ + 1) * LC]), sem, 16)
                    inc(sy.dma_start(out=hsc[p][:],
                                     in_=hs_sp[dt_ * 128:(dt_ + 1) * 128,
                                               ch_ * LC:(ch_ + 1) * LC]), sem, 16)
                    inc(sy.dma_start(out=sic[p][:],
                                     in_=silu_sp[dt_ * 128:(dt_ + 1) * 128,
                                                 ch_ * LC:(ch_ + 1) * LC]), sem, 16)
                loads[i] = val(sDs[i % 2])

            bcv = {}
            prev_tts = None
            p4_load(0)
            for ch in range(2):
                chunk_entry_v = val(sV)
                bst_mm = 0
                for n in range(N):
                    @blk.sync
                    def _(sy, ch=ch, n=n, bst_mm=bst_mm, chunk_entry_v=chunk_entry_v):
                        sy.wait_ge(sDx, val(sDx))
                        sy.wait_ge(sV, p2b_v2)
                        sy.wait_ge(sT, bst_mm)
                        if n == 0:
                            sy.wait_ge(sV, chunk_entry_v)
                        inc(sy.dma_start(out=bstage[:],
                                         in_=bcb[n:n + 1, ch * LC:(ch + 1) * LC]),
                            sDx, 16)
                        inc(sy.dma_start(out=cstage[:],
                                         in_=bcb[N + n:N + n + 1,
                                                 ch * LC:(ch + 1) * LC]), sDx, 16)
                    d_here = val(sDx)

                    @blk.tensor
                    def _(t, n=n, d_here=d_here):
                        t.wait_ge(sDx, d_here)
                        for fb in range(LC // 512):
                            bk = (n * 2 + fb) % 4
                            bank_wait(t, bk)
                            inc(t.matmul(pb[bk][:, 0:512], lhsT=ones_rb[:],
                                         rhs=bstage[0:1, fb * 512:(fb + 1) * 512],
                                         start=True, stop=True), sT)
                            t_here = val(sT)

                            @blk.vector
                            def _(v, n=n, fb=fb, bk=bk, t_here=t_here, ch=ch):
                                v.wait_ge(sT, t_here)
                                if ch == 0 and n == 0 and fb == 0:
                                    v.wait_ge(sDo, p3_d)
                                c = inc(v.tensor_copy(
                                    out=b_bc[:, n * LC + fb * 512:
                                             n * LC + (fb + 1) * 512],
                                    in_=pb[bk][:, 0:512]), sV)
                                bank_guard[bk] = (sV, c)
                        for fb in range(LC // 512):
                            bk = 4 + (n * 2 + fb) % 2
                            bank_wait(t, bk)
                            inc(t.matmul(pb[bk][:, 0:512], lhsT=ones_rb[:],
                                         rhs=cstage[0:1, fb * 512:(fb + 1) * 512],
                                         start=True, stop=True), sT)
                            t_here = val(sT)

                            @blk.vector
                            def _(v, n=n, fb=fb, bk=bk, t_here=t_here):
                                v.wait_ge(sT, t_here)
                                c = inc(v.tensor_copy(
                                    out=c_bc[:, n * LC + fb * 512:
                                             n * LC + (fb + 1) * 512],
                                    in_=pb[bk][:, 0:512]), sV)
                                bank_guard[bk] = (sV, c)
                    bst_mm = val(sT)
                bcv[ch] = val(sV)

                for dt_i in range(NT):
                    i = ch * NT + dt_i
                    if i + 1 < 2 * NT:
                        p4_load(i + 1)
                    p = i % 2
                    sem = sDs[p]
                    d_here = loads[i]

                    @blk.vector
                    def _(v, p=p, sem=sem, d_here=d_here, bcv_ch=bcv[ch]):
                        v.wait_ge(sem, d_here)
                        v.wait_ge(sV, bcv_ch)
                        inc(v.tensor_tensor(out=u_b[:], in0=dtc[p][:], in1=hsc[p][:],
                                            op=MUL), sV)
                    p4_u_v = val(sV)

                    tts_c = {}
                    for n in range(N):
                        @blk.scalar
                        def _(scl, dt_i=dt_i, n=n, p=p, sem=sem, d_here=d_here,
                              tts_c=tts_c, i=i, ptts=prev_tts):
                            scl.wait_ge(sem, d_here)
                            if i == 0 and n == 0:
                                scl.wait_ge(sDo, p3_d)
                            if n >= 2:
                                scl.wait_ge(sV, tts_c[n - 2])
                            elif ptts is not None:
                                scl.wait_ge(sV, ptts[N - 2 + n])
                            inc(scl.activation(
                                out=dAb[n % 2][:], in_=dtc[p][:], func=AF.Exp,
                                scale=acol[:, dt_i * N + n:dt_i * N + n + 1]), sS)
                        s_dA = val(sS)

                        @blk.vector
                        def _(v, ch=ch, dt_i=dt_i, n=n, p4_u_v=p4_u_v, s_dA=s_dA,
                              tts_c=tts_c):
                            v.wait_ge(sV, p4_u_v)
                            v.wait_ge(sS, s_dA)
                            if n > 0:
                                v.wait_ge(sV, tts_c[n - 1])
                            inc(v.tensor_tensor(out=dbu[:], in0=u_b[:],
                                                in1=b_bc[:, n * LC:(n + 1) * LC],
                                                op=MUL), sV)
                            v.wait_ge(sV, val(sV))
                            init = (0.0 if ch == 0
                                    else state[:, dt_i * N + n:dt_i * N + n + 1])
                            inc(v.tensor_tensor_scan(
                                out=s_int[:, n * LC:(n + 1) * LC],
                                data0=dAb[n % 2][:], data1=dbu[:], initial=init,
                                op0=MUL, op1=ADD), sV)
                        tts_c[n] = val(sV)

                    @blk.vector
                    def _(v, ch=ch, dt_i=dt_i, p=p, tts_c=tts_c, y3p=y3dma[p]):
                        v.wait_ge(sV, tts_c[N - 1])
                        if ch == 0:
                            sview = s_int.ap().rearrange("p (n l) -> p n l", n=N)
                            inc(v.tensor_copy(
                                out=state.ap()[:, dt_i * N:(dt_i + 1) * N],
                                in_=sview[:, :, LC - 1:LC]), sV)
                            v.wait_ge(sV, val(sV))
                        inc(v.tensor_tensor(out=s_int[:], in0=s_int[:], in1=c_bc[:],
                                            op=MUL), sV)
                        v.wait_ge(sV, val(sV))
                        inc(v.tensor_tensor(out=ych[:], in0=s_int[:, 0:LC],
                                            in1=s_int[:, LC:2 * LC], op=ADD), sV)
                        for n in range(2, N):
                            v.wait_ge(sV, val(sV))
                            inc(v.tensor_tensor(out=ych[:], in0=ych[:],
                                                in1=s_int[:, n * LC:(n + 1) * LC],
                                                op=ADD), sV)
                        v.wait_ge(sV, val(sV))
                        inc(v.scalar_tensor_tensor(out=ych[:], in0=hsc[p][:],
                                                   scalar=dcol[:, dt_i:dt_i + 1],
                                                   in1=ych[:], op0=MUL, op1=ADD), sV)
                        v.wait_ge(sV, val(sV))
                        v.wait_ge(sDo, y3p)
                        inc(v.tensor_tensor(out=y3c[p][:], in0=ych[:], in1=sic[p][:],
                                            op=MUL), sV)
                    y3v = val(sV)
                    pcons[p] = (y3v, val(sS))
                    prev_tts = dict(tts_c)

                    @blk.sync
                    def _(sy, ch=ch, dt_i=dt_i, p=p, y3v=y3v):
                        sy.wait_ge(sDo, val(sDo))
                        sy.wait_ge(sV, y3v)
                        inc(sy.dma_start(
                            out=y3_sp[dt_i * 128:(dt_i + 1) * 128,
                                      ch * LC:(ch + 1) * LC],
                            in_=y3c[p][:]), sDo, 16)
                    y3dma[p] = val(sDo)
            p4_v = val(sV)
            p4_y3_d = val(sDo)

            sc4a.close()
            opw = [[sb(f"opw{h}_{j}", [128, H], BF16, scope=sc4)
                    for j in range(2)] for h in range(2)]
            opwsem = [sDw, sDx]
            oev = [sb("oev0", [128, L], scope=sc4),
                   sb("oev1", [128, L], scope=sc4)]
            yst = [sb(f"yst{j}", [128, L], BF16, scope=sc4) for j in range(3)]
            ksem5 = [sDs[0], sDs[1], sDp[0]]

            prev_grp_t = 0
            opw_cons = [0, 0]
            oev_dma = [0, 0]
            yst_cons = [0, 0, 0]
            for grp in range(8):
                par = grp % 2

                @blk.sync
                def _(sy, grp=grp, par=par, oc=opw_cons[par]):
                    if grp == 0:
                        sy.wait_ge(sDo, p4_y3_d)
                    for hh in range(2):
                        sem = opwsem[hh]
                        sy.wait_ge(sem, val(sem))
                        sy.wait_ge(sT, oc)
                        inc(sy.dma_start(
                            out=opw[hh][par][:],
                            in_=woutT_sp[(2 * grp + hh) * 128:
                                         (2 * grp + hh + 1) * 128, :]), sem, 16)
                d_here2 = [(opwsem[0], val(opwsem[0])),
                           (opwsem[1], val(opwsem[1]))]

                for kb in range(NT):
                    sl = kb % 3

                    @blk.sync
                    def _(sy, grp=grp, kb=kb, sl=sl):
                        sem = ksem5[sl]
                        sy.wait_ge(sem, val(sem))
                        sy.wait_ge(sT, yst_cons[sl])
                        if grp == 0 and kb < 4:
                            sy.wait_ge(sDo, p4_y3_d)
                        inc(sy.dma_start(out=yst[sl][:],
                                         in_=y3_sp[kb * 128:(kb + 1) * 128, :]),
                            sem, 16)
                    dk_here = val(ksem5[sl])

                    @blk.tensor
                    def _(t, grp=grp, kb=kb, sl=sl, par=par, d_here2=d_here2,
                          dk_here=dk_here):
                        for s_, c_ in d_here2:
                            t.wait_ge(s_, c_)
                        t.wait_ge(ksem5[sl], dk_here)
                        if grp == 0 and kb == 0:
                            t.wait_ge(sV, p4_v)
                        if kb == 0:
                            for i in range(8):
                                bank_wait(t, i)
                        for hh in range(2):
                            for fi in range(4):
                                inc(t.matmul(
                                    pb[hh * 4 + fi][:, 0:512],
                                    lhsT=opw[hh][par][:, kb * 128:(kb + 1) * 128],
                                    rhs=yst[sl][:, fi * 512:(fi + 1) * 512],
                                    start=(kb == 0), stop=(kb == NT - 1)), sT)
                    yst_cons[sl] = val(sT)
                prev_grp_t = val(sT)
                opw_cons[par] = prev_grp_t
                mm_end = prev_grp_t

                for hh in range(2):
                    @blk.vector
                    def _(v, hh=hh, mm_end=mm_end, od=oev_dma[hh], grp=grp):
                        v.wait_ge(sT, mm_end)
                        v.wait_ge(sDp[hh], od)
                        if grp == 0:
                            v.wait_ge(sDo, p4_y3_d)
                        for fi in range(4):
                            c = inc(v.tensor_copy(
                                out=oev[hh][:, fi * 512:(fi + 1) * 512],
                                in_=pb[hh * 4 + fi][:, 0:512]), sV)
                            bank_guard[hh * 4 + fi] = (sV, c)
                    v_here = val(sV)

                    @blk.sync
                    def _(sy, grp=grp, hh=hh, v_here=v_here):
                        sem = sDp[hh]
                        sy.wait_ge(sem, val(sem))
                        sy.wait_ge(sV, v_here)
                        ht = grp * 2 + hh
                        inc(sy.dma_start(out=op_i[ht * 128:(ht + 1) * 128, :],
                                         in_=oev[hh][:, 0:LH]), sem, 16)
                        inc(sy.dma_start(out=op_i[H + ht * 128:H + (ht + 1) * 128, :],
                                         in_=oev[hh][:, LH:L]), sem, 16)
                    oev_dma[hh] = val(sDp[hh])
            p5a_d = [(sDp[0], val(sDp[0])), (sDp[1], val(sDp[1]))]

            @blk.gpsimd
            def _(g):
                for s_, c_ in p5a_d:
                    g.wait_ge(s_, c_)
                inc(g.collective_compute("ReduceScatter", ADD, replica_groups=PAIRS,
                                         ins=[op_i.opt()], outs=[op_o.opt()]), sG)
            p5a_g = val(sG)

        with ExitStack() as sc5:
            x2 = sb("x2", [128, NT * LH], scope=sc5)
            xs2 = tbf2[:, 0:H]
            att = xs[:, 0:LH]

            prev_xt = 0
            for lt in range(8):
                @blk.sync
                def _(sy, lt=lt, prev_xt=prev_xt):
                    sy.wait_ge(sDx, val(sDx))
                    sy.wait_ge(sT, prev_xt)
                    if lt == 0:
                        for s_, c_ in p5a_d:
                            sy.wait_ge(s_, c_)
                    inc(sy.dma_start(out=xs2[:], in_=x_h[lt * 128:(lt + 1) * 128, :]),
                        sDx, 16)
                d_here = val(sDx)

                @blk.tensor
                def _(t, lt=lt, d_here=d_here):
                    t.wait_ge(sDx, d_here)
                    for hb in range(NT):
                        bank_wait(t, hb % 2)
                        inc(t.transpose(pb[hb % 2][:, 0:128],
                                        in_=xs2[:, hb * 128:(hb + 1) * 128],
                                        identity=ident_f[:]), sT)
                        t_here = val(sT)

                        @blk.vector
                        def _(v, lt=lt, hb=hb, t_here=t_here):
                            v.wait_ge(sT, t_here)
                            if lt == 0 and hb == 0:
                                for s_, c_ in p5a_d:
                                    v.wait_ge(s_, c_)
                            c = inc(v.tensor_copy(
                                out=x2[:, hb * LH + lt * 128:hb * LH + (lt + 1) * 128],
                                in_=pb[hb % 2][:, 0:128]), sV)
                            bank_guard[hb % 2] = (sV, c)
                prev_xt = val(sT)
            p5b_v = val(sV)

            attb = [att, xs[:, LH:2 * LH]]
            attsem = [sDx, sDo]
            stt_done = {}
            for ht in range(NT):
                pr = ht % 2

                @blk.sync
                def _(sy, ht=ht, pr=pr):
                    sem = attsem[pr]
                    sy.wait_ge(sem, val(sem))
                    if ht < 2:
                        sy.wait_ge(sG, p5a_g)
                    else:
                        sy.wait_ge(sV, stt_done[ht - 2])
                    inc(sy.dma_start(out=attb[pr][:],
                                     in_=op_o[ht * 128:(ht + 1) * 128, :]), sem, 16)
                d_here = val(attsem[pr])

                @blk.vector
                def _(v, ht=ht, pr=pr, d_here=d_here):
                    v.wait_ge(attsem[pr], d_here)
                    v.wait_ge(sV, p5b_v)
                    inc(v.scalar_tensor_tensor(
                        out=x2[:, ht * LH:(ht + 1) * LH], in0=attb[pr][:],
                        scalar=ga_c[:, ht:ht + 1],
                        in1=x2[:, ht * LH:(ht + 1) * LH], op0=MUL, op1=ADD), sV)
                stt_done[ht] = val(sV)
            p5b_v2 = val(sV)

            h2 = sb("h2", [128, NT * LH], BF16, scope=sc5)
            mrow = sb("mrow", [1, LH], scope=sc5)
            rrow = sb("rrow", [1, LH], scope=sc5)
            m_bc = sb("m_bc", [128, LH], scope=sc5)
            r_bc = sb("r_bc", [128, LH], scope=sc5)
            tb2 = sb("tb2", [128, LH], scope=sc5)
            sqf = tb2

            stat_t = 0
            for ht in range(NT):
                @blk.scalar
                def _(scl, ht=ht, stat_t=stat_t, sd=stt_done[ht]):
                    scl.wait_ge(sV, sd)
                    scl.wait_ge(sT, stat_t)
                    inc(scl.activation(out=sqf[:], in_=x2[:, ht * LH:(ht + 1) * LH],
                                       func=AF.Square), sS)
                s_here = val(sS)

                @blk.tensor
                def _(t, ht=ht, s_here=s_here):
                    t.wait_ge(sS, s_here)
                    if ht == 0:
                        for i in (4, 5, 6, 7):
                            bank_wait(t, i)
                    for fi in range(2):
                        inc(t.matmul(pb[4 + fi][0:1, 0:512],
                                     lhsT=ones_cf[:],
                                     rhs=x2[:, ht * LH + fi * 512:
                                            ht * LH + (fi + 1) * 512],
                                     start=(ht == 0), stop=(ht == NT - 1)), sT)
                        inc(t.matmul(pb[6 + fi][0:1, 0:512],
                                     lhsT=ones_cf[:],
                                     rhs=sqf[:, fi * 512:(fi + 1) * 512],
                                     start=(ht == 0), stop=(ht == NT - 1)), sT)
                stat_t = val(sT)
            p6_t = val(sT)

            @blk.vector
            def _(v):
                v.wait_ge(sT, p6_t)
                for fi in range(2):
                    c = inc(v.tensor_copy(out=mrow[0:1, fi * 512:(fi + 1) * 512],
                                          in_=pb[4 + fi][0:1, 0:512]), sV)
                    bank_guard[4 + fi] = (sV, c)
                    c = inc(v.tensor_copy(out=rrow[0:1, fi * 512:(fi + 1) * 512],
                                          in_=pb[6 + fi][0:1, 0:512]), sV)
                    bank_guard[6 + fi] = (sV, c)
                v.wait_ge(sV, val(sV))
                inc(v.tensor_scalar(out=mrow[:], in0=mrow[:], scalar1=1.0 / H,
                                    scalar2=None, op0=MUL, op1=BYP), sV)
                v.wait_ge(sV, val(sV))
                inc(v.tensor_tensor(out=tb2[0:1, 0:LH], in0=mrow[:], in1=mrow[:],
                                    op=MUL), sV)
                v.wait_ge(sV, val(sV))
                inc(v.scalar_tensor_tensor(out=rrow[:], in0=rrow[:], scalar=1.0 / H,
                                           in1=tb2[0:1, 0:LH], op0=MUL, op1=SUB), sV)
                v.wait_ge(sV, val(sV))
                inc(v.tensor_scalar(out=rrow[:], in0=rrow[:], scalar1=LN_EPS,
                                    scalar2=None, op0=ADD, op1=BYP), sV)
            p6_v = val(sV)

            @blk.scalar
            def _(scl):
                scl.wait_ge(sV, p6_v)
                inc(scl.activation(out=rrow[:], in_=rrow[:], func=AF.Sqrt), sS)
            p6_s = val(sS)

            @blk.vector
            def _(v):
                v.wait_ge(sS, p6_s)
                inc(v.reciprocal(out=rrow[:], in_=rrow[:]), sV)
            p6_v2 = val(sV)

            @blk.tensor
            def _(t):
                t.wait_ge(sV, p6_v2)
                for fi in range(2):
                    bank_wait(t, 6)
                    inc(t.matmul(pb[6][:, 0:512], lhsT=ones_rf[:],
                                 rhs=mrow[0:1, fi * 512:(fi + 1) * 512],
                                 start=True, stop=True), sT)
                    t_here = val(sT)

                    @blk.vector
                    def _(v, fi=fi, t_here=t_here):
                        v.wait_ge(sT, t_here)
                        c = inc(v.tensor_copy(out=m_bc[:, fi * 512:(fi + 1) * 512],
                                              in_=pb[6][:, 0:512]), sV)
                        bank_guard[6] = (sV, c)
                for fi in range(2):
                    bank_wait(t, 7)
                    inc(t.matmul(pb[7][:, 0:512], lhsT=ones_rf[:],
                                 rhs=rrow[0:1, fi * 512:(fi + 1) * 512],
                                 start=True, stop=True), sT)
                    t_here = val(sT)

                    @blk.vector
                    def _(v, fi=fi, t_here=t_here):
                        v.wait_ge(sT, t_here)
                        c = inc(v.tensor_copy(out=r_bc[:, fi * 512:(fi + 1) * 512],
                                              in_=pb[7][:, 0:512]), sV)
                        bank_guard[7] = (sV, c)
            p6_bc_v = val(sV)

            prev_h2_s = 0
            for ht in range(NT):
                @blk.vector
                def _(v, ht=ht, prev_h2_s=prev_h2_s):
                    v.wait_ge(sV, p6_bc_v)
                    v.wait_ge(sS, prev_h2_s)
                    inc(v.tensor_tensor(out=tb2[:], in0=x2[:, ht * LH:(ht + 1) * LH],
                                        in1=m_bc[:], op=SUB), sV)
                    v.wait_ge(sV, val(sV))
                    inc(v.tensor_tensor(out=tb2[:], in0=tb2[:], in1=r_bc[:], op=MUL),
                        sV)
                v_here = val(sV)

                @blk.scalar
                def _(scl, ht=ht, v_here=v_here):
                    scl.wait_ge(sV, v_here)
                    inc(scl.activation(out=h2[:, ht * LH:(ht + 1) * LH], in_=tb2[:],
                                       func=AF.Identity, scale=cb1[:, ht:ht + 1],
                                       bias=sb_c[:, ht:ht + 1]), sS)
                prev_h2_s = val(sS)
            p6_h2_s = val(sS)

            fw = [sb(f"fw{j}", [128, H], BF16, scope=sc5) for j in range(3)]
            fwsem = [sDw, sDx, sDq[0]]
            hbb = [sb("hbb0", [128, LH], BF16, scope=sc5),
                   sb("hbb1", [128, LH], BF16, scope=sc5)]
            hq = [sb("hq0", [128, LH], BF16, scope=sc5),
                  sb("hq1", [128, LH], BF16, scope=sc5)]
            msr = mrow

            fw_cons = [0, 0, 0]
            hb_dma = [0, 0]
            ms_t = 0
            hq_s = [0, 0]
            spill_at = {}
            for m in range(HID // 128):
                sl4 = m % 3

                @blk.sync
                def _(sy, m=m, sl4=sl4):
                    sem = fwsem[sl4]
                    sy.wait_ge(sem, val(sem))
                    sy.wait_ge(sT, fw_cons[sl4])
                    if m == 0:
                        sy.wait_ge(sG, p0c_g)
                    inc(sy.dma_start(
                        out=fw[sl4][:],
                        in_=wfcT_o[m * 128:(m + 1) * 128, :]), sem, 16)
                d_here = val(fwsem[sl4])

                @blk.tensor
                def _(t, m=m, sl4=sl4, d_here=d_here):
                    t.wait_ge(fwsem[sl4], d_here)
                    if m == 0:
                        t.wait_ge(sS, p6_h2_s)
                    bank_wait(t, (m % 2) * 2)
                    bank_wait(t, 1 + (m % 2) * 2)
                    for kb in range(NT):
                        for fi in range(2):
                            inc(t.matmul(
                                pb[fi + (m % 2) * 2][:, 0:512],
                                lhsT=fw[sl4][:, kb * 128:(kb + 1) * 128],
                                rhs=h2[:, kb * LH + fi * 512:kb * LH + (fi + 1) * 512],
                                start=(kb == 0), stop=(kb == NT - 1)), sT)
                fw_cons[sl4] = val(sT)
                t_here = val(sT)

                @blk.scalar
                def _(scl, m=m, t_here=t_here, hbd=hb_dma[m % 2], ms_t=ms_t):
                    scl.wait_ge(sT, t_here)
                    scl.wait_ge(sDs[m % 2], hbd)
                    scl.wait_ge(sT, ms_t)
                    for fi in range(2):
                        inc(scl.activation(out=sqf[:, fi * 512:(fi + 1) * 512],
                                           in_=pb[fi + (m % 2) * 2][:, 0:512],
                                           func=AF.Relu), sS)
                    scl.wait_ge(sS, val(sS))
                    c = None
                    for fi in range(2):
                        c = inc(scl.activation(
                            out=hbb[m % 2][:, fi * 512:(fi + 1) * 512],
                            in_=sqf[:, fi * 512:(fi + 1) * 512],
                            func=AF.Square), sS)
                    for fi in range(2):
                        bank_guard[fi + (m % 2) * 2] = (sS, c)
                    scl.wait_ge(sS, val(sS))
                    inc(scl.dma_start(out=h_sp[m * 128:(m + 1) * 128, :],
                                      in_=hbb[m % 2][:]), sDs[m % 2], 16)
                    inc(scl.activation(out=hq[m % 2][:], in_=hbb[m % 2][:],
                                       func=AF.Square), sS)
                s_here = val(sS)
                hb_dma[m % 2] = val(sDs[m % 2])
                spill_at[m] = (sDs[m % 2], hb_dma[m % 2])
                hq_s[m % 2] = s_here

                if m > 0:
                    @blk.tensor
                    def _(t, mprev=m - 1, sprev=hq_s[(m - 1) % 2]):
                        t.wait_ge(sS, sprev)
                        if mprev == 0:
                            bank_wait(t, 6)
                            bank_wait(t, 7)
                        for fi in range(2):
                            inc(t.matmul(pb[6 + fi][0:1, 0:512],
                                         lhsT=ones_cb[:],
                                         rhs=hq[mprev % 2][:, fi * 512:(fi + 1) * 512],
                                         start=(mprev == 0), stop=False), sT)
                    ms_t = val(sT)

            @blk.tensor
            def _(t, mprev=HID // 128 - 1, sprev=hq_s[(HID // 128 - 1) % 2]):
                t.wait_ge(sS, sprev)
                for fi in range(2):
                    inc(t.matmul(pb[6 + fi][0:1, 0:512],
                                 lhsT=ones_cb[:],
                                 rhs=hq[mprev % 2][:, fi * 512:(fi + 1) * 512],
                                 start=False, stop=True), sT)
            ms_t = val(sT)
            p7_fc_d = [(sDs[0], val(sDs[0])), (sDs[1], val(sDs[1]))]

            @blk.vector
            def _(v):
                v.wait_ge(sT, ms_t)
                for fi in range(2):
                    c = inc(v.tensor_copy(out=msr[0:1, fi * 512:(fi + 1) * 512],
                                          in_=pb[6 + fi][0:1, 0:512]), sV)
                    bank_guard[6 + fi] = (sV, c)
                v.wait_ge(sV, val(sV))
                inc(v.tensor_scalar(out=msr[:], in0=msr[:], scalar1=1.0 / HID,
                                    scalar2=RMS_EPS, op0=MUL, op1=ADD), sV)
            p7_ms_v = val(sV)

            @blk.scalar
            def _(scl):
                scl.wait_ge(sV, p7_ms_v)
                inc(scl.activation(out=msr[:], in_=msr[:], func=AF.Sqrt), sS)
            p7_s = val(sS)

            @blk.vector
            def _(v):
                v.wait_ge(sS, p7_s)
                inc(v.reciprocal(out=msr[:], in_=msr[:]), sV)
            p7_v = val(sV)

            @blk.tensor
            def _(t):
                t.wait_ge(sV, p7_v)
                for fi in range(2):
                    bank_wait(t, 6)
                    inc(t.matmul(pb[6][:, 0:512], lhsT=ones_rf[:],
                                 rhs=msr[0:1, fi * 512:(fi + 1) * 512],
                                 start=True, stop=True), sT)
                    t_here = val(sT)

                    @blk.vector
                    def _(v, fi=fi, t_here=t_here):
                        v.wait_ge(sT, t_here)
                        c = inc(v.tensor_copy(out=m_bc[:, fi * 512:(fi + 1) * 512],
                                              in_=pb[6][:, 0:512]), sV)
                        bank_guard[6] = (sV, c)
            p7_rbc_v = val(sV)

            hw = [sb(f"hw{j}", [128, LH], BF16, scope=sc5) for j in range(3)]
            pw = [sb(f"pw{j}", [128, 4 * 128], BF16, scope=sc5) for j in range(3)]
            ksem = [sDw, sDx, sDq[0]]

            hw_cons = [0, 0, 0]
            for dg in range(4):
                mm_end = None
                for kb in range(HID // 128):
                    cshard, hb8 = kb // 8, kb % 8
                    sl = kb % 3

                    @blk.sync
                    def _(sy, dg=dg, kb=kb, cshard=cshard, hb8=hb8, sl=sl):
                        sem = ksem[sl]
                        sy.wait_ge(sem, val(sem))
                        sy.wait_ge(sT, hw_cons[sl])
                        if dg == 0:
                            s_, c_ = spill_at[kb]
                            sy.wait_ge(s_, c_)
                        inc(sy.dma_start(out=hw[sl][:],
                                         in_=h_sp[kb * 128:(kb + 1) * 128, :]),
                            sem, 16)
                        inc(sy.dma_start(
                            out=pw[sl].ap().rearrange("p (mm c) -> p mm c", c=128),
                            in_=wpjT_o.rearrange("(g p) c -> p g c", p=128)[
                                :, cshard * NT + dg * 4:cshard * NT + dg * 4 + 4,
                                hb8 * 128:(hb8 + 1) * 128]), sem, 16)
                    d_here = val(ksem[sl])

                    @blk.tensor
                    def _(t, dg=dg, kb=kb, sl=sl, d_here=d_here):
                        t.wait_ge(ksem[sl], d_here)
                        if kb == 0:
                            t.wait_ge(sV, p7_rbc_v)
                            for i in range(8):
                                bank_wait(t, i)
                        for mm in range(4):
                            for fi in range(2):
                                inc(t.matmul(
                                    pb[mm * 2 + fi][:, 0:512],
                                    lhsT=pw[sl][:, mm * 128:(mm + 1) * 128],
                                    rhs=hw[sl][:, fi * 512:(fi + 1) * 512],
                                    start=(kb == 0), stop=(kb == HID // 128 - 1)), sT)
                    mm_end = val(sT)
                    hw_cons[sl] = mm_end

                for mm in range(4):
                    ht = dg * 4 + mm

                    @blk.vector
                    def _(v, mm=mm, ht=ht, mm_end=mm_end, ventry=val(sV)):
                        v.wait_ge(sT, mm_end)
                        v.wait_ge(sV, ventry)
                        for fi in range(2):
                            c = inc(v.scalar_tensor_tensor(
                                out=tb2[:, fi * 512:(fi + 1) * 512],
                                in0=pb[mm * 2 + fi][:, 0:512],
                                scalar=gb_c[:, ht:ht + 1],
                                in1=m_bc[:, fi * 512:(fi + 1) * 512],
                                op0=MUL, op1=MUL), sV)
                            bank_guard[mm * 2 + fi] = (sV, c)
                        v.wait_ge(sV, val(sV))
                        inc(v.tensor_tensor(out=x2[:, ht * LH:(ht + 1) * LH],
                                            in0=tb2[:],
                                            in1=x2[:, ht * LH:(ht + 1) * LH],
                                            op=ADD), sV)
            p7_v2 = val(sV)

            out_dma = 0
            for lt in range(8):
                @blk.tensor
                def _(t, lt=lt, out_dma=out_dma):
                    t.wait_ge(sV, p7_v2)
                    t.wait_ge(sDo, out_dma)
                    for hb in range(NT):
                        bank_wait(t, hb % 2)
                        inc(t.transpose(pb[hb % 2][:, 0:128],
                                        in_=x2[:, hb * LH + lt * 128:
                                               hb * LH + (lt + 1) * 128],
                                        identity=ident_f[:]), sT)
                        t_here = val(sT)

                        @blk.vector
                        def _(v, hb=hb, t_here=t_here):
                            v.wait_ge(sT, t_here)
                            c = inc(v.tensor_copy(out=xs2[:, hb * 128:(hb + 1) * 128],
                                                  in_=pb[hb % 2][:, 0:128]), sV)
                            bank_guard[hb % 2] = (sV, c)
                v_here = val(sV)

                @blk.sync
                def _(sy, lt=lt, v_here=v_here):
                    sy.wait_ge(sDo, val(sDo))
                    sy.wait_ge(sV, v_here)
                    inc(sy.dma_start(out=out_my[lt * 128:(lt + 1) * 128, :],
                                     in_=xs2[:]), sDo, 16)
                out_dma = val(sDo)
            p8_d = val(sDo)

            @blk.sync
            def _(sy):
                sy.wait_ge(sDo, p8_d)

    return nc


_NC_CACHE = {}


def _get_nc():
    if "nc" not in _NC_CACHE:
        _NC_CACHE["nc"] = build_program()
    return _NC_CACHE["nc"]


def kernel(L0_attn_x, L0_attn_W_in_states, L0_attn_W_in_gates, L0_attn_conv_w,
           L0_attn_conv_b, L0_attn_W_dt, L0_attn_W_b, L0_attn_W_c,
           L0_attn_W_dtproj, L0_attn_b_dtproj, L0_attn_A_log, L0_attn_D,
           L0_attn_W_out, L0_ffn_W_fc, L0_ffn_W_proj, L0_c, L0_wmod):
    import ml_dtypes
    f = lambda a: np.ascontiguousarray(np.asarray(a, dtype=np.float32))
    X = f(L0_attn_x)
    Wis, Wig = f(L0_attn_W_in_states), f(L0_attn_W_in_gates)
    Cw, Cb = f(L0_attn_conv_w), f(L0_attn_conv_b)
    Wdt, Wb, Wc = f(L0_attn_W_dt), f(L0_attn_W_b), f(L0_attn_W_c)
    Wdp, Bdp = f(L0_attn_W_dtproj), f(L0_attn_b_dtproj)
    Alog, Dv = f(L0_attn_A_log), f(L0_attn_D)
    Wout, Wfc, Wpj = f(L0_attn_W_out), f(L0_ffn_W_fc), f(L0_ffn_W_proj)
    C0, Wmod = f(L0_c), f(L0_wmod)
    eyef = np.eye(128, dtype=np.float32)
    eyeb = np.eye(128, dtype=ml_dtypes.bfloat16)

    in_maps = []
    for c in range(NC):
        b, hf = c // 2, c % 2
        sd = slice(hf * DH, (hf + 1) * DH)
        in_maps.append({
            "x_b": X[b],
            "x_h": np.ascontiguousarray(X[b, hf * LH:(hf + 1) * LH]),
            "wis": np.ascontiguousarray(Wis[sd, :]),
            "wig": np.ascontiguousarray(Wig[sd, :]),
            "cw": np.ascontiguousarray(Cw[sd, :]),
            "cbia": np.ascontiguousarray(Cb[sd, None]),
            "wdt": np.ascontiguousarray(Wdt[:, sd]),
            "wbc": np.ascontiguousarray(
                np.concatenate([Wb[:, sd], Wc[:, sd]], axis=0)),
            "wdp": np.ascontiguousarray(Wdp[sd, :]),
            "bdp": np.ascontiguousarray(Bdp[sd, None]),
            "alog": np.ascontiguousarray(Alog[sd, :]),
            "dvec": np.ascontiguousarray(Dv[sd, None]),
            "wout_h": np.ascontiguousarray(Wout[:, sd]),
            "wfc_s": np.ascontiguousarray(
                Wfc[c * (HID // NC):(c + 1) * (HID // NC), :]),
            "wpj_s": np.ascontiguousarray(
                Wpj[:, c * (HID // NC):(c + 1) * (HID // NC)]),
            "wmod_s": np.ascontiguousarray(Wmod[:, hf * 6144:(hf + 1) * 6144]),
            "c0_my": np.ascontiguousarray(C0[b:b + 1, :]),
            "eyef": eyef,
            "eyeb": eyeb,
        })

    res = run_bass_kernel_spmd(_get_nc(), in_maps, core_ids=list(range(NC)))
    out = np.empty((B, L, H), dtype=np.float32)
    for c in range(NC):
        b, hf = c // 2, c % 2
        out[b, hf * LH:(hf + 1) * LH, :] = np.asarray(res.results[c]["out_my"])
    return out
